# version 12
# baseline (speedup 1.0000x reference)
import contextlib

import numpy as np
import ml_dtypes
import concourse.bass as bass
import concourse.mybir as mybir
from concourse.bass_utils import run_bass_kernel_spmd

F32 = mybir.dt.float32
F16 = mybir.dt.float16
AF = mybir.ActivationFunctionType
ALU = mybir.AluOpType
AX = mybir.AxisListType

NCORES = 8
TOK = 1024
D = 2048
N = 8192
C = 512
SEQ = 4096
TILES = 8
GT = 4
NGRP = 2
HSLB = 32
DCH = 16
NCH = 64
KTS = 8
KKT = 819.0
CBIG = 100.0
LN_EPS = 1e-5
RMS_EPS = 1.1920929e-07

ENGS = ("sync", "scalar", "vector", "tensor", "gpsimd")


def ts_(i, n=128):
    return slice(i * n, (i + 1) * n)


class Step:
    __slots__ = ("eng", "kind", "seq", "slot", "async_")


NPOOL = 12


class K:

    def __init__(self):
        self.items = []
        self.ce = {e: 0 for e in ENGS}
        self.dcount = {e: 0 for e in ENGS}
        self.dval = {(e, i): 0 for e in ENGS for i in range(NPOOL)}
        self.dhist = {e: [] for e in ENGS}
        self.lastb = ()

    def step(self, eng, deps=(), ndma=0, cc=False):
        def deco(fn):
            st = Step()
            st.eng = eng
            st.async_ = cc
            if ndma:
                st.kind = "d"
                idx = self.dcount[eng]
                self.dcount[eng] += 1
                st.slot = idx % NPOOL
                self.dval[(eng, st.slot)] += ndma * 16
                st.seq = self.dval[(eng, st.slot)]
                extra = ()
                if idx >= NPOOL:
                    extra = (self.dhist[eng][idx - NPOOL],)
                self.dhist[eng].append(st)
                self.items.append((st, fn,
                                   tuple(deps) + extra + tuple(self.lastb)))
            else:
                st.kind = "c"
                st.slot = -1
                self.ce[eng] += 1
                st.seq = self.ce[eng]
                self.items.append((st, fn, tuple(deps) + tuple(self.lastb)))
            return st
        return deco

    def barrier(self, engines=None):
        engines = tuple(engines) if engines is not None else ENGS
        lasts = {}
        for st, fn, deps in self.items:
            if st.eng in engines:
                lasts[(st.eng, st.kind == "d", st.slot)] = st
        deps = tuple(lasts.values())
        outs = []
        for e in engines:
            st = Step()
            st.eng = e
            st.kind = "b"
            st.slot = -1
            st.async_ = False
            self.ce[e] += 1
            st.seq = self.ce[e]
            self.items.append((st, None, deps))
            outs.append(st)
        self.lastb = tuple(outs)
        return outs

    def lower(self, nc):
        with contextlib.ExitStack() as ctx:
            sems = {}
            for e in ENGS:
                sems["se_" + e] = ctx.enter_context(nc.semaphore("se_" + e))
                for i in range(NPOOL):
                    sems[f"sd_{e}_{i}"] = ctx.enter_context(
                        nc.semaphore(f"sd_{e}_{i}"))
            block = ctx.enter_context(nc.Block())
            streams = {e: [] for e in ENGS}
            for st, fn, deps in self.items:
                streams[st.eng].append((st, fn, deps))

            def mk(e):
                def run(eng):
                    mark = {}
                    for st, fn, deps in streams[e]:
                        need_drain = False
                        for d in deps:
                            if d.kind == "d":
                                sem, val = f"sd_{d.eng}_{d.slot}", d.seq
                            else:
                                if d.eng == e and not getattr(d, "async_", False):
                                    need_drain = True
                                    continue
                                sem, val = "se_" + d.eng, d.seq
                            if mark.get(sem, 0) >= val:
                                continue
                            mark[sem] = val
                            eng.wait_ge(sems[sem], val)
                        if need_drain:
                            eng.drain()
                        if st.kind == "b":
                            eng.sem_inc(sems["se_" + e], 1)
                        elif st.kind == "d":
                            fn(eng, sems[f"sd_{e}_{st.slot}"])
                        else:
                            last = fn(eng)
                            last.then_inc(sems["se_" + e], 1)
                return run
            block.sync(mk("sync"))
            block.scalar(mk("scalar"))
            block.vector(mk("vector"))
            block.tensor(mk("tensor"))
            block.gpsimd(mk("gpsimd"))


def build(taps=()):
    nc = bass.Bass("TRN2", target_bir_lowering=False)
    k = K()
    tapd = {}

    def din(name, shape, dt=F32):
        return nc.dram_tensor(name, shape, dt, kind="ExternalInput").ap()

    h = din("h", [TOK, D])
    wsyn = din("wsyn", [TOK, D])
    wout = din("wout", [256, N])
    wdkv = din("wdkv", [C, D])
    wdq = din("wdq", [C, D])
    wuk = din("wuk", [D, C])
    wuv = din("wuv", [D, C])
    wuq = din("wuq", [D, C])
    wkr = din("wkr", [D, D])
    wqr = din("wqr", [D, C])
    wo = din("wo", [D, D])
    g1 = din("g1", [1, C])
    g2 = din("g2", [1, C])
    g3 = din("g3", [1, D])
    bsyn = din("bsyn", [1, N])
    bout = din("bout", [1, D])
    gamma = din("gamma", [1, D])
    beta = din("beta", [1, D])
    cosq = din("cosq", [TOK, 64])
    sinq = din("sinq", [TOK, 64])
    cosk = din("cosk", [TOK, 64])
    sink = din("sink", [TOK, 64])
    ident = din("ident", [128, 128], F16)
    ones1 = din("ones1", [1, 128], F16)
    onesf = din("onesf", [128, 1])
    iota8 = din("iota8", [128, 8])
    out = nc.dram_tensor("out", [TOK, D], F32, kind="ExternalOutput").ap()

    sqp = nc.dram_tensor("sqp", [1, 2], F32)
    sqa = nc.dram_tensor("sqa", [1, 2], F32)
    tsynp4 = [nc.dram_tensor(f"tsynp{j}", [D, 256], F16) for j in range(4)]
    tsyna4 = [nc.dram_tensor(f"tsyna{j}", [NCORES * D, 256], F16)
              for j in range(4)]
    toutp2 = [nc.dram_tensor(f"toutp{j}", [N // 2, 256], F16)
              for j in range(2)]
    touta2 = [nc.dram_tensor(f"touta{j}", [NCORES * N // 2, 256], F16)
              for j in range(2)]
    bsynp_d = nc.dram_tensor("bsynp_d", [1, N], F16)
    boutp_d = nc.dram_tensor("boutp_d", [1, D], F16)
    krtp = nc.dram_tensor("krtp", [D, TOK], F16)
    krta = nc.dram_tensor("krta", [4 * D, TOK], F16)
    ckvp = nc.dram_tensor("ckvp", [TOK, C], F16)
    ckva = nc.dram_tensor("ckva", [4 * TOK, C], F16)
    ckvtp = nc.dram_tensor("ckvtp", [C, TOK], F16)
    ckvta = nc.dram_tensor("ckvta", [4 * C, TOK], F16)

    top = contextlib.ExitStack()
    e_ = top.enter_context
    x_sb = e_(nc.sbuf_tensor("x_sb", [128, TILES * D], F16))
    identb = e_(nc.sbuf_tensor("identb", [128, 128], F16))
    ones1b = e_(nc.sbuf_tensor("ones1b", [1, 128], F16))
    onesfb = e_(nc.sbuf_tensor("onesfb", [128, 1], F32))
    iotab = e_(nc.sbuf_tensor("iotab", [128, 8], F32))
    gb2 = e_(nc.sbuf_tensor("gb2", [128, D], F16))
    bb2 = e_(nc.sbuf_tensor("bb2", [128, D], F16))
    ssynb = e_(nc.sbuf_tensor("ssynb", [128, 1], F32))
    soutb = e_(nc.sbuf_tensor("soutb", [128, 1], F32))
    hsynb = e_(nc.sbuf_tensor("hsynb", [128, 1], F32))
    houtb = e_(nc.sbuf_tensor("houtb", [128, 1], F32))
    sv1 = e_(nc.sbuf_tensor("sv1", [128, 1], F32))
    sv2 = e_(nc.sbuf_tensor("sv2", [128, 1], F32))
    sv3 = e_(nc.sbuf_tensor("sv3", [128, 1], F32))
    sv4 = e_(nc.sbuf_tensor("sv4", [128, 1], F32))
    ss1 = e_(nc.sbuf_tensor("ss1", [128, 1], F32))
    ss2 = e_(nc.sbuf_tensor("ss2", [128, 1], F32))
    cS = e_(nc.sbuf_tensor("cS", [128, 8], F32))
    cQ = e_(nc.sbuf_tensor("cQ", [128, 8], F32))
    cMU = e_(nc.sbuf_tensor("cMU", [128, 8], F32))
    cVR = e_(nc.sbuf_tensor("cVR", [128, 8], F32))
    cRS = e_(nc.sbuf_tensor("cRS", [128, 8], F32))
    cTH = e_(nc.sbuf_tensor("cTH", [128, 8], F32))
    u8 = e_(nc.sbuf_tensor("u8", [128, 8], F32))
    d8 = e_(nc.sbuf_tensor("d8", [128, 8], F32))
    hot = e_(nc.sbuf_tensor("hot", [128, 8], F32))
    cb = e_(nc.sbuf_tensor("cb", [128, 8], F32))
    svrow = e_(nc.sbuf_tensor("svrow", [1, 4], F32))
    identso = e_(nc.sbuf_tensor("identso", [128, 128], F16))
    u16 = e_(nc.sbuf_tensor("u16", [128, 16], F32))
    d16 = e_(nc.sbuf_tensor("d16", [128, 16], F32))
    onesr32 = e_(nc.sbuf_tensor("onesr32", [1, 128], F32))
    CB_LN, CB_RMS, CB_MH, CB_PH, CB_MO, CB_I12, CB_IC, CB_ID = range(8)

    def cbv(i):
        return cb[:, i:i + 1]

    @k.step("sync", ndma=4)
    def s_c0(e, sd):
        e.dma_start(out=identb[:], in_=ident).then_inc(sd, 16)
        e.dma_start(out=ones1b[:], in_=ones1).then_inc(sd, 16)
        e.dma_start(out=onesfb[:], in_=onesf).then_inc(sd, 16)
        e.dma_start(out=iotab[:], in_=iota8).then_inc(sd, 16)

    @k.step("vector")
    def s_cb(e):
        e.memset(cb[:, 0:1], LN_EPS)
        e.memset(cb[:, 1:2], RMS_EPS)
        e.memset(cb[:, 2:3], -0.5)
        e.memset(cb[:, 3:4], 0.5)
        e.memset(cb[:, 4:5], -1.0)
        e.memset(onesr32[:], 1.0)
        e.memset(cb[:, 5:6], 1.0 / 12.0)
        e.memset(cb[:, 6:7], 1.0 / C)
        return e.memset(cb[:, 7:8], 1.0 / D)

    @k.step("gpsimd", ndma=3)
    def s_x0(e, sd):
        e.dma_start(
            out=x_sb[:].rearrange("p (t d) -> p t d", t=TILES),
            in_=bass.AP(h.tensor, 0, [[D, 128], [128 * D, TILES], [1, D]]),
        ).then_inc(sd, 16)
        e.dma_start(out=gb2[0:1, :], in_=gamma).then_inc(sd, 16)
        e.dma_start(out=bb2[0:1, :], in_=beta).then_inc(sd, 16)

    @k.step("vector", deps=[s_x0])
    def s_gb(e):
        e.tensor_scalar(out=gb2[0:1, :], in0=gb2[0:1, :], scalar1=0.5,
                        scalar2=None, op0=ALU.mult)
        return e.tensor_scalar(out=bb2[0:1, :], in0=bb2[0:1, :], scalar1=0.5,
                               scalar2=None, op0=ALU.mult)


    qs = contextlib.ExitStack()
    wf = qs.enter_context(nc.sbuf_tensor("wf", [128, 16384], F32))
    tq = qs.enter_context(nc.sbuf_tensor("tq", [128, 16384], F16))
    tq2 = qs.enter_context(nc.sbuf_tensor("tq2", [128, 16384], F16))
    tq3 = qs.enter_context(nc.sbuf_tensor("tq3", [128, 16384], F16))
    sqsb = qs.enter_context(nc.sbuf_tensor("sqsb", [1, 2], F32))
    qacc = qs.enter_context(nc.sbuf_tensor("qacc", [128, 2], F32))
    pq = qs.enter_context(nc.psum_tensor("pq", [128, 1024], F16))
    pq2 = qs.enter_context(nc.psum_tensor("pq2", [128, 1024], F16))
    psml = qs.enter_context(nc.psum_tensor("psml", [1, 2], F32))
    pbc = qs.enter_context(nc.psum_tensor("pbc", [128, 512], F32))

    bcast_ev = []
    for which, buf in ((0, gb2), (1, bb2)):
        for sl in range(4):
            @k.step("tensor", deps=[s_gb, s_c0] + ([bcast_ev[-1]] if bcast_ev else []))
            def s_bcm(e, buf=buf, sl=sl):
                return e.matmul(pbc[:], ones1b[0:1, :],
                                buf[0:1, sl * 512:(sl + 1) * 512],
                                start=True, stop=True)

            @k.step("scalar", deps=[s_bcm])
            def s_bce(e, buf=buf, sl=sl):
                return e.copy(out=buf[:, sl * 512:(sl + 1) * 512], in_=pbc[:])
            bcast_ev.append(s_bce)
    s_gbb = bcast_ev[-1]

    @k.step("sync", ndma=1)
    def s_ldsyn(e, sd):
        e.dma_start(
            out=wf[:].rearrange("p (c f) -> p c f", c=8),
            in_=bass.AP(wsyn.tensor, 0, [[D, 128], [128 * D, 8], [1, D]]),
        ).then_inc(sd, 16)

    @k.step("gpsimd", ndma=1)
    def s_ldoutf16(e, sd):
        e.dma_start(
            out=tq[:].rearrange("p (c f) -> p c f", c=2),
            in_=bass.AP(wout.tensor, 0, [[N, 128], [128 * N, 2], [1, N]]),
        ).then_inc(sd, 16)

    @k.step("scalar", deps=[s_ldsyn])
    def s_abs0(e):
        return e.activation(out=tq2[:], in_=wf[:], func=AF.Abs,
                            accum_out=qacc[:, 0:1])

    @k.step("scalar", deps=[s_ldoutf16])
    def s_abs1(e):
        return e.activation(out=tq3[:], in_=tq[:], func=AF.Abs,
                            accum_out=qacc[:, 1:2])

    @k.step("tensor", deps=[s_abs0, s_abs1, s_c0])
    def s_sqmm(e):
        e.matmul(psml[0:1, 0:1], qacc[:, 0:1], onesfb[:], start=True, stop=True)
        return e.matmul(psml[0:1, 1:2], qacc[:, 1:2], onesfb[:], start=True,
                        stop=True)

    @k.step("scalar", deps=[s_sqmm])
    def s_sqev(e):
        return e.copy(out=sqsb[:], in_=psml[:])

    @k.step("gpsimd", deps=[s_sqev], ndma=1)
    def s_sqst(e, sd):
        e.dma_start(out=sqp.ap(), in_=sqsb[:]).then_inc(sd, 16)

    @k.step("gpsimd", deps=[s_sqst], cc=True)
    def s_ar(e):
        return e.collective_compute(
            "AllReduce", ALU.add, replica_groups=[list(range(NCORES))],
            ins=[sqp.ap().opt()], outs=[sqa.ap().opt()])

    @k.step("gpsimd", deps=[s_ar], ndma=1)
    def s_sqld(e, sd):
        e.dma_start(out=sqsb[:], in_=sqa.ap()).then_inc(sd, 16)

    nelem = float(N) * float(D)

    @k.step("vector", deps=[s_sqld])
    def s_scale(e):
        e.tensor_scalar(out=svrow[0:1, 0:1], in0=sqsb[0:1, 0:1],
                        scalar1=1.0 / nelem, scalar2=None, op0=ALU.mult)
        e.tensor_scalar(out=svrow[0:1, 1:2], in0=sqsb[0:1, 1:2],
                        scalar1=1.0 / nelem, scalar2=None, op0=ALU.mult)
        e.drain()
        e.tensor_scalar(out=svrow[0:1, 2:3], in0=svrow[0:1, 0:1], scalar1=0.5,
                        scalar2=0.5e-8, op0=ALU.mult, op1=ALU.add)
        return e.tensor_scalar(out=svrow[0:1, 3:4], in0=svrow[0:1, 1:2],
                               scalar1=0.5, scalar2=0.5e-8, op0=ALU.mult,
                               op1=ALU.add)

    @k.step("tensor", deps=[s_scale, s_cb])
    def s_scm(e):
        return e.matmul(pbc[:, 0:4], onesr32[0:1, :], svrow[0:1, :],
                        start=True, stop=True)

    @k.step("scalar", deps=[s_scm])
    def s_scbc(e):
        e.copy(out=ssynb[:], in_=pbc[:, 0:1])
        e.copy(out=soutb[:], in_=pbc[:, 1:2])
        e.copy(out=hsynb[:], in_=pbc[:, 2:3])
        return e.copy(out=houtb[:], in_=pbc[:, 3:4])

    @k.step("vector", deps=[s_scbc, s_c0])
    def s_iso(e):
        e.reciprocal(out=sv4[:], in_=soutb[:])
        e.drain()
        e.tensor_scalar(out=identso[:], in0=identb[:], scalar1=sv4[:],
                        scalar2=None, op0=ALU.mult)
        e.drain()
        return e.tensor_copy(out=sv4[:], in_=soutb[:])

    def quant_mat(which, dep_load, extra_q_deps=()):
        nchk, ncol = (8, D) if which == 0 else (2, N)
        half_sb = hsynb if which == 0 else houtb

        @k.step("vector", deps=[dep_load, s_scbc] + list(extra_q_deps))
        def s_q(e, which=which, half_sb=half_sb):
            e.tensor_scalar(out=sv1[:], in0=half_sb[:], scalar1=-1.0,
                            scalar2=None, op0=ALU.mult)
            src = wf[:] if which == 0 else tq[:]
            e.tensor_scalar(out=tq2[:], in0=src, scalar1=half_sb[:],
                            scalar2=None, op0=ALU.is_ge)
            e.drain()
            e.tensor_scalar(out=tq3[:], in0=src, scalar1=sv1[:],
                            scalar2=None, op0=ALU.is_le)
            e.drain()
            return e.tensor_sub(out=tq[:], in0=tq2[:], in1=tq3[:])

        ccn = ncol // 128
        ngrp = (nchk * ccn) // 8
        tr_steps = []
        te_steps = []
        for g in range(ngrp):
            ps = pq if g % 2 == 0 else pq2
            deps = [s_q, s_c0] if g < 2 else [te_steps[g - 2]]

            @k.step("tensor", deps=deps)
            def s_tr(e, g=g, ps=ps, ccn=ccn, ncol=ncol):
                last = None
                for j in range(8):
                    blk = g * 8 + j
                    rc, cc = blk // ccn, blk % ccn
                    last = e.transpose(
                        ps[:, ts_(j)],
                        tq[:, rc * ncol + cc * 128: rc * ncol + cc * 128 + 128],
                        identb[:])
                return last
            tr_steps.append(s_tr)

            @k.step("scalar", deps=[s_tr])
            def s_te(e, g=g, ps=ps, ccn=ccn, nchk=nchk):
                last = None
                for j in range(8):
                    blk = g * 8 + j
                    rc, cc = blk // ccn, blk % ccn
                    o0 = cc * (128 * nchk) + rc * 128
                    last = e.copy(out=tq2[:, o0:o0 + 128], in_=ps[:, ts_(j)])
                return last
            te_steps.append(s_te)

        wps = []
        if which == 0:
            for j in range(4):
                @k.step("sync", deps=[te_steps[min(4 * j + 3,
                                                   len(te_steps) - 1)]],
                        ndma=1)
                def s_wpj(e, sd, j=j):
                    e.dma_start(
                        out=bass.AP(tsynp4[j], 0,
                                    [[256, 128], [128 * 256, 16], [1, 256]]),
                        in_=bass.AP(tq2[:].tensor, j * 256,
                                    [[16384, 128], [1024, 16], [1, 256]]),
                    ).then_inc(sd, 16)
                wps.append(s_wpj)
        else:
            for j in range(2):
                @k.step("sync", deps=[te_steps[-1]], ndma=1)
                def s_wpj(e, sd, j=j):
                    e.dma_start(
                        out=bass.AP(toutp2[j], 0,
                                    [[256, 128], [128 * 256, 32], [1, 256]]),
                        in_=bass.AP(tq2[:].tensor, j * 32 * 256,
                                    [[16384, 128], [256, 32], [1, 256]]),
                    ).then_inc(sd, 16)
                wps.append(s_wpj)
        return wps, tr_steps[-1]

    wp_out, tr_out = quant_mat(1, s_ldoutf16)
    wp_syn, tr_syn = quant_mat(0, s_ldsyn,
                               extra_q_deps=wp_out + [tr_out])

    ag_syn4 = []
    for j in range(4):
        @k.step("gpsimd", deps=[wp_syn[j]], cc=True)
        def s_agsj(e, j=j):
            return e.collective_compute(
                "AllGather", ALU.bypass, replica_groups=[list(range(NCORES))],
                ins=[tsynp4[j].ap().opt()], outs=[tsyna4[j].ap().opt()])
        ag_syn4.append(s_agsj)
    ag_out2 = []
    for j in range(2):
        @k.step("gpsimd", deps=[wp_out[j]], cc=True)
        def s_agoj(e, j=j):
            return e.collective_compute(
                "AllGather", ALU.bypass, replica_groups=[list(range(NCORES))],
                ins=[toutp2[j].ap().opt()], outs=[touta2[j].ap().opt()])
        ag_out2.append(s_agoj)

    b_steps = []
    for which in range(2):
        bsrc = bsyn if which == 0 else bout
        bdst = bsynp_d if which == 0 else boutp_d
        s_sb = ssynb if which == 0 else soutb
        nn = N if which == 0 else D
        brow_ap = tq3[0:1, 0:nn].bitcast(F32)[0:1, 0:nn] if False else None
        brow = tq3[:].bitcast(F32)

        @k.step("sync", deps=wp_syn + [tr_syn], ndma=1)
        def s_bld(e, sd, bsrc=bsrc, nn=nn, brow=brow):
            e.dma_start(out=brow[0:1, 0:nn], in_=bsrc).then_inc(sd, 16)

        @k.step("vector", deps=[s_bld, s_scbc] + list(b_steps))
        def s_bdiv(e, nn=nn, s_sb=s_sb, brow=brow):
            e.reciprocal(out=sv1[0:1, 0:1], in_=s_sb[0:1, 0:1])
            e.drain()
            return e.tensor_scalar(out=tq[0:1, 0:nn], in0=brow[0:1, 0:nn],
                                   scalar1=sv1[0:1, 0:1], scalar2=None,
                                   op0=ALU.mult)

        @k.step("sync", deps=[s_bdiv], ndma=1)
        def s_bst(e, sd, bdst=bdst, nn=nn):
            e.dma_start(out=bdst.ap()[0:1, 0:nn],
                        in_=tq[0:1, 0:nn]).then_inc(sd, 16)
        b_steps.append(s_bst)

    k.barrier(engines=("sync", "scalar", "vector", "tensor"))
    qs.close()

    def ffn_phase(tag):
        ng_, gt_ = 4, 2
        fs = contextlib.ExitStack()
        en = fs.enter_context
        z2 = [en(nc.sbuf_tensor(f"z{tag}{i}", [128, gt_ * N], F16))
              for i in range(2)]
        x1g2 = [en(nc.sbuf_tensor(f"x1g{tag}{i}", [128, gt_ * D], F16))
                for i in range(2)]
        x1t2 = [en(nc.sbuf_tensor(f"x1t{tag}{i}", [128, DCH * 256], F16))
                for i in range(2)]
        wsl = [en(nc.sbuf_tensor(f"wsl{tag}{i}", [128, DCH * 256], F16))
               for i in range(2)]
        woc = [en(nc.sbuf_tensor(f"woc{tag}{i}", [128, 2048], F16))
               for i in range(3)]
        sB = en(nc.sbuf_tensor(f"sB{tag}", [128, N], F16))
        ysc = en(nc.sbuf_tensor(f"ysc{tag}", [128, N // 2], F16))
        jnk = en(nc.sbuf_tensor(f"jnk{tag}", [128, D], F16))
        tmp4 = [en(nc.sbuf_tensor(f"tmp4{tag}{t}", [128, D], F16))
                for t in range(gt_)]
        bst = [en(nc.sbuf_tensor(f"bst{tag}{i}", [1, 256], F16))
               for i in range(2)]
        bsto = [en(nc.sbuf_tensor(f"bsto{tag}{i}", [1, 512], F16))
                for i in range(2)]
        pz = [en(nc.psum_tensor(f"pz{tag}{i}", [128, 256], F32))
              for i in range(2)]
        pht = [en(nc.psum_tensor(f"pht{tag}{i}", [128, 1024], F16))
               for i in range(2)]
        px = [en(nc.psum_tensor(f"px{tag}{i}", [128, 1024], F16))
              for i in range(2)]
        po = [en(nc.psum_tensor(f"po{tag}{i}", [128, 512], F32))
              for i in range(gt_)]

        zmm, zev, omm = [], [], []
        th_steps, htr_ev, x1tev = [], [], []
        ln_apply_g, x1t_done_g, zmm_g_last, omm_g_last = {}, {}, {}, {}
        hid_g, ep, ep_last, l2a_last, grp_done = {}, {}, {}, {}, {}
        grp_res = {}
        stat_sig = [None]

        def emit_ln1(g):
            buf = g % 2
            apl = []
            for t in range(gt_):
                col = buf * 2 + t
                if g >= 2:
                    deps = [omm_g_last[g - 2]]
                else:
                    deps = [s_x0, s_gbb]

                @k.step("scalar", deps=deps)
                def s_st(e, g=g, t=t, buf=buf, col=col):
                    xt = x_sb[:, (g * gt_ + t) * D:(g * gt_ + t + 1) * D]
                    jk = x1g2[buf][:, t * D:(t + 1) * D]
                    e.activation(out=jk, in_=xt, func=AF.Copy,
                                 accum_out=cS[:, col:col + 1])
                    e.drain()
                    return e.activation(out=jk, in_=xt, func=AF.Square,
                                        accum_out=cQ[:, col:col + 1])

                @k.step("vector", deps=[s_st])
                def s_vr(e, col=col):
                    e.tensor_scalar(out=cMU[:, col:col + 1],
                                    in0=cS[:, col:col + 1],
                                    scalar1=1.0 / D, scalar2=None,
                                    op0=ALU.mult)
                    e.drain()
                    e.tensor_scalar(out=sv1[:], in0=cMU[:, col:col + 1],
                                    scalar1=cMU[:, col:col + 1], scalar2=None,
                                    op0=ALU.mult)
                    e.tensor_scalar(out=cVR[:, col:col + 1],
                                    in0=cQ[:, col:col + 1],
                                    scalar1=1.0 / D, scalar2=None,
                                    op0=ALU.mult)
                    e.drain()
                    return e.tensor_sub(out=cVR[:, col:col + 1],
                                        in0=cVR[:, col:col + 1], in1=sv1[:])

                @k.step("scalar", deps=[s_vr])
                def s_ap(e, g=g, t=t, buf=buf, col=col):
                    xt = x_sb[:, (g * gt_ + t) * D:(g * gt_ + t + 1) * D]
                    e.activation(out=ss1[:], in_=cVR[:, col:col + 1],
                                 func=AF.Ln, bias=cbv(CB_LN))
                    e.drain()
                    e.activation(out=cRS[:, col:col + 1], in_=ss1[:],
                                 func=AF.Exp, scale=cbv(CB_MH))
                    e.drain()
                    e.activation(out=ss2[:], in_=cMU[:, col:col + 1],
                                 func=AF.Identity, scale=cRS[:, col:col + 1])
                    e.drain()
                    e.activation(out=ss2[:], in_=ss2[:], func=AF.Identity,
                                 scale=cbv(CB_MO))
                    e.drain()
                    return e.activation(out=x1g2[buf][:, t * D:(t + 1) * D],
                                        in_=xt, func=AF.Identity, bias=ss2[:],
                                        scale=cRS[:, col:col + 1])
                apl.append(s_ap)
            ln_apply_g[g] = apl

        def emit_x1tr(g):
            buf = g % 2
            for t in range(gt_):
                for q8 in range(2):
                    gi = len(x1tev)
                    ps = px[gi % 2]
                    deps = [ln_apply_g[g][t]]
                    if gi >= 2:
                        deps.append(x1tev[gi - 2])
                    if g >= 2 and t == 0 and q8 == 0:
                        deps.append(zmm_g_last[g - 2])

                    @k.step("tensor", deps=deps)
                    def s_tr(e, g=g, t=t, q8=q8, ps=ps, buf=buf):
                        last = None
                        for j in range(8):
                            dc = q8 * 8 + j
                            last = e.transpose(
                                ps[:, ts_(j)],
                                x1g2[buf][:, t * D + dc * 128:
                                          t * D + dc * 128 + 128],
                                identb[:])
                        return last

                    @k.step("scalar", deps=[s_tr])
                    def s_tev(e, t=t, q8=q8, ps=ps, buf=buf):
                        last = None
                        for j in range(8):
                            dc = q8 * 8 + j
                            last = e.copy(
                                out=x1t2[buf][:, dc * 256 + t * 128:
                                              dc * 256 + t * 128 + 128],
                                in_=ps[:, ts_(j)])
                        return last
                    x1tev.append(s_tev)
            x1t_done_g[g] = x1tev[-1]

        def emit_z(g):
            buf = g % 2
            hs_seq = [p * 4 + j for j in range(4) for p in range(8)]
            for li, hs in enumerate(hs_seq):
                idx = g * HSLB + li
                depw = [zmm[(idx - 2) * gt_ + gt_ - 1]] if idx >= 2 else []
                depw = depw + [ag_syn4[hs % 4]]

                @k.step("sync", deps=depw + [b_steps[0]], ndma=2)
                def s_w(e, sd, hs=hs, idx=idx):
                    e.dma_start(
                        out=bass.AP(wsl[idx % 2][:].tensor, 0,
                                    [[DCH * 256, 128], [256, DCH // 2],
                                     [1, 256]]),
                        in_=bass.AP(tsyna4[hs % 4], ((hs // 4) * D) * 256,
                                    [[256, 128], [128 * 256, DCH // 2],
                                     [1, 256]]),
                    ).then_inc(sd, 16)
                    e.dma_start(out=bst[idx % 2][0:1, :],
                                in_=bsynp_d.ap()[0:1, hs * 256:(hs + 1) * 256]
                                ).then_inc(sd, 16)

                heng = "scalar" if (tag == "a" and g == 0) else "gpsimd"

                @k.step(heng, deps=depw, ndma=1)
                def s_b(e, sd, hs=hs, idx=idx):
                    e.dma_start(
                        out=bass.AP(wsl[idx % 2][:].tensor, (DCH // 2) * 256,
                                    [[DCH * 256, 128], [256, DCH // 2],
                                     [1, 256]]),
                        in_=bass.AP(tsyna4[hs % 4],
                                    ((hs // 4) * D + (DCH // 2) * 128) * 256,
                                    [[256, 128], [128 * 256, DCH // 2],
                                     [1, 256]]),
                    ).then_inc(sd, 16)

                for t in range(gt_):
                    kk = idx * gt_ + t
                    psz = pz[kk % 2]
                    deps = [s_w, s_b, x1t_done_g[g]]
                    if kk >= 2:
                        deps.append(zev[kk - 2])

                    @k.step("tensor", deps=deps)
                    def s_zm(e, hs=hs, t=t, psz=psz, idx=idx, buf=buf):
                        e.matmul(psz[:], ones1b[0:1, :], bst[idx % 2][0:1, :],
                                 start=True, stop=False)
                        last = None
                        for dc in range(DCH):
                            last = e.matmul(
                                psz[:],
                                x1t2[buf][:, dc * 256 + t * 128:
                                          dc * 256 + t * 128 + 128],
                                wsl[idx % 2][:, dc * 256:(dc + 1) * 256],
                                start=False, stop=(dc == DCH - 1))
                        return last
                    zmm.append(s_zm)

                    zdeps = [s_zm]
                    if li == 0 and t == 0 and g >= 2:
                        zdeps.append(omm_g_last[g - 2])

                    @k.step("scalar", deps=zdeps)
                    def s_ze(e, hs=hs, t=t, psz=psz, buf=buf):
                        return e.activation(
                            out=z2[buf][:, t * N + hs * 256:
                                        t * N + hs * 256 + 256],
                            in_=psz[:], func=AF.Identity, scale=ssynb[:])
                    zev.append(s_ze)
            zmm_g_last[g] = zmm[-1]

        def emit_stats0():
            z_done = [zev[HSLB * gt_ - 1]]

            @k.step("scalar", deps=z_done)
            def s_zst(e):
                zt = z2[0][:, 0:N]
                e.activation(out=sB[:], in_=zt, func=AF.Copy,
                             accum_out=cS[:, 4:5])
                e.drain()
                return e.activation(out=sB[:], in_=zt, func=AF.Square,
                                    accum_out=cQ[:, 4:5])

            @k.step("vector", deps=[s_zst])
            def s_zvr(e):
                e.tensor_scalar(out=cMU[:, 4:5], in0=cS[:, 4:5],
                                scalar1=1.0 / N, scalar2=None, op0=ALU.mult)
                e.drain()
                e.tensor_scalar(out=sv1[:], in0=cMU[:, 4:5],
                                scalar1=cMU[:, 4:5], scalar2=None,
                                op0=ALU.mult)
                e.tensor_scalar(out=cVR[:, 4:5], in0=cQ[:, 4:5],
                                scalar1=1.0 / N, scalar2=None, op0=ALU.mult)
                e.drain()
                return e.tensor_sub(out=cVR[:, 4:5], in0=cVR[:, 4:5],
                                    in1=sv1[:])

            @k.step("scalar", deps=[s_zvr])
            def s_sig(e):
                e.activation(out=ss1[:], in_=cVR[:, 4:5], func=AF.Ln)
                e.drain()
                return e.activation(out=cRS[:, 4:5], in_=ss1[:],
                                    func=AF.Exp, scale=cbv(CB_PH))
            stat_sig[0] = s_sig

        def emit_th(g, t):
            buf = g % 2
            gt = g * gt_ + t
            z_done = [zev[(g + 1) * HSLB * gt_ - 1]]
            if th_steps:
                sig_dep = th_steps[-1]
            else:
                sig_dep = stat_sig[0]
            deps = [sig_dep] + z_done
            if htr_ev:
                deps.append(htr_ev[-1])

            @k.step("vector", deps=deps)
            def s_th(e, g=g, t=t, buf=buf, gt=gt):
                zt = z2[buf][:, t * N:(t + 1) * N]
                th = cTH[:, 4 + gt % 2:5 + gt % 2]
                sig = cRS[:, 4:5]
                if gt == 0:
                    mu = cMU[:, 4:5]
                    e.tensor_scalar(out=sv1[:], in0=sig, scalar1=1.2815516,
                                    scalar2=None, op0=ALU.mult)
                    e.drain()
                    e.tensor_add(out=th, in0=mu, in1=sv1[:])
                else:
                    e.tensor_copy(out=th, in_=cTH[:, 4 + (gt - 1) % 2:
                                                  5 + (gt - 1) % 2])
                e.drain()
                e.tensor_scalar(out=sB[:], in0=zt, scalar1=th,
                                scalar2=None, op0=ALU.is_ge, op1=ALU.add,
                                accum_out=sv1[:])
                e.drain()
                e.tensor_copy(out=sv2[:], in_=th)
                e.tensor_scalar(out=sv3[:], in0=sv1[:], scalar1=-KKT,
                                scalar2=1.0 / 1437.7, op0=ALU.add,
                                op1=ALU.mult)
                e.drain()
                e.tensor_mul(out=sv3[:], in0=sv3[:], in1=sig)
                e.drain()
                e.tensor_add(out=th, in0=th, in1=sv3[:])
                e.drain()
                e.tensor_scalar(out=sB[:], in0=zt, scalar1=th,
                                scalar2=None, op0=ALU.is_ge, op1=ALU.add,
                                accum_out=sv4[:])
                e.drain()
                e.tensor_sub(out=sv3[:], in0=th, in1=sv2[:])
                e.drain()
                e.tensor_sub(out=sv2[:], in0=sv4[:], in1=sv1[:])
                e.drain()
                e.tensor_scalar(out=sv1[:], in0=sv2[:], scalar1=sv2[:],
                                scalar2=0.25, op0=ALU.mult, op1=ALU.add)
                e.drain()
                e.reciprocal(out=sv1[:], in_=sv1[:])
                e.drain()
                e.tensor_mul(out=sv3[:], in0=sv3[:], in1=sv2[:])
                e.drain()
                e.tensor_mul(out=sv3[:], in0=sv3[:], in1=sv1[:])
                e.drain()
                e.tensor_scalar(out=sv1[:], in0=sig, scalar1=-0.01,
                                scalar2=None, op0=ALU.mult)
                e.drain()
                e.tensor_max(out=sv3[:], in0=sv3[:], in1=sv1[:])
                e.drain()
                e.tensor_scalar(out=sv1[:], in0=sig, scalar1=-1e-5,
                                scalar2=None, op0=ALU.mult)
                e.drain()
                e.tensor_tensor(out=sv3[:], in0=sv3[:], in1=sv1[:],
                                op=ALU.min)
                e.tensor_scalar(out=sv4[:], in0=sv4[:], scalar1=-KKT,
                                scalar2=None, op0=ALU.add)
                e.drain()
                e.tensor_mul(out=sv4[:], in0=sv4[:], in1=sv3[:])
                e.drain()
                e.tensor_sub(out=th, in0=th, in1=sv4[:])
                e.drain()
                e.tensor_scalar(out=sB[:], in0=zt, scalar1=th,
                                scalar2=None, op0=ALU.is_ge, op1=ALU.add,
                                accum_out=sv4[:])
                e.drain()
                e.tensor_scalar(out=sv4[:], in0=sv4[:], scalar1=-KKT,
                                scalar2=None, op0=ALU.add)
                e.drain()
                e.tensor_mul(out=sv4[:], in0=sv4[:], in1=sv3[:])
                e.drain()
                e.tensor_sub(out=th, in0=th, in1=sv4[:])
                e.drain()
                e.tensor_scalar(out=sB[:], in0=zt, scalar1=th,
                                scalar2=None, op0=ALU.is_lt, op1=ALU.add,
                                accum_out=sv1[:])
                e.drain()
                xsc = x1t2[buf][:]
                H = N // 2
                for ci in range(2):
                    mh = sB[:, ci * H:(ci + 1) * H]
                    zh = zt[:, ci * H:(ci + 1) * H]
                    e.scalar_tensor_tensor(out=ysc[:], in0=mh, scalar=-CBIG,
                                           in1=zh, op0=ALU.mult,
                                           op1=ALU.subtract)
                    e.drain()
                    e.max(out=u16[:, 8 * ci:8 * ci + 8], in_=ysc[:])
                    e.tensor_scalar(out=xsc, in0=ysc[:], scalar1=-1.0,
                                    scalar2=-CBIG, op0=ALU.mult, op1=ALU.add)
                    e.drain()
                    e.max(out=d16[:, 8 * ci:8 * ci + 8], in_=xsc)
                e.drain()
                e.max(out=u8[:], in_=u16[:])
                e.max(out=d8[:], in_=d16[:])
                e.drain()
                e.tensor_scalar(out=sv2[:], in0=sv1[:], scalar1=-1.0,
                                scalar2=float(N) - KKT, op0=ALU.mult,
                                op1=ALU.add)
                e.drain()
                e.tensor_scalar(out=sv3[:], in0=sv2[:], scalar1=0.0,
                                scalar2=7.0, op0=ALU.max, op1=ALU.min)
                e.drain()
                e.tensor_scalar(out=hot[:], in0=iotab[:], scalar1=sv3[:],
                                scalar2=None, op0=ALU.is_equal)
                e.drain()
                e.tensor_mul(out=hot[:], in0=hot[:], in1=u8[:])
                e.drain()
                e.reduce_sum(out=sv3[:], in_=hot[:], axis=AX.X, negate=True)
                e.tensor_scalar(out=sv4[:], in0=sv2[:], scalar1=-1.0,
                                scalar2=-1.0, op0=ALU.mult, op1=ALU.add)
                e.drain()
                e.tensor_scalar(out=sv4[:], in0=sv4[:], scalar1=0.0,
                                scalar2=7.0, op0=ALU.max, op1=ALU.min)
                e.drain()
                e.tensor_scalar(out=hot[:], in0=iotab[:], scalar1=sv4[:],
                                scalar2=None, op0=ALU.is_equal)
                e.drain()
                e.tensor_mul(out=hot[:], in0=hot[:], in1=d8[:])
                e.drain()
                e.reduce_sum(out=sv4[:], in_=hot[:], axis=AX.X)
                e.tensor_scalar(out=sv1[:], in0=sv2[:], scalar1=0.0,
                                scalar2=None, op0=ALU.is_ge)
                e.drain()
                e.tensor_mul(out=sv3[:], in0=sv3[:], in1=sv1[:])
                e.drain()
                e.tensor_scalar(out=sv1[:], in0=sv1[:], scalar1=-1.0,
                                scalar2=1.0, op0=ALU.mult, op1=ALU.add)
                e.drain()
                e.tensor_mul(out=sv4[:], in0=sv4[:], in1=sv1[:])
                e.drain()
                e.tensor_add(out=th, in0=sv3[:], in1=sv4[:])
                e.drain()
                return e.scalar_tensor_tensor(out=sB[:], in0=zt, scalar=th,
                                              in1=zt, op0=ALU.is_ge,
                                              op1=ALU.mult)
            th_steps.append(s_th)

            for cg in range(8):
                gi = len(htr_ev)
                ps = pht[gi % 2]
                hdeps = [s_th]
                if gi >= 2:
                    hdeps.append(htr_ev[gi - 2])

                @k.step("tensor", deps=hdeps)
                def s_htr(e, t=t, cg=cg, ps=ps):
                    last = None
                    for j in range(8):
                        c = cg * 8 + j
                        last = e.transpose(ps[:, ts_(j)],
                                           sB[:, c * 128:(c + 1) * 128],
                                           identb[:])
                    return last

                @k.step("scalar", deps=[s_htr])
                def s_hte(e, t=t, cg=cg, ps=ps, buf=buf):
                    last = None
                    for j in range(8):
                        c = cg * 8 + j
                        last = e.activation(
                            out=z2[buf][:, t * N + c * 128:
                                        t * N + c * 128 + 128],
                            in_=ps[:, ts_(j)], func=AF.Relu)
                    return last
                htr_ev.append(s_hte)
            hid_g.setdefault(g, []).append(htr_ev[-1])

        def emit_outs(g):
            buf = g % 2
            for ds in range(4):
                dep_bs = list(b_steps)
                if len(omm) >= 16:
                    dep_bs.append(omm[-16])

                @k.step("sync", deps=dep_bs, ndma=1)
                def s_bout(e, sd, ds=ds):
                    e.dma_start(out=bsto[ds % 2][0:1, :],
                                in_=boutp_d.ap()[0:1, ds * 512:(ds + 1) * 512]
                                ).then_inc(sd, 16)

                for cq in range(NCH // 4):
                    i = len(omm)
                    depw = [omm[i - 3]] if i >= 3 else []
                    depw = depw + [ag_out2[cq // 8]]

                    @k.step("sync", deps=depw, ndma=1)
                    def s_wo(e, sd, ds=ds, cq=cq, i=i):
                        blk0 = ds * 2
                        e.dma_start(
                            out=bass.AP(woc[i % 3][:].tensor, 0,
                                        [[2048, 128], [512, 4], [1, 256]]),
                            in_=bass.AP(touta2[cq // 8],
                                        (blk0 * (N // 2)
                                         + (cq % 8) * 512) * 256,
                                        [[256, 128], [128 * 256, 4],
                                         [1, 256]]),
                        ).then_inc(sd, 16)

                    @k.step("gpsimd", deps=depw, ndma=1)
                    def s_wo2(e, sd, ds=ds, cq=cq, i=i):
                        blk0 = ds * 2
                        e.dma_start(
                            out=bass.AP(woc[i % 3][:].tensor, 256,
                                        [[2048, 128], [512, 4], [1, 256]]),
                            in_=bass.AP(touta2[cq // 8],
                                        ((blk0 + 1) * (N // 2)
                                         + (cq % 8) * 512) * 256,
                                        [[256, 128], [128 * 256, 4],
                                         [1, 256]]),
                        ).then_inc(sd, 16)

                    deps = [s_wo, s_wo2] + hid_g[g]
                    if i >= 1:
                        deps.append(omm[i - 1])
                    if ds > 0:
                        deps.append(ep[(g, ds - 1, gt_ - 1)])
                    elif cq == 0 and g >= 1:
                        deps.append(ep_last[g - 1])

                    @k.step("tensor", deps=deps + [s_bout])
                    def s_om(e, ds=ds, cq=cq, i=i, buf=buf):
                        last = None
                        for j in range(4):
                            c = cq * 4 + j
                            for t in range(gt_):
                                if c == 0:
                                    e.matmul(po[t][:], ones1b[0:1, :],
                                             bsto[ds % 2][0:1, :],
                                             start=True, stop=False)
                                    e.matmul(
                                        po[t][:], identso[:],
                                        x1g2[buf][:, t * D + ds * 512:
                                                  t * D + ds * 512 + 512],
                                        start=False, stop=False)
                                last = e.matmul(
                                    po[t][:],
                                    z2[buf][:, t * N + c * 128:
                                            t * N + c * 128 + 128],
                                    woc[i % 3][:, j * 512:(j + 1) * 512],
                                    start=False, stop=(c == NCH - 1))
                        return last
                    omm.append(s_om)

                for t in range(gt_):
                    evdeps = [omm[-1]]
                    if g >= 1:
                        evdeps.append(grp_res[(g - 1, t)])

                    @k.step("scalar", deps=evdeps)
                    def s_ev(e, ds=ds, t=t):
                        return e.activation(
                            out=tmp4[t][:, ds * 512:(ds + 1) * 512],
                            in_=po[t][:], func=AF.Identity, scale=soutb[:])
                    ep[(g, ds, t)] = s_ev
            ep_last[g] = ep[(g, 3, gt_ - 1)]
            omm_g_last[g] = omm[-1]

        def emit_ln2(g, t):
            buf = g % 2
            col = 6 + t
            tile_evs = [ep[(g, ds, t)] for ds in range(4)]

            @k.step("scalar", deps=tile_evs)
            def s_l2s(e, t=t, buf=buf, col=col):
                e.activation(out=jnk[:], in_=tmp4[t][:], func=AF.Copy,
                             accum_out=cS[:, col:col + 1])
                e.drain()
                return e.activation(out=jnk[:], in_=tmp4[t][:],
                                    func=AF.Square,
                                    accum_out=cQ[:, col:col + 1])

            @k.step("vector", deps=[s_l2s])
            def s_l2v(e, col=col):
                e.tensor_scalar(out=cMU[:, col:col + 1],
                                in0=cS[:, col:col + 1],
                                scalar1=1.0 / D, scalar2=None, op0=ALU.mult)
                e.drain()
                e.tensor_scalar(out=sv1[:], in0=cMU[:, col:col + 1],
                                scalar1=cMU[:, col:col + 1], scalar2=None,
                                op0=ALU.mult)
                e.tensor_scalar(out=cVR[:, col:col + 1],
                                in0=cQ[:, col:col + 1],
                                scalar1=1.0 / D, scalar2=None, op0=ALU.mult)
                e.drain()
                return e.tensor_sub(out=cVR[:, col:col + 1],
                                    in0=cVR[:, col:col + 1], in1=sv1[:])

            @k.step("scalar", deps=[s_l2v])
            def s_l2a(e, t=t, buf=buf, col=col):
                e.activation(out=ss1[:], in_=cVR[:, col:col + 1], func=AF.Ln,
                             bias=cbv(CB_LN))
                e.drain()
                e.activation(out=cRS[:, col:col + 1], in_=ss1[:], func=AF.Exp,
                             scale=cbv(CB_MH))
                e.drain()
                e.activation(out=ss2[:], in_=cMU[:, col:col + 1],
                             func=AF.Identity, scale=cRS[:, col:col + 1])
                e.drain()
                e.activation(out=ss2[:], in_=ss2[:], func=AF.Identity,
                             scale=cbv(CB_MO))
                e.drain()
                return e.activation(out=tmp4[t][:], in_=tmp4[t][:],
                                    func=AF.Identity, bias=ss2[:],
                                    scale=cRS[:, col:col + 1])
            l2a_last[g] = s_l2a

            @k.step("vector", deps=[s_l2a, s_gbb])
            def s_res(e, g=g, t=t, buf=buf):
                xt = x_sb[:, (g * gt_ + t) * D:(g * gt_ + t + 1) * D]
                e.tensor_mul(out=tmp4[t][:], in0=tmp4[t][:],
                             in1=gb2[:])
                e.drain()
                e.tensor_add(out=xt, in0=xt, in1=tmp4[t][:])
                e.drain()
                return e.tensor_add(out=xt, in0=xt, in1=bb2[:])
            grp_res[(g, t)] = s_res
            grp_done[g] = s_res

        for g in range(ng_ + 2):
            if g < ng_:
                emit_ln1(g)
                emit_x1tr(g)
            if 1 <= g <= ng_:
                emit_th(g - 1, 0)
            if g < ng_:
                emit_z(g)
                if g == 0:
                    emit_stats0()
            if 1 <= g <= ng_:
                emit_th(g - 1, 1)
            if 2 <= g <= ng_ + 1:
                emit_ln2(g - 2, 0)
                emit_ln2(g - 2, 1)
            if 1 <= g <= ng_:
                emit_outs(g - 1)

        k.barrier(engines=("sync", "scalar", "vector", "tensor"))
        fs.close()
        return [grp_done[ng_ - 1]]

    ffn_phase("a")

    def attn_phase():
        s_ = contextlib.ExitStack()
        en = s_.enter_context
        rg4 = [[0, 1, 2, 3], [4, 5, 6, 7]]

        pt1 = en(nc.psum_tensor("pt1", [128, 512], F16))
        pt2 = en(nc.psum_tensor("pt2", [128, 512], F16))
        pa = en(nc.psum_tensor("pa", [128, 512], F32))
        pb = en(nc.psum_tensor("pb", [128, 512], F32))
        psc = en(nc.psum_tensor("psc", [128, 512], F32))
        psc2 = en(nc.psum_tensor("psc2", [128, 512], F32))
        ppt = en(nc.psum_tensor("ppt", [128, 512], F16))
        pov = en(nc.psum_tensor("pov", [128, 512], F32))

        AR = 80896
        ar = en(nc.sbuf_tensor("arena", [128, AR], F16))
        arf = ar[:].bitcast(F32)

        def v16(o, n):
            return ar[:, o:o + n]

        def v32(o, n):
            return arf[:, o // 2: o // 2 + n]

        O_x2t = 0
        O_rtmp = 16384
        O_rtm2 = 20480
        O_absw = 22528
        O_aqT = 30720
        O_cqT = 34816
        O_R1 = 38912
        O_R2 = 55296
        O_R3 = 71680
        x2t = v16(O_x2t, 16384)
        qrT = v16(O_x2t, 16384)
        rtmp = v32(O_rtmp, 2048)
        rtm2 = v16(O_rtm2, 2048)
        absw = v16(O_absw, 8192)
        aqT = v16(O_aqT, 4096)
        cqT = v16(O_cqT, 4096)
        g1b = v16(O_R3, 512)
        g2b = v16(O_R3 + 512, 512)
        g3b = v16(O_R3 + 1024, 2048)
        aqb = v16(O_R3 + 3072, 4096)
        ptile = v16(O_R3 + 3072, 512)
        pTt = v16(O_R3 + 3584, 512)
        fCA = v32(O_R3 + 4096, 8)
        fNM = v32(O_R3 + 4128, 8)
        fCO = v32(O_R3 + 4160, 8)
        fRS = v32(O_R3 + 4192, 8)
        mrun = v32(O_R3 + 4224, 8)
        lrun = v32(O_R3 + 4256, 8)

        @k.step("gpsimd", ndma=3)
        def s_gl(e, sd):
            e.dma_start(out=g1b[0:1, :], in_=g1).then_inc(sd, 16)
            e.dma_start(out=g2b[0:1, :], in_=g2).then_inc(sd, 16)
            e.dma_start(out=g3b[0:1, :], in_=g3).then_inc(sd, 16)

        gbc_ev = []
        for buf, nsl in ((g1b, 1), (g2b, 1), (g3b, 4)):
            for sl in range(nsl):
                @k.step("tensor", deps=[s_gl] + ([gbc_ev[-1]] if gbc_ev else []))
                def s_gm(e, buf=buf, sl=sl):
                    return e.matmul(pa[:], ones1b[0:1, :],
                                    buf[0:1, sl * 512:(sl + 1) * 512],
                                    start=True, stop=True)

                @k.step("scalar", deps=[s_gm])
                def s_ge(e, buf=buf, sl=sl):
                    return e.copy(out=buf[:, sl * 512:(sl + 1) * 512],
                                  in_=pa[:])
                gbc_ev.append(s_ge)
        s_gbc = gbc_ev[-1]

        woT = v16(O_R1, 32768)
        wuvb = v16(O_x2t, 8192)

        @k.step("gpsimd", ndma=2)
        def s_awl(e, sd):
            e.dma_start(
                out=woT.rearrange("p (c f) -> p c f", c=DCH),
                in_=bass.AP(wo.tensor, 0, [[D, 128], [128 * D, DCH], [1, D]]),
            ).then_inc(sd, 16)
            e.dma_start(
                out=wuvb.rearrange("p (c f) -> p c f", c=DCH),
                in_=bass.AP(wuv.tensor, 0, [[C, 128], [128 * C, DCH], [1, C]]),
            ).then_inc(sd, 16)

        wo_tr = []
        for idx, (rc, cc) in enumerate(
                [(r, c) for r in range(DCH) for c in range(r, DCH)]):
            deps = [s_awl] if idx < 2 else [wo_tr[idx - 2]]
            pss = pt1 if idx % 2 == 0 else pt2

            @k.step("tensor", deps=deps)
            def s_wt(e, rc=rc, cc=cc, pss=pss):
                i1 = e.transpose(pss[:, 0:128],
                                 woT[:, rc * D + cc * 128: rc * D + cc * 128 + 128],
                                 identb[:])
                if cc != rc:
                    i1 = e.transpose(pss[:, 128:256],
                                     woT[:, cc * D + rc * 128:
                                         cc * D + rc * 128 + 128], identb[:])
                return i1

            @k.step("scalar", deps=[s_wt])
            def s_we(e, rc=rc, cc=cc, pss=pss):
                i1 = e.copy(out=woT[:, cc * D + rc * 128: cc * D + rc * 128 + 128],
                            in_=pss[:, 0:128])
                if cc != rc:
                    i1 = e.copy(out=woT[:, rc * D + cc * 128:
                                        rc * D + cc * 128 + 128],
                                in_=pss[:, 128:256])
                return i1
            wo_tr.append(s_we)
        wo_done = wo_tr[-1]

        absw_ev = []
        for kk2, (cs, es) in enumerate(
                [(a, b) for a in range(4) for b in range(4)]):
            ps = pa if kk2 % 2 == 0 else pb
            deps = [wo_done] if kk2 < 2 else [absw_ev[kk2 - 2]]

            @k.step("tensor", deps=deps)
            def s_am(e, cs=cs, es=es, ps=ps):
                last = None
                for ec in range(DCH):
                    last = e.matmul(
                        ps[:],
                        wuvb[:, ec * C + cs * 128: ec * C + cs * 128 + 128],
                        woT[:, ec * D + es * 512: ec * D + es * 512 + 512],
                        start=(ec == 0), stop=(ec == DCH - 1))
                return last

            @k.step("scalar", deps=[s_am])
            def s_ae(e, cs=cs, es=es, ps=ps):
                return e.copy(out=absw[:, cs * D + es * 512:
                                       cs * D + es * 512 + 512], in_=ps[:])
            absw_ev.append(s_ae)
        absw_done = absw_ev[-1]

        wqb = v16(O_R1, 8192)
        wkb = v16(O_R1 + 8192, 8192)
        Asb = v16(O_R2, 2048)

        @k.step("gpsimd", deps=[absw_done], ndma=2)
        def s_bl(e, sd):
            e.dma_start(
                out=wqb.rearrange("p (c f) -> p c f", c=DCH),
                in_=bass.AP(wuq.tensor, 0, [[C, 128], [128 * C, DCH], [1, C]]),
            ).then_inc(sd, 16)
            e.dma_start(
                out=wkb.rearrange("p (c f) -> p c f", c=DCH),
                in_=bass.AP(wuk.tensor, 0, [[C, 128], [128 * C, DCH], [1, C]]),
            ).then_inc(sd, 16)

        A_ev = []
        for cs in range(4):
            ps = pa if cs % 2 == 0 else pb
            deps = [s_bl] if cs < 2 else [A_ev[cs - 2]]

            @k.step("tensor", deps=deps)
            def s_Am(e, cs=cs, ps=ps):
                last = None
                for ec in range(DCH):
                    last = e.matmul(
                        ps[:],
                        wqb[:, ec * C + cs * 128: ec * C + cs * 128 + 128],
                        wkb[:, ec * C:(ec + 1) * C],
                        start=(ec == 0), stop=(ec == DCH - 1))
                return last

            @k.step("scalar", deps=[s_Am])
            def s_Ae(e, cs=cs, ps=ps):
                return e.copy(out=Asb[:, cs * C:(cs + 1) * C], in_=ps[:])
            A_ev.append(s_Ae)

        wdqn = v16(O_R1, 8192)
        Bsb = v16(O_R2 + 8192, 8192)

        @k.step("gpsimd", deps=[A_ev[-1]], ndma=1)
        def s_dql(e, sd):
            e.dma_start(
                out=wdqn.rearrange("p (c f) -> p c f", c=4),
                in_=bass.AP(wdq.tensor, 0, [[D, 128], [128 * D, 4], [1, D]]),
            ).then_inc(sd, 16)

        B_ev = []
        for ds in range(DCH):
            ps = pa if ds % 2 == 0 else pb
            deps = [s_dql] if ds < 2 else [B_ev[ds - 2]]

            @k.step("tensor", deps=deps)
            def s_Bm(e, ds=ds, ps=ps):
                last = None
                for cc in range(4):
                    last = e.matmul(
                        ps[:],
                        wdqn[:, cc * D + ds * 128: cc * D + ds * 128 + 128],
                        Asb[:, cc * C:(cc + 1) * C],
                        start=(cc == 0), stop=(cc == 3))
                return last

            @k.step("scalar", deps=[s_Bm])
            def s_Be(e, ds=ds, ps=ps):
                return e.activation(out=Bsb[:, ds * C:(ds + 1) * C], in_=ps[:],
                                    func=AF.Identity, scale=cbv(CB_I12))
            B_ev.append(s_Be)
        B_done = B_ev[-1]

        wdqT = v16(O_R1 + 8192, 8192)
        wnat = v16(O_x2t, 8192)
        wdkvT = v16(O_R2, 8192)

        @k.step("gpsimd", deps=[B_done], ndma=1)
        def s_kvl(e, sd):
            e.dma_start(
                out=wnat.rearrange("p (c f) -> p c f", c=4),
                in_=bass.AP(wdkv.tensor, 0, [[D, 128], [128 * D, 4], [1, D]]),
            ).then_inc(sd, 16)

        jobs = []
        for cc in range(4):
            for dc in range(DCH):
                jobs.append((wdqn, cc, dc, D, wdqT, C))
        for cc in range(4):
            for dc in range(DCH):
                jobs.append((wnat, cc, dc, D, wdkvT, C))
        sw_ev = []
        for jg in range(len(jobs) // 4):
            ps = pt1 if jg % 2 == 0 else pt2
            deps = [s_kvl, B_done] if jg < 2 else [sw_ev[jg - 2]]

            @k.step("tensor", deps=deps)
            def s_st(e, jg=jg, ps=ps):
                last = None
                for j in range(4):
                    srcb, rc, cc2, srcw, dst, dn = jobs[jg * 4 + j]
                    last = e.transpose(
                        ps[:, ts_(j)],
                        srcb[:, rc * srcw + cc2 * 128: rc * srcw + cc2 * 128 + 128],
                        identb[:])
                return last

            @k.step("scalar", deps=[s_st])
            def s_se(e, jg=jg, ps=ps):
                last = None
                for j in range(4):
                    srcb, rc, cc2, srcw, dst, dn = jobs[jg * 4 + j]
                    last = e.copy(
                        out=dst[:, cc2 * dn + rc * 128: cc2 * dn + rc * 128 + 128],
                        in_=ps[:, ts_(j)])
                return last
            sw_ev.append(s_se)
        sw_done = sw_ev[-1]

        x2t_ev = []
        ln2_apply = []
        for t in range(TILES):
            @k.step("scalar", deps=([ln2_apply[-1]] if ln2_apply else []))
            def s_st2(e, t=t):
                xt = x_sb[:, t * D:(t + 1) * D]
                e.activation(out=rtm2[:], in_=xt, func=AF.Copy,
                             accum_out=cS[:, t:t + 1])
                e.drain()
                return e.activation(out=rtm2[:], in_=xt, func=AF.Square,
                                    accum_out=cQ[:, t:t + 1])

            @k.step("vector", deps=[s_st2])
            def s_vr2(e, t=t):
                e.tensor_scalar(out=cMU[:, t:t + 1], in0=cS[:, t:t + 1],
                                scalar1=1.0 / D, scalar2=None, op0=ALU.mult)
                e.drain()
                e.tensor_scalar(out=sv1[:], in0=cMU[:, t:t + 1],
                                scalar1=cMU[:, t:t + 1], scalar2=None,
                                op0=ALU.mult)
                e.tensor_scalar(out=cVR[:, t:t + 1], in0=cQ[:, t:t + 1],
                                scalar1=1.0 / D, scalar2=None, op0=ALU.mult)
                e.drain()
                return e.tensor_sub(out=cVR[:, t:t + 1], in0=cVR[:, t:t + 1],
                                    in1=sv1[:])

            @k.step("scalar", deps=[s_vr2])
            def s_ap2(e, t=t):
                xt = x_sb[:, t * D:(t + 1) * D]
                e.activation(out=ss1[:], in_=cVR[:, t:t + 1], func=AF.Ln,
                             bias=cbv(CB_LN))
                e.drain()
                e.activation(out=cRS[:, t:t + 1], in_=ss1[:], func=AF.Exp,
                             scale=cbv(CB_MH))
                e.drain()
                e.activation(out=ss2[:], in_=cMU[:, t:t + 1], func=AF.Identity,
                             scale=cRS[:, t:t + 1])
                e.drain()
                e.activation(out=ss2[:], in_=ss2[:], func=AF.Identity, scale=cbv(CB_MO))
                e.drain()
                return e.activation(out=rtm2[:], in_=xt, func=AF.Identity,
                                    bias=ss2[:], scale=cRS[:, t:t + 1])
            ln2_apply.append(s_ap2)

            for q4 in range(4):
                gi = t * 4 + q4
                ps = pt1 if gi % 2 == 0 else pt2
                deps = [s_ap2, sw_done]
                if gi >= 2:
                    deps.append(x2t_ev[gi - 2])

                @k.step("tensor", deps=deps)
                def s_xt(e, t=t, q4=q4, ps=ps):
                    last = None
                    for j in range(4):
                        dc = q4 * 4 + j
                        last = e.transpose(ps[:, ts_(j)],
                                           rtm2[:, dc * 128:(dc + 1) * 128],
                                           identb[:])
                    return last

                @k.step("scalar", deps=[s_xt])
                def s_xe(e, t=t, q4=q4, ps=ps):
                    last = None
                    for j in range(4):
                        dc = q4 * 4 + j
                        last = e.copy(
                            out=x2t[:, dc * TOK + t * 128: dc * TOK + t * 128 + 128],
                            in_=ps[:, ts_(j)])
                    return last
                x2t_ev.append(s_xe)
        x2t_done = x2t_ev[-1]

        cqb = v16(O_R1, 4096)
        ckvb = v16(O_R1 + 4096, 4096)
        prj_ev = []
        for t in range(TILES):
            for which in range(3):
                kk2 = t * 3 + which
                ps = (pa, pb, pa)[which]
                wsrc = (wdqT, wdkvT, Bsb)[which]
                deps = [x2t_done, sw_done, B_done]
                if kk2 >= 1:
                    deps.append(prj_ev[kk2 - 1])

                @k.step("tensor", deps=deps)
                def s_pm(e, t=t, ps=ps, wsrc=wsrc):
                    last = None
                    for dc in range(DCH):
                        last = e.matmul(
                            ps[:],
                            x2t[:, dc * TOK + t * 128: dc * TOK + t * 128 + 128],
                            wsrc[:, dc * C:(dc + 1) * C],
                            start=(dc == 0), stop=(dc == DCH - 1))
                    return last

                rcol = cRS if which == 0 else cTH
                if which < 2:
                    @k.step("scalar", deps=[s_pm])
                    def s_ps(e, t=t, ps=ps, rcol=rcol):
                        e.activation(out=rtm2[:, 0:C], in_=ps[:],
                                     func=AF.Square, accum_out=ss1[:])
                        e.drain()
                        e.activation(out=ss2[:], in_=ss1[:], func=AF.Ln,
                                     bias=cbv(CB_RMS), scale=cbv(CB_IC))
                        e.drain()
                        return e.activation(out=rcol[:, t:t + 1], in_=ss2[:],
                                            func=AF.Exp, scale=cbv(CB_MH))

                    @k.step("vector", deps=[s_ps])
                    def s_pv(e, t=t, ps=ps, which=which, rcol=rcol):
                        dst = cqb if which == 0 else ckvb
                        gby = g1b if which == 0 else g2b
                        return e.scalar_tensor_tensor(
                            out=dst[:, t * C:(t + 1) * C], in0=ps[:],
                            scalar=rcol[:, t:t + 1], in1=gby[:], op0=ALU.mult,
                            op1=ALU.mult)
                    prj_ev.append(s_pv)
                else:
                    @k.step("vector", deps=[s_pm])
                    def s_pv2(e, t=t, ps=ps):
                        return e.tensor_copy(out=aqb[:, t * C:(t + 1) * C],
                                             in_=ps[:])
                    prj_ev.append(s_pv2)
        prj_done = prj_ev[-1]

        ckvT = v16(O_R2, 4096)
        jobs2 = []
        for t in range(TILES):
            for cc in range(4):
                jobs2.append((cqb, t, cc, cqT))
        for t in range(TILES):
            for cc in range(4):
                jobs2.append((aqb, t, cc, aqT))
        for t in range(TILES):
            for cc in range(4):
                jobs2.append((ckvb, t, cc, ckvT))
        j2_ev = []
        for jg in range(len(jobs2) // 4):
            ps = pt1 if jg % 2 == 0 else pt2
            deps = [prj_done]
            if jg >= 2:
                deps.append(j2_ev[jg - 2])

            @k.step("tensor", deps=deps)
            def s_j2(e, jg=jg, ps=ps):
                last = None
                for j in range(4):
                    srcb, t, cc, dst = jobs2[jg * 4 + j]
                    last = e.transpose(
                        ps[:, ts_(j)],
                        srcb[:, t * C + cc * 128: t * C + cc * 128 + 128],
                        identb[:])
                return last

            @k.step("scalar", deps=[s_j2])
            def s_j2e(e, jg=jg, ps=ps):
                last = None
                for j in range(4):
                    srcb, t, cc, dst = jobs2[jg * 4 + j]
                    last = e.copy(
                        out=dst[:, cc * TOK + t * 128: cc * TOK + t * 128 + 128],
                        in_=ps[:, ts_(j)])
                return last
            j2_ev.append(s_j2e)
        j2_done = j2_ev[-1]

        @k.step("sync", deps=[prj_done], ndma=1)
        def s_ckvst(e, sd):
            e.dma_start(
                out=bass.AP(ckvp, 0, [[C, 128], [128 * C, TILES], [1, C]]),
                in_=ckvb[:].rearrange("p (t f) -> p t f", t=TILES),
            ).then_inc(sd, 16)

        @k.step("sync", deps=[j2_done], ndma=1)
        def s_ckvtst(e, sd):
            e.dma_start(
                out=bass.AP(ckvtp, 0, [[TOK, 128], [128 * TOK, 4], [1, TOK]]),
                in_=ckvT[:].rearrange("p (c f) -> p c f", c=4),
            ).then_inc(sd, 16)

        @k.step("gpsimd", deps=[s_ckvst], cc=True)
        def s_agc(e):
            return e.collective_compute(
                "AllGather", ALU.bypass, replica_groups=rg4,
                ins=[ckvp.ap().opt()], outs=[ckva.ap().opt()])

        @k.step("gpsimd", deps=[s_ckvtst], cc=True)
        def s_agct(e):
            return e.collective_compute(
                "AllGather", ALU.bypass, replica_groups=rg4,
                ins=[ckvtp.ap().opt()], outs=[ckvta.ap().opt()])

        wkrT = v16(O_R1, 32768)
        tcos2 = v32(O_R3 + 7168, 512)
        tsin2 = v32(O_R3 + 8192, 512)
        qtile = v16(O_R3 + 3072, 2048)
        rstg = v16(O_R3 + 5120, 2048)
        krTt = v16(O_rtm2, 2048)

        @k.step("sync", deps=[s_ckvtst], ndma=2)
        def s_tb2(e, sd):
            e.dma_start(out=tcos2.rearrange("p (t f) -> p t f", t=TILES),
                        in_=bass.AP(cosk.tensor, 0,
                                    [[64, 128], [128 * 64, TILES], [1, 64]])
                        ).then_inc(sd, 16)
            e.dma_start(out=tsin2.rearrange("p (t f) -> p t f", t=TILES),
                        in_=bass.AP(sink.tensor, 0,
                                    [[64, 128], [128 * 64, TILES], [1, 64]])
                        ).then_inc(sd, 16)

        @k.step("gpsimd", deps=[j2_done, s_ckvst], ndma=1)
        def s_krl(e, sd):
            e.dma_start(
                out=wkrT.rearrange("p (c f) -> p c f", c=DCH),
                in_=bass.AP(wkr.tensor, 0, [[D, 128], [128 * D, DCH], [1, D]]),
            ).then_inc(sd, 16)

        kr_tr = []
        for idx, (rc, cc) in enumerate(
                [(r, c) for r in range(DCH) for c in range(r, DCH)]):
            deps = [s_krl, j2_done] if idx < 2 else [kr_tr[idx - 2]]
            pss = pt1 if idx % 2 == 0 else pt2

            @k.step("tensor", deps=deps)
            def s_kt(e, rc=rc, cc=cc, pss=pss):
                i1 = e.transpose(pss[:, 0:128],
                                 wkrT[:, rc * D + cc * 128:
                                      rc * D + cc * 128 + 128], identb[:])
                if cc != rc:
                    i1 = e.transpose(pss[:, 128:256],
                                     wkrT[:, cc * D + rc * 128:
                                          cc * D + rc * 128 + 128], identb[:])
                return i1

            @k.step("scalar", deps=[s_kt])
            def s_ke(e, rc=rc, cc=cc, pss=pss):
                i1 = e.copy(out=wkrT[:, cc * D + rc * 128:
                                     cc * D + rc * 128 + 128], in_=pss[:, 0:128])
                if cc != rc:
                    i1 = e.copy(out=wkrT[:, rc * D + cc * 128:
                                         rc * D + cc * 128 + 128],
                                in_=pss[:, 128:256])
                return i1
            kr_tr.append(s_ke)
        wkr_done = kr_tr[-1]

        def rope_emit(e, t, ctab, stab):
            def sv3(apo, extra, dims):
                return bass.AP(apo.tensor, apo.offset + extra,
                               [[apo.ap[0][0], 128]] + dims)
            ev = sv3(rtmp, 0, [[128, DCH], [2, 64]])
            od = sv3(rtmp, 1, [[128, DCH], [2, 64]])
            cv = sv3(ctab, t * 64, [[0, DCH], [1, 64]])
            sv = sv3(stab, t * 64, [[0, DCH], [1, 64]])
            oev = sv3(qtile, 0, [[128, DCH], [2, 64]])
            ood = sv3(qtile, 1, [[128, DCH], [2, 64]])
            e2 = sv3(rstg, 0, [[128, DCH], [2, 64]])
            o2 = sv3(rstg, 1, [[128, DCH], [2, 64]])
            e.tensor_tensor(out=oev, in0=ev, in1=cv, op=ALU.mult)
            e.tensor_tensor(out=ood, in0=od, in1=cv, op=ALU.mult)
            e.tensor_tensor(out=e2, in0=od, in1=sv, op=ALU.mult)
            e.tensor_tensor(out=o2, in0=ev, in1=sv, op=ALU.mult)
            e.drain()
            e.tensor_tensor(out=oev, in0=oev, in1=e2, op=ALU.subtract)
            return e.tensor_tensor(out=ood, in0=ood, in1=o2, op=ALU.add)

        def proj_rope_store(stat_src, w_src, n_k, ctab, stab, dstT, dst_dram,
                            mm_deps, per_tile_store, tbl_dep):
            ev_s = []
            rp_s = []
            tr_s = []
            st_s = []
            for t in range(TILES):
                for es in range(4):
                    kk2 = t * 4 + es
                    ps = pa if kk2 % 2 == 0 else pb
                    deps = list(mm_deps)
                    if kk2 >= 2:
                        deps.append(ev_s[kk2 - 2])

                    @k.step("tensor", deps=deps)
                    def s_m(e, t=t, es=es, ps=ps, stat_src=stat_src,
                            w_src=w_src, n_k=n_k):
                        last = None
                        for ic in range(n_k):
                            last = e.matmul(
                                ps[:],
                                stat_src[:, ic * TOK + t * 128:
                                         ic * TOK + t * 128 + 128],
                                w_src[:, ic * D + es * 512:
                                      ic * D + es * 512 + 512],
                                start=(ic == 0), stop=(ic == n_k - 1))
                        return last

                    rp_dep = [rp_s[t - 1]] if (es == 0 and t > 0) else []

                    @k.step("scalar", deps=[s_m] + rp_dep)
                    def s_e(e, es=es, ps=ps):
                        return e.copy(out=rtmp[:, es * 512:(es + 1) * 512],
                                      in_=ps[:])
                    ev_s.append(s_e)

                dep_r = [ev_s[t * 4 + 3], tbl_dep]
                if t > 0:
                    dep_r.append(tr_s[-1])

                @k.step("vector", deps=dep_r)
                def s_r(e, t=t, ctab=ctab, stab=stab):
                    return rope_emit(e, t, ctab, stab)
                rp_s.append(s_r)

                for q4 in range(4):
                    gi = t * 4 + q4
                    ps = pt1 if gi % 2 == 0 else pt2
                    deps = [s_r]
                    if gi >= 2:
                        deps.append(tr_s[gi - 2])

                    @k.step("tensor", deps=deps)
                    def s_t(e, q4=q4, ps=ps):
                        last = None
                        for j in range(4):
                            dc = q4 * 4 + j
                            last = e.transpose(
                                ps[:, ts_(j)],
                                qtile[:, dc * 128:(dc + 1) * 128], identb[:])
                        return last

                    dep_te = [s_t]
                    if per_tile_store and q4 == 0 and t > 0:
                        dep_te.append(st_s[-1])

                    @k.step("scalar", deps=dep_te)
                    def s_te2(e, t=t, q4=q4, ps=ps, dstT=dstT):
                        last = None
                        for j in range(4):
                            dc = q4 * 4 + j
                            if per_tile_store:
                                o0 = (q4 * 4 + j) * 128
                                last = e.copy(out=krTt[:, o0:o0 + 128],
                                              in_=ps[:, ts_(j)])
                            else:
                                last = e.copy(
                                    out=dstT[:, dc * TOK + t * 128:
                                             dc * TOK + t * 128 + 128],
                                    in_=ps[:, ts_(j)])
                        return last
                    tr_s.append(s_te2)

                if per_tile_store:
                    @k.step("sync", deps=[tr_s[-1]], ndma=1)
                    def s_sto(e, sd, t=t, dst_dram=dst_dram):
                        e.dma_start(
                            out=bass.AP(dst_dram, t * 128,
                                        [[TOK, 128], [128 * TOK, DCH], [1, 128]]),
                            in_=krTt[:].rearrange("p (c f) -> p c f", c=DCH),
                        ).then_inc(sd, 16)
                    st_s.append(s_sto)
            return (st_s[-1] if per_tile_store else tr_s[-1])

        kr_done = proj_rope_store(x2t, wkrT, DCH, tcos2, tsin2, None, krtp,
                                  [wkr_done], True, s_tb2)

        @k.step("gpsimd", deps=[kr_done], cc=True)
        def s_agk(e):
            return e.collective_compute(
                "AllGather", ALU.bypass, replica_groups=rg4,
                ins=[krtp.ap().opt()], outs=[krta.ap().opt()])

        wqrn = v16(O_R1, 8192)
        wqrT = v16(O_R1 + 8192, 8192)
        tcos = v32(O_R2, 512)
        tsin = v32(O_R2 + 1024, 512)

        @k.step("gpsimd", deps=[kr_done], ndma=1)
        def s_qrl(e, sd):
            e.dma_start(
                out=wqrn.rearrange("p (c f) -> p c f", c=DCH),
                in_=bass.AP(wqr.tensor, 0, [[C, 128], [128 * C, DCH], [1, C]]),
            ).then_inc(sd, 16)

        @k.step("sync", deps=[kr_done], ndma=2)
        def s_tb1(e, sd):
            e.dma_start(out=tcos.rearrange("p (t f) -> p t f", t=TILES),
                        in_=bass.AP(cosq.tensor, 0,
                                    [[64, 128], [128 * 64, TILES], [1, 64]])
                        ).then_inc(sd, 16)
            e.dma_start(out=tsin.rearrange("p (t f) -> p t f", t=TILES),
                        in_=bass.AP(sinq.tensor, 0,
                                    [[64, 128], [128 * 64, TILES], [1, 64]])
                        ).then_inc(sd, 16)

        qr_tr = []
        jobs3 = [(dc, cc) for dc in range(DCH) for cc in range(4)]
        for jg in range(len(jobs3) // 4):
            ps = pt1 if jg % 2 == 0 else pt2
            deps = [s_qrl, kr_done] if jg < 2 else [qr_tr[jg - 2]]

            @k.step("tensor", deps=deps)
            def s_q3(e, jg=jg, ps=ps):
                last = None
                for j in range(4):
                    dc, cc2 = jobs3[jg * 4 + j]
                    last = e.transpose(
                        ps[:, ts_(j)],
                        wqrn[:, dc * C + cc2 * 128: dc * C + cc2 * 128 + 128],
                        identb[:])
                return last

            @k.step("scalar", deps=[s_q3])
            def s_q3e(e, jg=jg, ps=ps):
                last = None
                for j in range(4):
                    dc, cc2 = jobs3[jg * 4 + j]
                    last = e.copy(
                        out=wqrT[:, cc2 * D + dc * 128: cc2 * D + dc * 128 + 128],
                        in_=ps[:, ts_(j)])
                return last
            qr_tr.append(s_q3e)
        wqr_done = qr_tr[-1]

        qr_done = proj_rope_store(cqT, wqrT, 4, tcos, tsin, qrT, None,
                                  [wqr_done], False, s_tb1)

        kb_kr = [v16(O_R1, 8192), v16(O_R1 + 8192, 8192)]
        kb_ct = [v16(O_R2, 2048), v16(O_R2 + 2048, 2048)]
        kb_cv = [v16(O_R2 + 4096, 2048), v16(O_R2 + 6144, 2048)]
        ost = [v32(O_R2 + 8192 + i * 1024, 512) for i in range(TILES)]

        @k.step("vector", deps=[qr_done, s_agk])
        def s_finit(e):
            e.memset(mrun[:], -1e30)
            e.memset(lrun[:], 0.0)
            last = None
            for t in range(TILES):
                last = e.memset(ost[t][:], 0.0)
            return last

        kdma = []
        smx_v2 = []
        smx_s1 = []
        pt_ev = []
        ou_mm = []
        pend = []

        def emit_pt_ov(kk2, kt, t, ps, s_s1h):
            deps = [s_s1h]
            if kk2 >= 1:
                deps.append(pt_ev[kk2 - 1])

            @k.step("tensor", deps=deps)
            def s_pt(e):
                last = None
                for j in range(4):
                    last = e.transpose(ppt[:, ts_(j)],
                                       ptile[:, j * 128:(j + 1) * 128],
                                       identb[:])
                return last

            dep_e = [s_pt]
            if kk2 >= 1:
                dep_e.append(ou_mm[kk2 - 1])

            @k.step("scalar", deps=dep_e)
            def s_pte(e):
                return e.copy(out=pTt[:], in_=ppt[:])
            pt_ev.append(s_pte)

            dep_o = [s_pte]
            if kk2 >= 1:
                dep_o.append(smx_v2[kk2 - 1])

            @k.step("tensor", deps=dep_o)
            def s_ov(e, kt=kt):
                last = None
                for cc in range(4):
                    last = e.matmul(
                        pov[:],
                        pTt[:, cc * 128:(cc + 1) * 128],
                        kb_cv[kt % 2][:, cc * C:(cc + 1) * C],
                        start=(cc == 0), stop=(cc == 3))
                return last
            ou_mm.append(s_ov)

            @k.step("vector", deps=[s_ov, s_s1h])
            def s_v2(e, t=t):
                e.scalar_tensor_tensor(out=lrun[:, t:t + 1],
                                       in0=lrun[:, t:t + 1],
                                       scalar=fCO[:, t:t + 1],
                                       in1=fRS[:, t:t + 1], op0=ALU.mult,
                                       op1=ALU.add)
                return e.scalar_tensor_tensor(out=ost[t][:], in0=ost[t][:],
                                              scalar=fCO[:, t:t + 1],
                                              in1=pov[:], op0=ALU.mult,
                                              op1=ALU.add)
            smx_v2.append(s_v2)
            return s_pt

        last_pt = [None]
        for kt in range(KTS):
            deps = [s_agk, s_agc, s_agct, s_finit, qr_done]
            if kt >= 2:
                deps.append(smx_v2[(kt - 2) * TILES + TILES - 1])

            @k.step("sync", deps=deps, ndma=3)
            def s_kd(e, sd, kt=kt):
                b = kt // 2
                c0 = (kt % 2) * 512
                e.dma_start(
                    out=kb_kr[kt % 2].rearrange("p (c f) -> p c f", c=DCH),
                    in_=bass.AP(krta, (b * D) * TOK + c0,
                                [[TOK, 128], [128 * TOK, DCH], [1, 512]]),
                ).then_inc(sd, 16)
                e.dma_start(
                    out=kb_ct[kt % 2].rearrange("p (c f) -> p c f", c=4),
                    in_=bass.AP(ckvta, (b * C) * TOK + c0,
                                [[TOK, 128], [128 * TOK, 4], [1, 512]]),
                ).then_inc(sd, 16)
                e.dma_start(
                    out=kb_cv[kt % 2].rearrange("p (c f) -> p c f", c=4),
                    in_=bass.AP(ckva, kt * 512 * C,
                                [[C, 128], [128 * C, 4], [1, C]]),
                ).then_inc(sd, 16)
            kdma.append(s_kd)

            for t in range(TILES):
                kk2 = kt * TILES + t
                ps = psc if kk2 % 2 == 0 else psc2
                deps = [s_kd]
                if kk2 >= 2:
                    deps.append(smx_s1[kk2 - 2])

                @k.step("tensor", deps=deps)
                def s_sc2(e, kt=kt, t=t, ps=ps):
                    last = None
                    for cc in range(4):
                        last = e.matmul(
                            ps[:],
                            aqT[:, cc * TOK + t * 128: cc * TOK + t * 128 + 128],
                            kb_ct[kt % 2][:, cc * 512:(cc + 1) * 512],
                            start=(cc == 0), stop=False)
                    for ec in range(DCH):
                        last = e.matmul(
                            ps[:],
                            qrT[:, ec * TOK + t * 128: ec * TOK + t * 128 + 128],
                            kb_kr[kt % 2][:, ec * 512:(ec + 1) * 512],
                            start=False, stop=(ec == DCH - 1))
                    return last

                @k.step("vector", deps=[s_sc2])
                def s_v1(e, t=t, ps=ps):
                    e.reduce_max(out=sv1[:], in_=ps[:], axis=AX.X)
                    e.drain()
                    e.tensor_max(out=sv2[:], in0=sv1[:], in1=mrun[:, t:t + 1])
                    e.drain()
                    e.tensor_sub(out=fCA[:, t:t + 1], in0=mrun[:, t:t + 1],
                                 in1=sv2[:])
                    e.drain()
                    e.tensor_copy(out=mrun[:, t:t + 1], in_=sv2[:])
                    return e.tensor_scalar(out=fNM[:, t:t + 1], in0=sv2[:],
                                           scalar1=-1.0, scalar2=None,
                                           op0=ALU.mult)

                if pend:
                    last_pt[0] = emit_pt_ov(*pend.pop())
                dep_s1 = [s_v1]
                if last_pt[0] is not None:
                    dep_s1.append(last_pt[0])

                @k.step("scalar", deps=dep_s1)
                def s_s1(e, t=t, ps=ps):
                    e.activation(out=fCO[:, t:t + 1], in_=fCA[:, t:t + 1],
                                 func=AF.Exp)
                    return e.activation(out=ptile[:], in_=ps[:], func=AF.Exp,
                                        bias=fNM[:, t:t + 1],
                                        accum_out=fRS[:, t:t + 1])
                smx_s1.append(s_s1)
                pend.append((kk2, kt, t, ps, s_s1))

        emit_pt_ov(*pend.pop())

        u_done = []
        um_ev = []
        for t in range(TILES):
            @k.step("vector", deps=[smx_v2[-1]] + ([u_done[-1]] if u_done else []))
            def s_uo(e, t=t):
                e.reciprocal(out=sv1[:], in_=lrun[:, t:t + 1])
                e.drain()
                return e.tensor_scalar(out=ptile[:, 0:C], in0=ost[t][:],
                                       scalar1=sv1[:], scalar2=None,
                                       op0=ALU.mult)

            @k.step("tensor", deps=[s_uo] + ([um_ev[-1]] if um_ev else []))
            def s_ut(e):
                last = None
                for j in range(4):
                    last = e.transpose(ppt[:, ts_(j)],
                                       ptile[:, j * 128:(j + 1) * 128],
                                       identb[:])
                return last

            @k.step("scalar", deps=[s_ut])
            def s_ue(e):
                return e.copy(out=pTt[:], in_=ppt[:])

            um_s = []
            for es in range(4):
                ps = psc if es % 2 == 0 else psc2
                deps = [s_ue]
                if es >= 2:
                    deps.append(um_ev[-2 + es - 2 + 1] if False else um_ev[-1])

                @k.step("tensor", deps=deps)
                def s_um(e, es=es, ps=ps):
                    last = None
                    for cc in range(4):
                        last = e.matmul(
                            ps[:],
                            pTt[:, cc * 128:(cc + 1) * 128],
                            absw[:, cc * D + es * 512: cc * D + es * 512 + 512],
                            start=(cc == 0), stop=(cc == 3))
                    return last
                um_s.append(s_um)

                @k.step("scalar", deps=[s_um])
                def s_ume(e, es=es, ps=ps):
                    return e.copy(out=rtmp[:, es * 512:(es + 1) * 512],
                                  in_=ps[:])
                um_ev.append(s_ume)

            @k.step("scalar", deps=[um_ev[-1]])
            def s_us(e, t=t):
                e.activation(out=rtm2[:], in_=rtmp[:], func=AF.Square,
                             accum_out=ss1[:])
                e.drain()
                e.activation(out=ss2[:], in_=ss1[:], func=AF.Ln, bias=cbv(CB_RMS),
                             scale=cbv(CB_ID))
                e.drain()
                return e.activation(out=cRS[:, t:t + 1], in_=ss2[:],
                                    func=AF.Exp, scale=cbv(CB_MH))

            @k.step("vector", deps=[s_us, s_gbc])
            def s_ux(e, t=t):
                xt = x_sb[:, t * D:(t + 1) * D]
                e.scalar_tensor_tensor(out=rtm2[:], in0=rtmp[:],
                                       scalar=cRS[:, t:t + 1], in1=g3b[:],
                                       op0=ALU.mult, op1=ALU.mult)
                e.drain()
                return e.tensor_add(out=xt, in0=xt, in1=rtm2[:])
            u_done.append(s_ux)

        k.barrier()
        s_.close()

    attn_phase()
    ffn_phase("b")

    @k.step("gpsimd", ndma=1)
    def s_out(e, sd):
        e.dma_start(
            out=bass.AP(out.tensor, 0, [[D, 128], [128 * D, TILES], [1, D]]),
            in_=x_sb[:].rearrange("p (t d) -> p t d", t=TILES),
        ).then_inc(sd, 16)

    @k.step("gpsimd", deps=[s_out])
    def s_fin(e):
        return e.memset(svrow[0:1, 0:1], 0.0)

    k.lower(nc)
    top.close()
    return nc, tapd


def _host_inputs(inputs):
    h = np.ascontiguousarray(
        np.asarray(inputs["L0_attn_h"]).reshape(NCORES * TOK, D))
    wsyn = np.ascontiguousarray(np.asarray(inputs["L0_ffn_W_syn"]))
    wout = np.ascontiguousarray(np.asarray(inputs["L0_ffn_W_out"]))
    f32 = np.float32
    i = np.arange(64, dtype=np.float64)
    freqs = 1.0 / (10000.0 ** (2.0 * i / 128.0))
    ident = np.eye(128).astype(np.float16)
    ones1 = np.ones((1, 128), np.float16)
    onesf = np.ones((128, 1), f32)
    iota = np.broadcast_to(np.arange(8, dtype=f32), (128, 8)).copy()
    maps = []
    for c in range(NCORES):
        pos = (c % 4) * TOK + np.arange(TOK, dtype=np.float64)
        ang = pos[:, None] * freqs[None, :]
        cos = np.cos(ang).astype(f32)
        sin = np.sin(ang).astype(f32)
        m = {
            "h": np.ascontiguousarray(h[c * TOK:(c + 1) * TOK]),
            "wsyn": np.ascontiguousarray(wsyn[c * TOK:(c + 1) * TOK]),
            "wout": np.ascontiguousarray(wout[c * 256:(c + 1) * 256]),
            "wdkv": np.asarray(inputs["L0_attn_w_d_kv"]),
            "wdq": np.asarray(inputs["L0_attn_w_d_q"]),
            "wuk": np.asarray(inputs["L0_attn_w_u_k"]),
            "wuv": np.asarray(inputs["L0_attn_w_u_v"]),
            "wuq": np.asarray(inputs["L0_attn_w_u_q"]),
            "wkr": np.asarray(inputs["L0_attn_w_kr"]),
            "wqr": np.asarray(inputs["L0_attn_w_qr"]),
            "wo": np.asarray(inputs["L0_attn_w_o"]),
            "g1": np.asarray(inputs["L0_attn_g1"]).reshape(1, C),
            "g2": np.asarray(inputs["L0_attn_g2"]).reshape(1, C),
            "g3": np.asarray(inputs["L0_attn_g3"]).reshape(1, D),
            "bsyn": np.asarray(inputs["L0_ffn_b_syn"]).reshape(1, N),
            "bout": np.asarray(inputs["L0_ffn_b_out"]).reshape(1, D),
            "gamma": np.asarray(inputs["L0_ffn_gamma"]).reshape(1, D),
            "beta": np.asarray(inputs["L0_ffn_beta"]).reshape(1, D),
            "cosq": (cos / 12.0).astype(f32),
            "sinq": (sin / 12.0).astype(f32),
            "cosk": cos,
            "sink": sin,
            "ident": ident,
            "ones1": ones1,
            "onesf": onesf,
            "iota8": iota,
        }
        maps.append(m)
    return maps


def kernel(**inputs):
    nc, _ = build()
    maps = _host_inputs(inputs)
    res = run_bass_kernel_spmd(nc, maps, core_ids=list(range(NCORES)))
    outs = np.concatenate([r["out"] for r in res.results], axis=0)
    return np.ascontiguousarray(outs.reshape(2, SEQ, D).astype(np.float32))


# version 13
# speedup vs baseline: 1.0833x; 1.0833x over previous
import contextlib

import numpy as np
import ml_dtypes
import concourse.bass as bass
import concourse.mybir as mybir
from concourse.bass_utils import run_bass_kernel_spmd

F32 = mybir.dt.float32
F16 = mybir.dt.float16
AF = mybir.ActivationFunctionType
ALU = mybir.AluOpType
AX = mybir.AxisListType

NCORES = 8
TOK = 1024
D = 2048
N = 8192
C = 512
SEQ = 4096
TILES = 8
GT = 4
NGRP = 2
HSLB = 32
DCH = 16
NCH = 64
KTS = 8
KKT = 819.0
CBIG = 100.0
LN_EPS = 1e-5
RMS_EPS = 1.1920929e-07

ENGS = ("sync", "scalar", "vector", "tensor", "gpsimd")


def ts_(i, n=128):
    return slice(i * n, (i + 1) * n)


class Step:
    __slots__ = ("eng", "kind", "seq", "slot", "async_")


NPOOL = 12


class K:

    def __init__(self):
        self.items = []
        self.ce = {e: 0 for e in ENGS}
        self.dcount = {e: 0 for e in ENGS}
        self.dval = {(e, i): 0 for e in ENGS for i in range(NPOOL)}
        self.dhist = {e: [] for e in ENGS}
        self.lastb = ()

    def step(self, eng, deps=(), ndma=0, cc=False):
        def deco(fn):
            st = Step()
            st.eng = eng
            st.async_ = cc
            if ndma:
                st.kind = "d"
                idx = self.dcount[eng]
                self.dcount[eng] += 1
                st.slot = idx % NPOOL
                self.dval[(eng, st.slot)] += ndma * 16
                st.seq = self.dval[(eng, st.slot)]
                extra = ()
                if idx >= NPOOL:
                    extra = (self.dhist[eng][idx - NPOOL],)
                self.dhist[eng].append(st)
                self.items.append((st, fn,
                                   tuple(deps) + extra + tuple(self.lastb)))
            else:
                st.kind = "c"
                st.slot = -1
                self.ce[eng] += 1
                st.seq = self.ce[eng]
                self.items.append((st, fn, tuple(deps) + tuple(self.lastb)))
            return st
        return deco

    def barrier(self, engines=None):
        engines = tuple(engines) if engines is not None else ENGS
        lasts = {}
        for st, fn, deps in self.items:
            if st.eng in engines:
                lasts[(st.eng, st.kind == "d", st.slot)] = st
        deps = tuple(lasts.values())
        outs = []
        for e in engines:
            st = Step()
            st.eng = e
            st.kind = "b"
            st.slot = -1
            st.async_ = False
            self.ce[e] += 1
            st.seq = self.ce[e]
            self.items.append((st, None, deps))
            outs.append(st)
        self.lastb = tuple(outs)
        return outs

    def lower(self, nc):
        with contextlib.ExitStack() as ctx:
            sems = {}
            for e in ENGS:
                sems["se_" + e] = ctx.enter_context(nc.semaphore("se_" + e))
                for i in range(NPOOL):
                    sems[f"sd_{e}_{i}"] = ctx.enter_context(
                        nc.semaphore(f"sd_{e}_{i}"))
            block = ctx.enter_context(nc.Block())
            streams = {e: [] for e in ENGS}
            for st, fn, deps in self.items:
                streams[st.eng].append((st, fn, deps))

            def mk(e):
                def run(eng):
                    mark = {}
                    for st, fn, deps in streams[e]:
                        need_drain = False
                        for d in deps:
                            if d.kind == "d":
                                sem, val = f"sd_{d.eng}_{d.slot}", d.seq
                            else:
                                if d.eng == e and not getattr(d, "async_", False):
                                    need_drain = True
                                    continue
                                sem, val = "se_" + d.eng, d.seq
                            if mark.get(sem, 0) >= val:
                                continue
                            mark[sem] = val
                            eng.wait_ge(sems[sem], val)
                        if need_drain:
                            eng.drain()
                        if st.kind == "b":
                            eng.sem_inc(sems["se_" + e], 1)
                        elif st.kind == "d":
                            fn(eng, sems[f"sd_{e}_{st.slot}"])
                        else:
                            last = fn(eng)
                            last.then_inc(sems["se_" + e], 1)
                return run
            block.sync(mk("sync"))
            block.scalar(mk("scalar"))
            block.vector(mk("vector"))
            block.tensor(mk("tensor"))
            block.gpsimd(mk("gpsimd"))


def build(taps=()):
    nc = bass.Bass("TRN2", target_bir_lowering=False)
    k = K()
    tapd = {}

    def din(name, shape, dt=F32):
        return nc.dram_tensor(name, shape, dt, kind="ExternalInput").ap()

    h = din("h", [TOK, D])
    wsyn = din("wsyn", [TOK, D])
    wout = din("wout", [256, N])
    wdkv = din("wdkv", [C, D])
    wdq = din("wdq", [C, D])
    wuk = din("wuk", [D, C])
    wuv = din("wuv", [D, C])
    wuq = din("wuq", [D, C])
    wkr = din("wkr", [D, D])
    wqr = din("wqr", [D, C])
    wo = din("wo", [D, D])
    g1 = din("g1", [1, C])
    g2 = din("g2", [1, C])
    g3 = din("g3", [1, D])
    bsyn = din("bsyn", [1, N])
    bout = din("bout", [1, D])
    gamma = din("gamma", [1, D])
    beta = din("beta", [1, D])
    cosq = din("cosq", [TOK, 64])
    sinq = din("sinq", [TOK, 64])
    cosk = din("cosk", [TOK, 64])
    sink = din("sink", [TOK, 64])
    ident = din("ident", [128, 128], F16)
    ones1 = din("ones1", [1, 128], F16)
    onesf = din("onesf", [128, 1])
    iota8 = din("iota8", [128, 8])
    out = nc.dram_tensor("out", [TOK, D], F32, kind="ExternalOutput").ap()

    sqp = nc.dram_tensor("sqp", [1, 2], F32)
    sqa = nc.dram_tensor("sqa", [1, 2], F32)
    tsynp4 = [nc.dram_tensor(f"tsynp{j}", [D, 256], F16) for j in range(4)]
    tsyna4 = [nc.dram_tensor(f"tsyna{j}", [NCORES * D, 256], F16)
              for j in range(4)]
    toutp2 = [nc.dram_tensor(f"toutp{j}", [N // 2, 256], F16)
              for j in range(2)]
    touta2 = [nc.dram_tensor(f"touta{j}", [NCORES * N // 2, 256], F16)
              for j in range(2)]
    bsynp_d = nc.dram_tensor("bsynp_d", [1, N], F16)
    boutp_d = nc.dram_tensor("boutp_d", [1, D], F16)
    krtp = nc.dram_tensor("krtp", [D, TOK], F16)
    krta = nc.dram_tensor("krta", [4 * D, TOK], F16)
    ckvp = nc.dram_tensor("ckvp", [TOK, C], F16)
    ckva = nc.dram_tensor("ckva", [4 * TOK, C], F16)
    ckvtp = nc.dram_tensor("ckvtp", [C, TOK], F16)
    ckvta = nc.dram_tensor("ckvta", [4 * C, TOK], F16)

    top = contextlib.ExitStack()
    e_ = top.enter_context
    x_sb = e_(nc.sbuf_tensor("x_sb", [128, TILES * D], F16))
    identb = e_(nc.sbuf_tensor("identb", [128, 128], F16))
    ones1b = e_(nc.sbuf_tensor("ones1b", [1, 128], F16))
    onesfb = e_(nc.sbuf_tensor("onesfb", [128, 1], F32))
    iotab = e_(nc.sbuf_tensor("iotab", [128, 8], F32))
    gb2 = e_(nc.sbuf_tensor("gb2", [128, D], F16))
    bb2 = e_(nc.sbuf_tensor("bb2", [128, D], F16))
    ssynb = e_(nc.sbuf_tensor("ssynb", [128, 1], F32))
    soutb = e_(nc.sbuf_tensor("soutb", [128, 1], F32))
    hsynb = e_(nc.sbuf_tensor("hsynb", [128, 1], F32))
    houtb = e_(nc.sbuf_tensor("houtb", [128, 1], F32))
    sv1 = e_(nc.sbuf_tensor("sv1", [128, 1], F32))
    sv2 = e_(nc.sbuf_tensor("sv2", [128, 1], F32))
    sv3 = e_(nc.sbuf_tensor("sv3", [128, 1], F32))
    sv4 = e_(nc.sbuf_tensor("sv4", [128, 1], F32))
    ss1 = e_(nc.sbuf_tensor("ss1", [128, 1], F32))
    ss2 = e_(nc.sbuf_tensor("ss2", [128, 1], F32))
    cS = e_(nc.sbuf_tensor("cS", [128, 8], F32))
    cQ = e_(nc.sbuf_tensor("cQ", [128, 8], F32))
    cMU = e_(nc.sbuf_tensor("cMU", [128, 8], F32))
    cVR = e_(nc.sbuf_tensor("cVR", [128, 8], F32))
    cRS = e_(nc.sbuf_tensor("cRS", [128, 8], F32))
    cTH = e_(nc.sbuf_tensor("cTH", [128, 8], F32))
    u8 = e_(nc.sbuf_tensor("u8", [128, 8], F32))
    d8 = e_(nc.sbuf_tensor("d8", [128, 8], F32))
    hot = e_(nc.sbuf_tensor("hot", [128, 8], F32))
    cb = e_(nc.sbuf_tensor("cb", [128, 8], F32))
    svrow = e_(nc.sbuf_tensor("svrow", [1, 4], F32))
    identso = e_(nc.sbuf_tensor("identso", [128, 128], F16))
    u16 = e_(nc.sbuf_tensor("u16", [128, 16], F32))
    d16 = e_(nc.sbuf_tensor("d16", [128, 16], F32))
    onesr32 = e_(nc.sbuf_tensor("onesr32", [1, 128], F32))
    CB_LN, CB_RMS, CB_MH, CB_PH, CB_MO, CB_I12, CB_IC, CB_ID = range(8)

    def cbv(i):
        return cb[:, i:i + 1]

    @k.step("sync", ndma=4)
    def s_c0(e, sd):
        e.dma_start(out=identb[:], in_=ident).then_inc(sd, 16)
        e.dma_start(out=ones1b[:], in_=ones1).then_inc(sd, 16)
        e.dma_start(out=onesfb[:], in_=onesf).then_inc(sd, 16)
        e.dma_start(out=iotab[:], in_=iota8).then_inc(sd, 16)

    @k.step("vector")
    def s_cb(e):
        e.memset(cb[:, 0:1], LN_EPS)
        e.memset(cb[:, 1:2], RMS_EPS)
        e.memset(cb[:, 2:3], -0.5)
        e.memset(cb[:, 3:4], 0.5)
        e.memset(cb[:, 4:5], -1.0)
        e.memset(onesr32[:], 1.0)
        e.memset(cb[:, 5:6], 1.0 / 12.0)
        e.memset(cb[:, 6:7], 1.0 / C)
        return e.memset(cb[:, 7:8], 1.0 / D)

    @k.step("gpsimd", ndma=3)
    def s_x0(e, sd):
        e.dma_start(
            out=x_sb[:].rearrange("p (t d) -> p t d", t=TILES),
            in_=bass.AP(h.tensor, 0, [[D, 128], [128 * D, TILES], [1, D]]),
        ).then_inc(sd, 16)
        e.dma_start(out=gb2[0:1, :], in_=gamma).then_inc(sd, 16)
        e.dma_start(out=bb2[0:1, :], in_=beta).then_inc(sd, 16)

    @k.step("vector", deps=[s_x0])
    def s_gb(e):
        e.tensor_scalar(out=gb2[0:1, :], in0=gb2[0:1, :], scalar1=0.5,
                        scalar2=None, op0=ALU.mult)
        return e.tensor_scalar(out=bb2[0:1, :], in0=bb2[0:1, :], scalar1=0.5,
                               scalar2=None, op0=ALU.mult)


    qs = contextlib.ExitStack()
    wf = qs.enter_context(nc.sbuf_tensor("wf", [128, 16384], F32))
    tq = qs.enter_context(nc.sbuf_tensor("tq", [128, 16384], F16))
    tq2 = qs.enter_context(nc.sbuf_tensor("tq2", [128, 16384], F16))
    tq3 = qs.enter_context(nc.sbuf_tensor("tq3", [128, 16384], F16))
    sqsb = qs.enter_context(nc.sbuf_tensor("sqsb", [1, 2], F32))
    qacc = qs.enter_context(nc.sbuf_tensor("qacc", [128, 2], F32))
    pq = qs.enter_context(nc.psum_tensor("pq", [128, 1024], F16))
    pq2 = qs.enter_context(nc.psum_tensor("pq2", [128, 1024], F16))
    psml = qs.enter_context(nc.psum_tensor("psml", [1, 2], F32))
    pbc = qs.enter_context(nc.psum_tensor("pbc", [128, 512], F32))

    bcast_ev = []
    for which, buf in ((0, gb2), (1, bb2)):
        for sl in range(4):
            @k.step("tensor", deps=[s_gb, s_c0] + ([bcast_ev[-1]] if bcast_ev else []))
            def s_bcm(e, buf=buf, sl=sl):
                return e.matmul(pbc[:], ones1b[0:1, :],
                                buf[0:1, sl * 512:(sl + 1) * 512],
                                start=True, stop=True)

            @k.step("scalar", deps=[s_bcm])
            def s_bce(e, buf=buf, sl=sl):
                return e.copy(out=buf[:, sl * 512:(sl + 1) * 512], in_=pbc[:])
            bcast_ev.append(s_bce)
    s_gbb = bcast_ev[-1]

    @k.step("sync", ndma=1)
    def s_ldsyn(e, sd):
        e.dma_start(
            out=wf[:].rearrange("p (c f) -> p c f", c=8),
            in_=bass.AP(wsyn.tensor, 0, [[D, 128], [128 * D, 8], [1, D]]),
        ).then_inc(sd, 16)

    @k.step("gpsimd", ndma=1)
    def s_ldoutf16(e, sd):
        e.dma_start(
            out=tq[:].rearrange("p (c f) -> p c f", c=2),
            in_=bass.AP(wout.tensor, 0, [[N, 128], [128 * N, 2], [1, N]]),
        ).then_inc(sd, 16)

    @k.step("scalar", deps=[s_ldsyn])
    def s_abs0(e):
        return e.activation(out=tq2[:], in_=wf[:], func=AF.Abs,
                            accum_out=qacc[:, 0:1])

    @k.step("scalar", deps=[s_ldoutf16])
    def s_abs1(e):
        return e.activation(out=tq3[:], in_=tq[:], func=AF.Abs,
                            accum_out=qacc[:, 1:2])

    @k.step("tensor", deps=[s_abs0, s_abs1, s_c0])
    def s_sqmm(e):
        e.matmul(psml[0:1, 0:1], qacc[:, 0:1], onesfb[:], start=True, stop=True)
        return e.matmul(psml[0:1, 1:2], qacc[:, 1:2], onesfb[:], start=True,
                        stop=True)

    @k.step("scalar", deps=[s_sqmm])
    def s_sqev(e):
        return e.copy(out=sqsb[:], in_=psml[:])

    @k.step("gpsimd", deps=[s_sqev], ndma=1)
    def s_sqst(e, sd):
        e.dma_start(out=sqp.ap(), in_=sqsb[:]).then_inc(sd, 16)

    @k.step("gpsimd", deps=[s_sqst], cc=True)
    def s_ar(e):
        return e.collective_compute(
            "AllReduce", ALU.add, replica_groups=[list(range(NCORES))],
            ins=[sqp.ap().opt()], outs=[sqa.ap().opt()])

    @k.step("gpsimd", deps=[s_ar], ndma=1)
    def s_sqld(e, sd):
        e.dma_start(out=sqsb[:], in_=sqa.ap()).then_inc(sd, 16)

    nelem = float(N) * float(D)

    @k.step("vector", deps=[s_sqld])
    def s_scale(e):
        e.tensor_scalar(out=svrow[0:1, 0:1], in0=sqsb[0:1, 0:1],
                        scalar1=1.0 / nelem, scalar2=None, op0=ALU.mult)
        e.tensor_scalar(out=svrow[0:1, 1:2], in0=sqsb[0:1, 1:2],
                        scalar1=1.0 / nelem, scalar2=None, op0=ALU.mult)
        e.drain()
        e.tensor_scalar(out=svrow[0:1, 2:3], in0=svrow[0:1, 0:1], scalar1=0.5,
                        scalar2=0.5e-8, op0=ALU.mult, op1=ALU.add)
        return e.tensor_scalar(out=svrow[0:1, 3:4], in0=svrow[0:1, 1:2],
                               scalar1=0.5, scalar2=0.5e-8, op0=ALU.mult,
                               op1=ALU.add)

    @k.step("tensor", deps=[s_scale, s_cb])
    def s_scm(e):
        return e.matmul(pbc[:, 0:4], onesr32[0:1, :], svrow[0:1, :],
                        start=True, stop=True)

    @k.step("scalar", deps=[s_scm])
    def s_scbc(e):
        e.copy(out=ssynb[:], in_=pbc[:, 0:1])
        e.copy(out=soutb[:], in_=pbc[:, 1:2])
        e.copy(out=hsynb[:], in_=pbc[:, 2:3])
        return e.copy(out=houtb[:], in_=pbc[:, 3:4])

    @k.step("vector", deps=[s_scbc, s_c0])
    def s_iso(e):
        e.reciprocal(out=sv4[:], in_=soutb[:])
        e.drain()
        e.tensor_scalar(out=identso[:], in0=identb[:], scalar1=sv4[:],
                        scalar2=None, op0=ALU.mult)
        e.drain()
        return e.tensor_copy(out=sv4[:], in_=soutb[:])

    def quant_mat(which, dep_load, extra_q_deps=()):
        nchk, ncol = (8, D) if which == 0 else (2, N)
        half_sb = hsynb if which == 0 else houtb

        @k.step("vector", deps=[dep_load, s_scbc] + list(extra_q_deps))
        def s_q(e, which=which, half_sb=half_sb):
            e.tensor_scalar(out=sv1[:], in0=half_sb[:], scalar1=-1.0,
                            scalar2=None, op0=ALU.mult)
            src = wf[:] if which == 0 else tq[:]
            e.tensor_scalar(out=tq2[:], in0=src, scalar1=half_sb[:],
                            scalar2=None, op0=ALU.is_ge)
            e.drain()
            e.tensor_scalar(out=tq3[:], in0=src, scalar1=sv1[:],
                            scalar2=None, op0=ALU.is_le)
            e.drain()
            return e.tensor_sub(out=tq[:], in0=tq2[:], in1=tq3[:])

        ccn = ncol // 128
        ngrp = (nchk * ccn) // 8
        tr_steps = []
        te_steps = []
        for g in range(ngrp):
            ps = pq if g % 2 == 0 else pq2
            deps = [s_q, s_c0] if g < 2 else [te_steps[g - 2]]

            @k.step("tensor", deps=deps)
            def s_tr(e, g=g, ps=ps, ccn=ccn, ncol=ncol):
                last = None
                for j in range(8):
                    blk = g * 8 + j
                    rc, cc = blk // ccn, blk % ccn
                    last = e.transpose(
                        ps[:, ts_(j)],
                        tq[:, rc * ncol + cc * 128: rc * ncol + cc * 128 + 128],
                        identb[:])
                return last
            tr_steps.append(s_tr)

            @k.step("scalar", deps=[s_tr])
            def s_te(e, g=g, ps=ps, ccn=ccn, nchk=nchk):
                last = None
                for j in range(8):
                    blk = g * 8 + j
                    rc, cc = blk // ccn, blk % ccn
                    o0 = cc * (128 * nchk) + rc * 128
                    last = e.copy(out=tq2[:, o0:o0 + 128], in_=ps[:, ts_(j)])
                return last
            te_steps.append(s_te)

        wps = []
        if which == 0:
            for j in range(4):
                @k.step("sync", deps=[te_steps[min(4 * j + 3,
                                                   len(te_steps) - 1)]],
                        ndma=1)
                def s_wpj(e, sd, j=j):
                    e.dma_start(
                        out=bass.AP(tsynp4[j], 0,
                                    [[256, 128], [128 * 256, 16], [1, 256]]),
                        in_=bass.AP(tq2[:].tensor, j * 256,
                                    [[16384, 128], [1024, 16], [1, 256]]),
                    ).then_inc(sd, 16)
                wps.append(s_wpj)
        else:
            for j in range(2):
                @k.step("sync", deps=[te_steps[-1]], ndma=1)
                def s_wpj(e, sd, j=j):
                    e.dma_start(
                        out=bass.AP(toutp2[j], 0,
                                    [[256, 128], [128 * 256, 32], [1, 256]]),
                        in_=bass.AP(tq2[:].tensor, j * 32 * 256,
                                    [[16384, 128], [256, 32], [1, 256]]),
                    ).then_inc(sd, 16)
                wps.append(s_wpj)
        return wps, tr_steps[-1]

    wp_out, tr_out = quant_mat(1, s_ldoutf16)
    wp_syn, tr_syn = quant_mat(0, s_ldsyn,
                               extra_q_deps=wp_out + [tr_out])

    ag_syn4 = []
    for j in range(4):
        @k.step("gpsimd", deps=[wp_syn[j]], cc=True)
        def s_agsj(e, j=j):
            return e.collective_compute(
                "AllGather", ALU.bypass, replica_groups=[list(range(NCORES))],
                ins=[tsynp4[j].ap().opt()], outs=[tsyna4[j].ap().opt()])
        ag_syn4.append(s_agsj)
    ag_out2 = []
    for j in range(2):
        @k.step("gpsimd", deps=[wp_out[j]], cc=True)
        def s_agoj(e, j=j):
            return e.collective_compute(
                "AllGather", ALU.bypass, replica_groups=[list(range(NCORES))],
                ins=[toutp2[j].ap().opt()], outs=[touta2[j].ap().opt()])
        ag_out2.append(s_agoj)

    b_steps = []
    for which in range(2):
        bsrc = bsyn if which == 0 else bout
        bdst = bsynp_d if which == 0 else boutp_d
        s_sb = ssynb if which == 0 else soutb
        nn = N if which == 0 else D
        brow_ap = tq3[0:1, 0:nn].bitcast(F32)[0:1, 0:nn] if False else None
        brow = tq3[:].bitcast(F32)

        @k.step("sync", deps=wp_syn + [tr_syn], ndma=1)
        def s_bld(e, sd, bsrc=bsrc, nn=nn, brow=brow):
            e.dma_start(out=brow[0:1, 0:nn], in_=bsrc).then_inc(sd, 16)

        @k.step("vector", deps=[s_bld, s_scbc] + list(b_steps))
        def s_bdiv(e, nn=nn, s_sb=s_sb, brow=brow):
            e.reciprocal(out=sv1[0:1, 0:1], in_=s_sb[0:1, 0:1])
            e.drain()
            return e.tensor_scalar(out=tq[0:1, 0:nn], in0=brow[0:1, 0:nn],
                                   scalar1=sv1[0:1, 0:1], scalar2=None,
                                   op0=ALU.mult)

        @k.step("sync", deps=[s_bdiv], ndma=1)
        def s_bst(e, sd, bdst=bdst, nn=nn):
            e.dma_start(out=bdst.ap()[0:1, 0:nn],
                        in_=tq[0:1, 0:nn]).then_inc(sd, 16)
        b_steps.append(s_bst)

    k.barrier(engines=("sync", "scalar", "vector", "tensor"))
    qs.close()

    def ffn_phase(tag):
        ng_, gt_ = 4, 2
        fs = contextlib.ExitStack()
        en = fs.enter_context
        z2 = [en(nc.sbuf_tensor(f"z{tag}{i}", [128, gt_ * N], F16))
              for i in range(2)]
        x1g2 = [en(nc.sbuf_tensor(f"x1g{tag}{i}", [128, gt_ * D], F16))
                for i in range(2)]
        x1t2 = [en(nc.sbuf_tensor(f"x1t{tag}{i}", [128, DCH * 256], F16))
                for i in range(2)]
        wsl = [en(nc.sbuf_tensor(f"wsl{tag}{i}", [128, DCH * 256], F16))
               for i in range(2)]
        woc = [en(nc.sbuf_tensor(f"woc{tag}{i}", [128, 2048], F16))
               for i in range(3)]
        sB = en(nc.sbuf_tensor(f"sB{tag}", [128, N], F16))
        ysc = en(nc.sbuf_tensor(f"ysc{tag}", [128, N // 2], F16))
        jnk = en(nc.sbuf_tensor(f"jnk{tag}", [128, D], F16))
        tmp4 = [en(nc.sbuf_tensor(f"tmp4{tag}{t}", [128, D], F16))
                for t in range(gt_)]
        bst = [en(nc.sbuf_tensor(f"bst{tag}{i}", [1, 256], F16))
               for i in range(2)]
        bsto = [en(nc.sbuf_tensor(f"bsto{tag}{i}", [1, 512], F16))
                for i in range(2)]
        pz = [en(nc.psum_tensor(f"pz{tag}{i}", [128, 256], F32))
              for i in range(2)]
        pht = [en(nc.psum_tensor(f"pht{tag}{i}", [128, 1024], F16))
               for i in range(2)]
        px = [en(nc.psum_tensor(f"px{tag}{i}", [128, 1024], F16))
              for i in range(2)]
        po = [en(nc.psum_tensor(f"po{tag}{i}", [128, 512], F32))
              for i in range(gt_)]

        zmm, zev, omm = [], [], []
        th_steps, htr_ev, x1tev = [], [], []
        ln_apply_g, x1t_done_g, zmm_g_last, omm_g_last = {}, {}, {}, {}
        hid_g, ep, ep_last, l2a_last, grp_done = {}, {}, {}, {}, {}
        grp_res = {}
        stat_sig = [None]

        def emit_ln1(g):
            buf = g % 2
            apl = []
            for t in range(gt_):
                col = buf * 2 + t
                if g >= 2:
                    deps = [omm_g_last[g - 2]]
                else:
                    deps = [s_x0, s_gbb]

                @k.step("scalar", deps=deps)
                def s_st(e, g=g, t=t, buf=buf, col=col):
                    xt = x_sb[:, (g * gt_ + t) * D:(g * gt_ + t + 1) * D]
                    jk = x1g2[buf][:, t * D:(t + 1) * D]
                    e.activation(out=jk, in_=xt, func=AF.Copy,
                                 accum_out=cS[:, col:col + 1])
                    e.drain()
                    return e.activation(out=jk, in_=xt, func=AF.Square,
                                        accum_out=cQ[:, col:col + 1])

                @k.step("vector", deps=[s_st])
                def s_vr(e, col=col):
                    e.tensor_scalar(out=cMU[:, col:col + 1],
                                    in0=cS[:, col:col + 1],
                                    scalar1=1.0 / D, scalar2=None,
                                    op0=ALU.mult)
                    e.drain()
                    e.tensor_scalar(out=sv1[:], in0=cMU[:, col:col + 1],
                                    scalar1=cMU[:, col:col + 1], scalar2=None,
                                    op0=ALU.mult)
                    e.tensor_scalar(out=cVR[:, col:col + 1],
                                    in0=cQ[:, col:col + 1],
                                    scalar1=1.0 / D, scalar2=None,
                                    op0=ALU.mult)
                    e.drain()
                    return e.tensor_sub(out=cVR[:, col:col + 1],
                                        in0=cVR[:, col:col + 1], in1=sv1[:])

                @k.step("scalar", deps=[s_vr])
                def s_ap(e, g=g, t=t, buf=buf, col=col):
                    xt = x_sb[:, (g * gt_ + t) * D:(g * gt_ + t + 1) * D]
                    e.activation(out=ss1[:], in_=cVR[:, col:col + 1],
                                 func=AF.Ln, bias=cbv(CB_LN))
                    e.drain()
                    e.activation(out=cRS[:, col:col + 1], in_=ss1[:],
                                 func=AF.Exp, scale=cbv(CB_MH))
                    e.drain()
                    e.activation(out=ss2[:], in_=cMU[:, col:col + 1],
                                 func=AF.Identity, scale=cRS[:, col:col + 1])
                    e.drain()
                    e.activation(out=ss2[:], in_=ss2[:], func=AF.Identity,
                                 scale=cbv(CB_MO))
                    e.drain()
                    return e.activation(out=x1g2[buf][:, t * D:(t + 1) * D],
                                        in_=xt, func=AF.Identity, bias=ss2[:],
                                        scale=cRS[:, col:col + 1])
                apl.append(s_ap)
            ln_apply_g[g] = apl

        def emit_x1tr(g):
            buf = g % 2
            for t in range(gt_):
                for q8 in range(2):
                    gi = len(x1tev)
                    ps = px[gi % 2]
                    deps = [ln_apply_g[g][t]]
                    if gi >= 2:
                        deps.append(x1tev[gi - 2])
                    if g >= 2 and t == 0 and q8 == 0:
                        deps.append(zmm_g_last[g - 2])

                    @k.step("tensor", deps=deps)
                    def s_tr(e, g=g, t=t, q8=q8, ps=ps, buf=buf):
                        last = None
                        for j in range(8):
                            dc = q8 * 8 + j
                            last = e.transpose(
                                ps[:, ts_(j)],
                                x1g2[buf][:, t * D + dc * 128:
                                          t * D + dc * 128 + 128],
                                identb[:])
                        return last

                    @k.step("scalar", deps=[s_tr])
                    def s_tev(e, t=t, q8=q8, ps=ps, buf=buf):
                        last = None
                        for j in range(8):
                            dc = q8 * 8 + j
                            last = e.copy(
                                out=x1t2[buf][:, dc * 256 + t * 128:
                                              dc * 256 + t * 128 + 128],
                                in_=ps[:, ts_(j)])
                        return last
                    x1tev.append(s_tev)
            x1t_done_g[g] = x1tev[-1]

        def emit_z(g):
            buf = g % 2
            hs_seq = [p * 4 + j for j in range(4) for p in range(8)]
            for li, hs in enumerate(hs_seq):
                idx = g * HSLB + li
                depw = [zmm[(idx - 2) * gt_ + gt_ - 1]] if idx >= 2 else []
                depw = depw + [ag_syn4[hs % 4]]

                @k.step("sync", deps=depw + [b_steps[0]], ndma=2)
                def s_w(e, sd, hs=hs, idx=idx):
                    e.dma_start(
                        out=wsl[idx % 2][:].rearrange("p (c f) -> p c f",
                                                      c=DCH),
                        in_=bass.AP(tsyna4[hs % 4], ((hs // 4) * D) * 256,
                                    [[256, 128], [128 * 256, DCH], [1, 256]]),
                    ).then_inc(sd, 16)
                    e.dma_start(out=bst[idx % 2][0:1, :],
                                in_=bsynp_d.ap()[0:1, hs * 256:(hs + 1) * 256]
                                ).then_inc(sd, 16)
                s_b = s_w

                for t in range(gt_):
                    kk = idx * gt_ + t
                    psz = pz[kk % 2]
                    deps = [s_w, s_b, x1t_done_g[g]]
                    if kk >= 2:
                        deps.append(zev[kk - 2])

                    @k.step("tensor", deps=deps)
                    def s_zm(e, hs=hs, t=t, psz=psz, idx=idx, buf=buf):
                        e.matmul(psz[:], ones1b[0:1, :], bst[idx % 2][0:1, :],
                                 start=True, stop=False)
                        last = None
                        for dc in range(DCH):
                            last = e.matmul(
                                psz[:],
                                x1t2[buf][:, dc * 256 + t * 128:
                                          dc * 256 + t * 128 + 128],
                                wsl[idx % 2][:, dc * 256:(dc + 1) * 256],
                                start=False, stop=(dc == DCH - 1))
                        return last
                    zmm.append(s_zm)

                    zdeps = [s_zm]
                    if li == 0 and t == 0 and g >= 2:
                        zdeps.append(omm_g_last[g - 2])

                    @k.step("scalar", deps=zdeps)
                    def s_ze(e, hs=hs, t=t, psz=psz, buf=buf):
                        return e.activation(
                            out=z2[buf][:, t * N + hs * 256:
                                        t * N + hs * 256 + 256],
                            in_=psz[:], func=AF.Identity, scale=ssynb[:])
                    zev.append(s_ze)
            zmm_g_last[g] = zmm[-1]

        def emit_stats0():
            z_done = [zev[HSLB * gt_ - 1]]

            @k.step("scalar", deps=z_done)
            def s_zst(e):
                zt = z2[0][:, 0:N]
                e.activation(out=sB[:], in_=zt, func=AF.Copy,
                             accum_out=cS[:, 4:5])
                e.drain()
                return e.activation(out=sB[:], in_=zt, func=AF.Square,
                                    accum_out=cQ[:, 4:5])

            @k.step("vector", deps=[s_zst])
            def s_zvr(e):
                e.tensor_scalar(out=cMU[:, 4:5], in0=cS[:, 4:5],
                                scalar1=1.0 / N, scalar2=None, op0=ALU.mult)
                e.drain()
                e.tensor_scalar(out=sv1[:], in0=cMU[:, 4:5],
                                scalar1=cMU[:, 4:5], scalar2=None,
                                op0=ALU.mult)
                e.tensor_scalar(out=cVR[:, 4:5], in0=cQ[:, 4:5],
                                scalar1=1.0 / N, scalar2=None, op0=ALU.mult)
                e.drain()
                return e.tensor_sub(out=cVR[:, 4:5], in0=cVR[:, 4:5],
                                    in1=sv1[:])

            @k.step("scalar", deps=[s_zvr])
            def s_sig(e):
                e.activation(out=ss1[:], in_=cVR[:, 4:5], func=AF.Ln)
                e.drain()
                return e.activation(out=cRS[:, 4:5], in_=ss1[:],
                                    func=AF.Exp, scale=cbv(CB_PH))
            stat_sig[0] = s_sig

        def emit_th(g, t):
            buf = g % 2
            gt = g * gt_ + t
            z_done = [zev[(g + 1) * HSLB * gt_ - 1]]
            if th_steps:
                sig_dep = th_steps[-1]
            else:
                sig_dep = stat_sig[0]
            deps = [sig_dep] + z_done
            if htr_ev:
                deps.append(htr_ev[-1])

            @k.step("vector", deps=deps)
            def s_th(e, g=g, t=t, buf=buf, gt=gt):
                zt = z2[buf][:, t * N:(t + 1) * N]
                th = cTH[:, 4 + gt % 2:5 + gt % 2]
                sig = cRS[:, 4:5]
                if gt == 0:
                    mu = cMU[:, 4:5]
                    e.tensor_scalar(out=sv1[:], in0=sig, scalar1=1.2815516,
                                    scalar2=None, op0=ALU.mult)
                    e.drain()
                    e.tensor_add(out=th, in0=mu, in1=sv1[:])
                else:
                    e.tensor_copy(out=th, in_=cTH[:, 4 + (gt - 1) % 2:
                                                  5 + (gt - 1) % 2])
                e.drain()
                e.tensor_scalar(out=sB[:], in0=zt, scalar1=th,
                                scalar2=None, op0=ALU.is_ge, op1=ALU.add,
                                accum_out=sv1[:])
                e.drain()
                e.tensor_copy(out=sv2[:], in_=th)
                e.tensor_scalar(out=sv3[:], in0=sv1[:], scalar1=-KKT,
                                scalar2=1.0 / 1437.7, op0=ALU.add,
                                op1=ALU.mult)
                e.drain()
                e.tensor_mul(out=sv3[:], in0=sv3[:], in1=sig)
                e.drain()
                e.tensor_add(out=th, in0=th, in1=sv3[:])
                e.drain()
                e.tensor_scalar(out=sB[:], in0=zt, scalar1=th,
                                scalar2=None, op0=ALU.is_ge, op1=ALU.add,
                                accum_out=sv4[:])
                e.drain()
                e.tensor_sub(out=sv3[:], in0=th, in1=sv2[:])
                e.drain()
                e.tensor_sub(out=sv2[:], in0=sv4[:], in1=sv1[:])
                e.drain()
                e.tensor_scalar(out=sv1[:], in0=sv2[:], scalar1=sv2[:],
                                scalar2=0.25, op0=ALU.mult, op1=ALU.add)
                e.drain()
                e.reciprocal(out=sv1[:], in_=sv1[:])
                e.drain()
                e.tensor_mul(out=sv3[:], in0=sv3[:], in1=sv2[:])
                e.drain()
                e.tensor_mul(out=sv3[:], in0=sv3[:], in1=sv1[:])
                e.drain()
                e.tensor_scalar(out=sv1[:], in0=sig, scalar1=-0.01,
                                scalar2=None, op0=ALU.mult)
                e.drain()
                e.tensor_max(out=sv3[:], in0=sv3[:], in1=sv1[:])
                e.drain()
                e.tensor_scalar(out=sv1[:], in0=sig, scalar1=-1e-5,
                                scalar2=None, op0=ALU.mult)
                e.drain()
                e.tensor_tensor(out=sv3[:], in0=sv3[:], in1=sv1[:],
                                op=ALU.min)
                e.tensor_scalar(out=sv4[:], in0=sv4[:], scalar1=-KKT,
                                scalar2=None, op0=ALU.add)
                e.drain()
                e.tensor_mul(out=sv4[:], in0=sv4[:], in1=sv3[:])
                e.drain()
                e.tensor_sub(out=th, in0=th, in1=sv4[:])
                e.drain()
                e.tensor_scalar(out=sB[:], in0=zt, scalar1=th,
                                scalar2=None, op0=ALU.is_ge, op1=ALU.add,
                                accum_out=sv4[:])
                e.drain()
                e.tensor_scalar(out=sv4[:], in0=sv4[:], scalar1=-KKT,
                                scalar2=None, op0=ALU.add)
                e.drain()
                e.tensor_mul(out=sv4[:], in0=sv4[:], in1=sv3[:])
                e.drain()
                e.tensor_sub(out=th, in0=th, in1=sv4[:])
                e.drain()
                e.tensor_scalar(out=sB[:], in0=zt, scalar1=th,
                                scalar2=None, op0=ALU.is_lt, op1=ALU.add,
                                accum_out=sv1[:])
                e.drain()
                xsc = x1t2[buf][:]
                H = N // 2
                for ci in range(2):
                    mh = sB[:, ci * H:(ci + 1) * H]
                    zh = zt[:, ci * H:(ci + 1) * H]
                    e.scalar_tensor_tensor(out=ysc[:], in0=mh, scalar=-CBIG,
                                           in1=zh, op0=ALU.mult,
                                           op1=ALU.subtract)
                    e.drain()
                    e.max(out=u16[:, 8 * ci:8 * ci + 8], in_=ysc[:])
                    e.tensor_scalar(out=xsc, in0=ysc[:], scalar1=-1.0,
                                    scalar2=-CBIG, op0=ALU.mult, op1=ALU.add)
                    e.drain()
                    e.max(out=d16[:, 8 * ci:8 * ci + 8], in_=xsc)
                e.drain()
                e.max(out=u8[:], in_=u16[:])
                e.max(out=d8[:], in_=d16[:])
                e.drain()
                e.tensor_scalar(out=sv2[:], in0=sv1[:], scalar1=-1.0,
                                scalar2=float(N) - KKT, op0=ALU.mult,
                                op1=ALU.add)
                e.drain()
                e.tensor_scalar(out=sv3[:], in0=sv2[:], scalar1=0.0,
                                scalar2=7.0, op0=ALU.max, op1=ALU.min)
                e.drain()
                e.tensor_scalar(out=hot[:], in0=iotab[:], scalar1=sv3[:],
                                scalar2=None, op0=ALU.is_equal)
                e.drain()
                e.tensor_mul(out=hot[:], in0=hot[:], in1=u8[:])
                e.drain()
                e.reduce_sum(out=sv3[:], in_=hot[:], axis=AX.X, negate=True)
                e.tensor_scalar(out=sv4[:], in0=sv2[:], scalar1=-1.0,
                                scalar2=-1.0, op0=ALU.mult, op1=ALU.add)
                e.drain()
                e.tensor_scalar(out=sv4[:], in0=sv4[:], scalar1=0.0,
                                scalar2=7.0, op0=ALU.max, op1=ALU.min)
                e.drain()
                e.tensor_scalar(out=hot[:], in0=iotab[:], scalar1=sv4[:],
                                scalar2=None, op0=ALU.is_equal)
                e.drain()
                e.tensor_mul(out=hot[:], in0=hot[:], in1=d8[:])
                e.drain()
                e.reduce_sum(out=sv4[:], in_=hot[:], axis=AX.X)
                e.tensor_scalar(out=sv1[:], in0=sv2[:], scalar1=0.0,
                                scalar2=None, op0=ALU.is_ge)
                e.drain()
                e.tensor_mul(out=sv3[:], in0=sv3[:], in1=sv1[:])
                e.drain()
                e.tensor_scalar(out=sv1[:], in0=sv1[:], scalar1=-1.0,
                                scalar2=1.0, op0=ALU.mult, op1=ALU.add)
                e.drain()
                e.tensor_mul(out=sv4[:], in0=sv4[:], in1=sv1[:])
                e.drain()
                e.tensor_add(out=th, in0=sv3[:], in1=sv4[:])
                e.drain()
                return e.scalar_tensor_tensor(out=sB[:], in0=zt, scalar=th,
                                              in1=zt, op0=ALU.is_ge,
                                              op1=ALU.mult)
            th_steps.append(s_th)

            for cg in range(8):
                gi = len(htr_ev)
                ps = pht[gi % 2]
                hdeps = [s_th]
                if gi >= 2:
                    hdeps.append(htr_ev[gi - 2])

                @k.step("tensor", deps=hdeps)
                def s_htr(e, t=t, cg=cg, ps=ps):
                    last = None
                    for j in range(8):
                        c = cg * 8 + j
                        last = e.transpose(ps[:, ts_(j)],
                                           sB[:, c * 128:(c + 1) * 128],
                                           identb[:])
                    return last

                @k.step("scalar", deps=[s_htr])
                def s_hte(e, t=t, cg=cg, ps=ps, buf=buf):
                    last = None
                    for j in range(8):
                        c = cg * 8 + j
                        last = e.activation(
                            out=z2[buf][:, t * N + c * 128:
                                        t * N + c * 128 + 128],
                            in_=ps[:, ts_(j)], func=AF.Relu)
                    return last
                htr_ev.append(s_hte)
            hid_g.setdefault(g, []).append(htr_ev[-1])

        def emit_outs(g):
            buf = g % 2
            for ds in range(4):
                dep_bs = list(b_steps)
                if len(omm) >= 16:
                    dep_bs.append(omm[-16])

                @k.step("sync", deps=dep_bs, ndma=1)
                def s_bout(e, sd, ds=ds):
                    e.dma_start(out=bsto[ds % 2][0:1, :],
                                in_=boutp_d.ap()[0:1, ds * 512:(ds + 1) * 512]
                                ).then_inc(sd, 16)

                for cq in range(NCH // 4):
                    i = len(omm)
                    depw = [omm[i - 3]] if i >= 3 else []
                    depw = depw + [ag_out2[cq // 8]]

                    @k.step("sync", deps=depw, ndma=2)
                    def s_wo(e, sd, ds=ds, cq=cq, i=i):
                        blk0 = ds * 2
                        for b in range(2):
                            e.dma_start(
                                out=bass.AP(woc[i % 3][:].tensor, b * 256,
                                            [[2048, 128], [512, 4],
                                             [1, 256]]),
                                in_=bass.AP(touta2[cq // 8],
                                            ((blk0 + b) * (N // 2)
                                             + (cq % 8) * 512) * 256,
                                            [[256, 128], [128 * 256, 4],
                                             [1, 256]]),
                            ).then_inc(sd, 16)

                    deps = [s_wo] + hid_g[g]
                    if i >= 1:
                        deps.append(omm[i - 1])
                    if ds > 0:
                        deps.append(ep[(g, ds - 1, gt_ - 1)])
                    elif cq == 0 and g >= 1:
                        deps.append(ep_last[g - 1])

                    @k.step("tensor", deps=deps + [s_bout])
                    def s_om(e, ds=ds, cq=cq, i=i, buf=buf):
                        last = None
                        for j in range(4):
                            c = cq * 4 + j
                            for t in range(gt_):
                                if c == 0:
                                    e.matmul(po[t][:], ones1b[0:1, :],
                                             bsto[ds % 2][0:1, :],
                                             start=True, stop=False)
                                    e.matmul(
                                        po[t][:], identso[:],
                                        x1g2[buf][:, t * D + ds * 512:
                                                  t * D + ds * 512 + 512],
                                        start=False, stop=False)
                                last = e.matmul(
                                    po[t][:],
                                    z2[buf][:, t * N + c * 128:
                                            t * N + c * 128 + 128],
                                    woc[i % 3][:, j * 512:(j + 1) * 512],
                                    start=False, stop=(c == NCH - 1))
                        return last
                    omm.append(s_om)

                for t in range(gt_):
                    evdeps = [omm[-1]]
                    if g >= 1:
                        evdeps.append(grp_res[(g - 1, t)])

                    @k.step("scalar", deps=evdeps)
                    def s_ev(e, ds=ds, t=t):
                        return e.activation(
                            out=tmp4[t][:, ds * 512:(ds + 1) * 512],
                            in_=po[t][:], func=AF.Identity, scale=soutb[:])
                    ep[(g, ds, t)] = s_ev
            ep_last[g] = ep[(g, 3, gt_ - 1)]
            omm_g_last[g] = omm[-1]

        def emit_ln2(g, t):
            buf = g % 2
            col = 6 + t
            tile_evs = [ep[(g, ds, t)] for ds in range(4)]

            @k.step("scalar", deps=tile_evs)
            def s_l2s(e, t=t, buf=buf, col=col):
                e.activation(out=jnk[:], in_=tmp4[t][:], func=AF.Copy,
                             accum_out=cS[:, col:col + 1])
                e.drain()
                return e.activation(out=jnk[:], in_=tmp4[t][:],
                                    func=AF.Square,
                                    accum_out=cQ[:, col:col + 1])

            @k.step("vector", deps=[s_l2s])
            def s_l2v(e, col=col):
                e.tensor_scalar(out=cMU[:, col:col + 1],
                                in0=cS[:, col:col + 1],
                                scalar1=1.0 / D, scalar2=None, op0=ALU.mult)
                e.drain()
                e.tensor_scalar(out=sv1[:], in0=cMU[:, col:col + 1],
                                scalar1=cMU[:, col:col + 1], scalar2=None,
                                op0=ALU.mult)
                e.tensor_scalar(out=cVR[:, col:col + 1],
                                in0=cQ[:, col:col + 1],
                                scalar1=1.0 / D, scalar2=None, op0=ALU.mult)
                e.drain()
                return e.tensor_sub(out=cVR[:, col:col + 1],
                                    in0=cVR[:, col:col + 1], in1=sv1[:])

            @k.step("scalar", deps=[s_l2v])
            def s_l2a(e, t=t, buf=buf, col=col):
                e.activation(out=ss1[:], in_=cVR[:, col:col + 1], func=AF.Ln,
                             bias=cbv(CB_LN))
                e.drain()
                e.activation(out=cRS[:, col:col + 1], in_=ss1[:], func=AF.Exp,
                             scale=cbv(CB_MH))
                e.drain()
                e.activation(out=ss2[:], in_=cMU[:, col:col + 1],
                             func=AF.Identity, scale=cRS[:, col:col + 1])
                e.drain()
                e.activation(out=ss2[:], in_=ss2[:], func=AF.Identity,
                             scale=cbv(CB_MO))
                e.drain()
                return e.activation(out=tmp4[t][:], in_=tmp4[t][:],
                                    func=AF.Identity, bias=ss2[:],
                                    scale=cRS[:, col:col + 1])
            l2a_last[g] = s_l2a

            @k.step("vector", deps=[s_l2a, s_gbb])
            def s_res(e, g=g, t=t, buf=buf):
                xt = x_sb[:, (g * gt_ + t) * D:(g * gt_ + t + 1) * D]
                e.tensor_mul(out=tmp4[t][:], in0=tmp4[t][:],
                             in1=gb2[:])
                e.drain()
                e.tensor_add(out=xt, in0=xt, in1=tmp4[t][:])
                e.drain()
                return e.tensor_add(out=xt, in0=xt, in1=bb2[:])
            grp_res[(g, t)] = s_res
            grp_done[g] = s_res

        for g in range(ng_ + 2):
            if g < ng_:
                emit_ln1(g)
                emit_x1tr(g)
            if 1 <= g <= ng_:
                emit_th(g - 1, 0)
            if g < ng_:
                emit_z(g)
                if g == 0:
                    emit_stats0()
            if 1 <= g <= ng_:
                emit_th(g - 1, 1)
            if 2 <= g <= ng_ + 1:
                emit_ln2(g - 2, 0)
                emit_ln2(g - 2, 1)
            if 1 <= g <= ng_:
                emit_outs(g - 1)

        k.barrier(engines=("sync", "scalar", "vector", "tensor"))
        fs.close()
        return [grp_done[ng_ - 1]]

    ffn_phase("a")

    def attn_phase():
        s_ = contextlib.ExitStack()
        en = s_.enter_context
        rg4 = [[0, 1, 2, 3], [4, 5, 6, 7]]

        pt1 = en(nc.psum_tensor("pt1", [128, 512], F16))
        pt2 = en(nc.psum_tensor("pt2", [128, 512], F16))
        pa = en(nc.psum_tensor("pa", [128, 512], F32))
        pb = en(nc.psum_tensor("pb", [128, 512], F32))
        psc = en(nc.psum_tensor("psc", [128, 512], F32))
        psc2 = en(nc.psum_tensor("psc2", [128, 512], F32))
        ppt = en(nc.psum_tensor("ppt", [128, 512], F16))
        pov = en(nc.psum_tensor("pov", [128, 512], F32))

        AR = 80896
        ar = en(nc.sbuf_tensor("arena", [128, AR], F16))
        arf = ar[:].bitcast(F32)

        def v16(o, n):
            return ar[:, o:o + n]

        def v32(o, n):
            return arf[:, o // 2: o // 2 + n]

        O_x2t = 0
        O_rtmp = 16384
        O_rtm2 = 20480
        O_absw = 22528
        O_aqT = 30720
        O_cqT = 34816
        O_R1 = 38912
        O_R2 = 55296
        O_R3 = 71680
        x2t = v16(O_x2t, 16384)
        qrT = v16(O_x2t, 16384)
        rtmp = v32(O_rtmp, 2048)
        rtm2 = v16(O_rtm2, 2048)
        absw = v16(O_absw, 8192)
        aqT = v16(O_aqT, 4096)
        cqT = v16(O_cqT, 4096)
        g1b = v16(O_R3, 512)
        g2b = v16(O_R3 + 512, 512)
        g3b = v16(O_R3 + 1024, 2048)
        aqb = v16(O_R3 + 3072, 4096)
        ptile = v16(O_R3 + 3072, 512)
        pTt = v16(O_R3 + 3584, 512)
        fCA = v32(O_R3 + 4096, 8)
        fNM = v32(O_R3 + 4128, 8)
        fCO = v32(O_R3 + 4160, 8)
        fRS = v32(O_R3 + 4192, 8)
        mrun = v32(O_R3 + 4224, 8)
        lrun = v32(O_R3 + 4256, 8)

        @k.step("gpsimd", ndma=3)
        def s_gl(e, sd):
            e.dma_start(out=g1b[0:1, :], in_=g1).then_inc(sd, 16)
            e.dma_start(out=g2b[0:1, :], in_=g2).then_inc(sd, 16)
            e.dma_start(out=g3b[0:1, :], in_=g3).then_inc(sd, 16)

        gbc_ev = []
        for buf, nsl in ((g1b, 1), (g2b, 1), (g3b, 4)):
            for sl in range(nsl):
                @k.step("tensor", deps=[s_gl] + ([gbc_ev[-1]] if gbc_ev else []))
                def s_gm(e, buf=buf, sl=sl):
                    return e.matmul(pa[:], ones1b[0:1, :],
                                    buf[0:1, sl * 512:(sl + 1) * 512],
                                    start=True, stop=True)

                @k.step("scalar", deps=[s_gm])
                def s_ge(e, buf=buf, sl=sl):
                    return e.copy(out=buf[:, sl * 512:(sl + 1) * 512],
                                  in_=pa[:])
                gbc_ev.append(s_ge)
        s_gbc = gbc_ev[-1]

        woT = v16(O_R1, 32768)
        wuvb = v16(O_x2t, 8192)

        @k.step("gpsimd", ndma=2)
        def s_awl(e, sd):
            e.dma_start(
                out=woT.rearrange("p (c f) -> p c f", c=DCH),
                in_=bass.AP(wo.tensor, 0, [[D, 128], [128 * D, DCH], [1, D]]),
            ).then_inc(sd, 16)
            e.dma_start(
                out=wuvb.rearrange("p (c f) -> p c f", c=DCH),
                in_=bass.AP(wuv.tensor, 0, [[C, 128], [128 * C, DCH], [1, C]]),
            ).then_inc(sd, 16)

        wo_tr = []
        for idx, (rc, cc) in enumerate(
                [(r, c) for r in range(DCH) for c in range(r, DCH)]):
            deps = [s_awl] if idx < 2 else [wo_tr[idx - 2]]
            pss = pt1 if idx % 2 == 0 else pt2

            @k.step("tensor", deps=deps)
            def s_wt(e, rc=rc, cc=cc, pss=pss):
                i1 = e.transpose(pss[:, 0:128],
                                 woT[:, rc * D + cc * 128: rc * D + cc * 128 + 128],
                                 identb[:])
                if cc != rc:
                    i1 = e.transpose(pss[:, 128:256],
                                     woT[:, cc * D + rc * 128:
                                         cc * D + rc * 128 + 128], identb[:])
                return i1

            @k.step("scalar", deps=[s_wt])
            def s_we(e, rc=rc, cc=cc, pss=pss):
                i1 = e.copy(out=woT[:, cc * D + rc * 128: cc * D + rc * 128 + 128],
                            in_=pss[:, 0:128])
                if cc != rc:
                    i1 = e.copy(out=woT[:, rc * D + cc * 128:
                                        rc * D + cc * 128 + 128],
                                in_=pss[:, 128:256])
                return i1
            wo_tr.append(s_we)
        wo_done = wo_tr[-1]

        absw_ev = []
        for kk2, (cs, es) in enumerate(
                [(a, b) for a in range(4) for b in range(4)]):
            ps = pa if kk2 % 2 == 0 else pb
            deps = [wo_done] if kk2 < 2 else [absw_ev[kk2 - 2]]

            @k.step("tensor", deps=deps)
            def s_am(e, cs=cs, es=es, ps=ps):
                last = None
                for ec in range(DCH):
                    last = e.matmul(
                        ps[:],
                        wuvb[:, ec * C + cs * 128: ec * C + cs * 128 + 128],
                        woT[:, ec * D + es * 512: ec * D + es * 512 + 512],
                        start=(ec == 0), stop=(ec == DCH - 1))
                return last

            @k.step("scalar", deps=[s_am])
            def s_ae(e, cs=cs, es=es, ps=ps):
                return e.copy(out=absw[:, cs * D + es * 512:
                                       cs * D + es * 512 + 512], in_=ps[:])
            absw_ev.append(s_ae)
        absw_done = absw_ev[-1]

        wqb = v16(O_R1, 8192)
        wkb = v16(O_R1 + 8192, 8192)
        Asb = v16(O_R2, 2048)

        @k.step("gpsimd", deps=[absw_done], ndma=2)
        def s_bl(e, sd):
            e.dma_start(
                out=wqb.rearrange("p (c f) -> p c f", c=DCH),
                in_=bass.AP(wuq.tensor, 0, [[C, 128], [128 * C, DCH], [1, C]]),
            ).then_inc(sd, 16)
            e.dma_start(
                out=wkb.rearrange("p (c f) -> p c f", c=DCH),
                in_=bass.AP(wuk.tensor, 0, [[C, 128], [128 * C, DCH], [1, C]]),
            ).then_inc(sd, 16)

        A_ev = []
        for cs in range(4):
            ps = pa if cs % 2 == 0 else pb
            deps = [s_bl] if cs < 2 else [A_ev[cs - 2]]

            @k.step("tensor", deps=deps)
            def s_Am(e, cs=cs, ps=ps):
                last = None
                for ec in range(DCH):
                    last = e.matmul(
                        ps[:],
                        wqb[:, ec * C + cs * 128: ec * C + cs * 128 + 128],
                        wkb[:, ec * C:(ec + 1) * C],
                        start=(ec == 0), stop=(ec == DCH - 1))
                return last

            @k.step("scalar", deps=[s_Am])
            def s_Ae(e, cs=cs, ps=ps):
                return e.copy(out=Asb[:, cs * C:(cs + 1) * C], in_=ps[:])
            A_ev.append(s_Ae)

        wdqn = v16(O_R1, 8192)
        Bsb = v16(O_R2 + 8192, 8192)

        @k.step("gpsimd", deps=[A_ev[-1]], ndma=1)
        def s_dql(e, sd):
            e.dma_start(
                out=wdqn.rearrange("p (c f) -> p c f", c=4),
                in_=bass.AP(wdq.tensor, 0, [[D, 128], [128 * D, 4], [1, D]]),
            ).then_inc(sd, 16)

        B_ev = []
        for ds in range(DCH):
            ps = pa if ds % 2 == 0 else pb
            deps = [s_dql] if ds < 2 else [B_ev[ds - 2]]

            @k.step("tensor", deps=deps)
            def s_Bm(e, ds=ds, ps=ps):
                last = None
                for cc in range(4):
                    last = e.matmul(
                        ps[:],
                        wdqn[:, cc * D + ds * 128: cc * D + ds * 128 + 128],
                        Asb[:, cc * C:(cc + 1) * C],
                        start=(cc == 0), stop=(cc == 3))
                return last

            @k.step("scalar", deps=[s_Bm])
            def s_Be(e, ds=ds, ps=ps):
                return e.activation(out=Bsb[:, ds * C:(ds + 1) * C], in_=ps[:],
                                    func=AF.Identity, scale=cbv(CB_I12))
            B_ev.append(s_Be)
        B_done = B_ev[-1]

        wdqT = v16(O_R1 + 8192, 8192)
        wnat = v16(O_x2t, 8192)
        wdkvT = v16(O_R2, 8192)

        @k.step("gpsimd", deps=[B_done], ndma=1)
        def s_kvl(e, sd):
            e.dma_start(
                out=wnat.rearrange("p (c f) -> p c f", c=4),
                in_=bass.AP(wdkv.tensor, 0, [[D, 128], [128 * D, 4], [1, D]]),
            ).then_inc(sd, 16)

        jobs = []
        for cc in range(4):
            for dc in range(DCH):
                jobs.append((wdqn, cc, dc, D, wdqT, C))
        for cc in range(4):
            for dc in range(DCH):
                jobs.append((wnat, cc, dc, D, wdkvT, C))
        sw_ev = []
        for jg in range(len(jobs) // 4):
            ps = pt1 if jg % 2 == 0 else pt2
            deps = [s_kvl, B_done] if jg < 2 else [sw_ev[jg - 2]]

            @k.step("tensor", deps=deps)
            def s_st(e, jg=jg, ps=ps):
                last = None
                for j in range(4):
                    srcb, rc, cc2, srcw, dst, dn = jobs[jg * 4 + j]
                    last = e.transpose(
                        ps[:, ts_(j)],
                        srcb[:, rc * srcw + cc2 * 128: rc * srcw + cc2 * 128 + 128],
                        identb[:])
                return last

            @k.step("scalar", deps=[s_st])
            def s_se(e, jg=jg, ps=ps):
                last = None
                for j in range(4):
                    srcb, rc, cc2, srcw, dst, dn = jobs[jg * 4 + j]
                    last = e.copy(
                        out=dst[:, cc2 * dn + rc * 128: cc2 * dn + rc * 128 + 128],
                        in_=ps[:, ts_(j)])
                return last
            sw_ev.append(s_se)
        sw_done = sw_ev[-1]

        x2t_ev = []
        ln2_apply = []
        for t in range(TILES):
            @k.step("scalar", deps=([ln2_apply[-1]] if ln2_apply else []))
            def s_st2(e, t=t):
                xt = x_sb[:, t * D:(t + 1) * D]
                e.activation(out=rtm2[:], in_=xt, func=AF.Copy,
                             accum_out=cS[:, t:t + 1])
                e.drain()
                return e.activation(out=rtm2[:], in_=xt, func=AF.Square,
                                    accum_out=cQ[:, t:t + 1])

            @k.step("vector", deps=[s_st2])
            def s_vr2(e, t=t):
                e.tensor_scalar(out=cMU[:, t:t + 1], in0=cS[:, t:t + 1],
                                scalar1=1.0 / D, scalar2=None, op0=ALU.mult)
                e.drain()
                e.tensor_scalar(out=sv1[:], in0=cMU[:, t:t + 1],
                                scalar1=cMU[:, t:t + 1], scalar2=None,
                                op0=ALU.mult)
                e.tensor_scalar(out=cVR[:, t:t + 1], in0=cQ[:, t:t + 1],
                                scalar1=1.0 / D, scalar2=None, op0=ALU.mult)
                e.drain()
                return e.tensor_sub(out=cVR[:, t:t + 1], in0=cVR[:, t:t + 1],
                                    in1=sv1[:])

            @k.step("scalar", deps=[s_vr2])
            def s_ap2(e, t=t):
                xt = x_sb[:, t * D:(t + 1) * D]
                e.activation(out=ss1[:], in_=cVR[:, t:t + 1], func=AF.Ln,
                             bias=cbv(CB_LN))
                e.drain()
                e.activation(out=cRS[:, t:t + 1], in_=ss1[:], func=AF.Exp,
                             scale=cbv(CB_MH))
                e.drain()
                e.activation(out=ss2[:], in_=cMU[:, t:t + 1], func=AF.Identity,
                             scale=cRS[:, t:t + 1])
                e.drain()
                e.activation(out=ss2[:], in_=ss2[:], func=AF.Identity, scale=cbv(CB_MO))
                e.drain()
                return e.activation(out=rtm2[:], in_=xt, func=AF.Identity,
                                    bias=ss2[:], scale=cRS[:, t:t + 1])
            ln2_apply.append(s_ap2)

            for q4 in range(4):
                gi = t * 4 + q4
                ps = pt1 if gi % 2 == 0 else pt2
                deps = [s_ap2, sw_done]
                if gi >= 2:
                    deps.append(x2t_ev[gi - 2])

                @k.step("tensor", deps=deps)
                def s_xt(e, t=t, q4=q4, ps=ps):
                    last = None
                    for j in range(4):
                        dc = q4 * 4 + j
                        last = e.transpose(ps[:, ts_(j)],
                                           rtm2[:, dc * 128:(dc + 1) * 128],
                                           identb[:])
                    return last

                @k.step("scalar", deps=[s_xt])
                def s_xe(e, t=t, q4=q4, ps=ps):
                    last = None
                    for j in range(4):
                        dc = q4 * 4 + j
                        last = e.copy(
                            out=x2t[:, dc * TOK + t * 128: dc * TOK + t * 128 + 128],
                            in_=ps[:, ts_(j)])
                    return last
                x2t_ev.append(s_xe)
        x2t_done = x2t_ev[-1]

        cqb = v16(O_R1, 4096)
        ckvb = v16(O_R1 + 4096, 4096)
        prj_ev = []
        for t in range(TILES):
            for which in range(3):
                kk2 = t * 3 + which
                ps = (pa, pb, pa)[which]
                wsrc = (wdqT, wdkvT, Bsb)[which]
                deps = [x2t_done, sw_done, B_done]
                if kk2 >= 1:
                    deps.append(prj_ev[kk2 - 1])

                @k.step("tensor", deps=deps)
                def s_pm(e, t=t, ps=ps, wsrc=wsrc):
                    last = None
                    for dc in range(DCH):
                        last = e.matmul(
                            ps[:],
                            x2t[:, dc * TOK + t * 128: dc * TOK + t * 128 + 128],
                            wsrc[:, dc * C:(dc + 1) * C],
                            start=(dc == 0), stop=(dc == DCH - 1))
                    return last

                rcol = cRS if which == 0 else cTH
                if which < 2:
                    @k.step("scalar", deps=[s_pm])
                    def s_ps(e, t=t, ps=ps, rcol=rcol):
                        e.activation(out=rtm2[:, 0:C], in_=ps[:],
                                     func=AF.Square, accum_out=ss1[:])
                        e.drain()
                        e.activation(out=ss2[:], in_=ss1[:], func=AF.Ln,
                                     bias=cbv(CB_RMS), scale=cbv(CB_IC))
                        e.drain()
                        return e.activation(out=rcol[:, t:t + 1], in_=ss2[:],
                                            func=AF.Exp, scale=cbv(CB_MH))

                    @k.step("vector", deps=[s_ps])
                    def s_pv(e, t=t, ps=ps, which=which, rcol=rcol):
                        dst = cqb if which == 0 else ckvb
                        gby = g1b if which == 0 else g2b
                        return e.scalar_tensor_tensor(
                            out=dst[:, t * C:(t + 1) * C], in0=ps[:],
                            scalar=rcol[:, t:t + 1], in1=gby[:], op0=ALU.mult,
                            op1=ALU.mult)
                    prj_ev.append(s_pv)
                else:
                    @k.step("vector", deps=[s_pm])
                    def s_pv2(e, t=t, ps=ps):
                        return e.tensor_copy(out=aqb[:, t * C:(t + 1) * C],
                                             in_=ps[:])
                    prj_ev.append(s_pv2)
        prj_done = prj_ev[-1]

        ckvT = v16(O_R2, 4096)
        jobs2 = []
        for t in range(TILES):
            for cc in range(4):
                jobs2.append((cqb, t, cc, cqT))
        for t in range(TILES):
            for cc in range(4):
                jobs2.append((aqb, t, cc, aqT))
        for t in range(TILES):
            for cc in range(4):
                jobs2.append((ckvb, t, cc, ckvT))
        j2_ev = []
        for jg in range(len(jobs2) // 4):
            ps = pt1 if jg % 2 == 0 else pt2
            deps = [prj_done]
            if jg >= 2:
                deps.append(j2_ev[jg - 2])

            @k.step("tensor", deps=deps)
            def s_j2(e, jg=jg, ps=ps):
                last = None
                for j in range(4):
                    srcb, t, cc, dst = jobs2[jg * 4 + j]
                    last = e.transpose(
                        ps[:, ts_(j)],
                        srcb[:, t * C + cc * 128: t * C + cc * 128 + 128],
                        identb[:])
                return last

            @k.step("scalar", deps=[s_j2])
            def s_j2e(e, jg=jg, ps=ps):
                last = None
                for j in range(4):
                    srcb, t, cc, dst = jobs2[jg * 4 + j]
                    last = e.copy(
                        out=dst[:, cc * TOK + t * 128: cc * TOK + t * 128 + 128],
                        in_=ps[:, ts_(j)])
                return last
            j2_ev.append(s_j2e)
        j2_done = j2_ev[-1]

        @k.step("sync", deps=[prj_done], ndma=1)
        def s_ckvst(e, sd):
            e.dma_start(
                out=bass.AP(ckvp, 0, [[C, 128], [128 * C, TILES], [1, C]]),
                in_=ckvb[:].rearrange("p (t f) -> p t f", t=TILES),
            ).then_inc(sd, 16)

        @k.step("sync", deps=[j2_done], ndma=1)
        def s_ckvtst(e, sd):
            e.dma_start(
                out=bass.AP(ckvtp, 0, [[TOK, 128], [128 * TOK, 4], [1, TOK]]),
                in_=ckvT[:].rearrange("p (c f) -> p c f", c=4),
            ).then_inc(sd, 16)

        @k.step("gpsimd", deps=[s_ckvst], cc=True)
        def s_agc(e):
            return e.collective_compute(
                "AllGather", ALU.bypass, replica_groups=rg4,
                ins=[ckvp.ap().opt()], outs=[ckva.ap().opt()])

        @k.step("gpsimd", deps=[s_ckvtst], cc=True)
        def s_agct(e):
            return e.collective_compute(
                "AllGather", ALU.bypass, replica_groups=rg4,
                ins=[ckvtp.ap().opt()], outs=[ckvta.ap().opt()])

        wkrT = v16(O_R1, 32768)
        tcos2 = v32(O_R3 + 7168, 512)
        tsin2 = v32(O_R3 + 8192, 512)
        qtile = v16(O_R3 + 3072, 2048)
        rstg = v16(O_R3 + 5120, 2048)
        krTt = v16(O_rtm2, 2048)

        @k.step("sync", deps=[s_ckvtst], ndma=2)
        def s_tb2(e, sd):
            e.dma_start(out=tcos2.rearrange("p (t f) -> p t f", t=TILES),
                        in_=bass.AP(cosk.tensor, 0,
                                    [[64, 128], [128 * 64, TILES], [1, 64]])
                        ).then_inc(sd, 16)
            e.dma_start(out=tsin2.rearrange("p (t f) -> p t f", t=TILES),
                        in_=bass.AP(sink.tensor, 0,
                                    [[64, 128], [128 * 64, TILES], [1, 64]])
                        ).then_inc(sd, 16)

        @k.step("gpsimd", deps=[j2_done, s_ckvst], ndma=1)
        def s_krl(e, sd):
            e.dma_start(
                out=wkrT.rearrange("p (c f) -> p c f", c=DCH),
                in_=bass.AP(wkr.tensor, 0, [[D, 128], [128 * D, DCH], [1, D]]),
            ).then_inc(sd, 16)

        kr_tr = []
        for idx, (rc, cc) in enumerate(
                [(r, c) for r in range(DCH) for c in range(r, DCH)]):
            deps = [s_krl, j2_done] if idx < 2 else [kr_tr[idx - 2]]
            pss = pt1 if idx % 2 == 0 else pt2

            @k.step("tensor", deps=deps)
            def s_kt(e, rc=rc, cc=cc, pss=pss):
                i1 = e.transpose(pss[:, 0:128],
                                 wkrT[:, rc * D + cc * 128:
                                      rc * D + cc * 128 + 128], identb[:])
                if cc != rc:
                    i1 = e.transpose(pss[:, 128:256],
                                     wkrT[:, cc * D + rc * 128:
                                          cc * D + rc * 128 + 128], identb[:])
                return i1

            @k.step("scalar", deps=[s_kt])
            def s_ke(e, rc=rc, cc=cc, pss=pss):
                i1 = e.copy(out=wkrT[:, cc * D + rc * 128:
                                     cc * D + rc * 128 + 128], in_=pss[:, 0:128])
                if cc != rc:
                    i1 = e.copy(out=wkrT[:, rc * D + cc * 128:
                                         rc * D + cc * 128 + 128],
                                in_=pss[:, 128:256])
                return i1
            kr_tr.append(s_ke)
        wkr_done = kr_tr[-1]

        def rope_emit(e, t, ctab, stab):
            def sv3(apo, extra, dims):
                return bass.AP(apo.tensor, apo.offset + extra,
                               [[apo.ap[0][0], 128]] + dims)
            ev = sv3(rtmp, 0, [[128, DCH], [2, 64]])
            od = sv3(rtmp, 1, [[128, DCH], [2, 64]])
            cv = sv3(ctab, t * 64, [[0, DCH], [1, 64]])
            sv = sv3(stab, t * 64, [[0, DCH], [1, 64]])
            oev = sv3(qtile, 0, [[128, DCH], [2, 64]])
            ood = sv3(qtile, 1, [[128, DCH], [2, 64]])
            e2 = sv3(rstg, 0, [[128, DCH], [2, 64]])
            o2 = sv3(rstg, 1, [[128, DCH], [2, 64]])
            e.tensor_tensor(out=oev, in0=ev, in1=cv, op=ALU.mult)
            e.tensor_tensor(out=ood, in0=od, in1=cv, op=ALU.mult)
            e.tensor_tensor(out=e2, in0=od, in1=sv, op=ALU.mult)
            e.tensor_tensor(out=o2, in0=ev, in1=sv, op=ALU.mult)
            e.drain()
            e.tensor_tensor(out=oev, in0=oev, in1=e2, op=ALU.subtract)
            return e.tensor_tensor(out=ood, in0=ood, in1=o2, op=ALU.add)

        def proj_rope_store(stat_src, w_src, n_k, ctab, stab, dstT, dst_dram,
                            mm_deps, per_tile_store, tbl_dep):
            ev_s = []
            rp_s = []
            tr_s = []
            st_s = []
            for t in range(TILES):
                for es in range(4):
                    kk2 = t * 4 + es
                    ps = pa if kk2 % 2 == 0 else pb
                    deps = list(mm_deps)
                    if kk2 >= 2:
                        deps.append(ev_s[kk2 - 2])

                    @k.step("tensor", deps=deps)
                    def s_m(e, t=t, es=es, ps=ps, stat_src=stat_src,
                            w_src=w_src, n_k=n_k):
                        last = None
                        for ic in range(n_k):
                            last = e.matmul(
                                ps[:],
                                stat_src[:, ic * TOK + t * 128:
                                         ic * TOK + t * 128 + 128],
                                w_src[:, ic * D + es * 512:
                                      ic * D + es * 512 + 512],
                                start=(ic == 0), stop=(ic == n_k - 1))
                        return last

                    rp_dep = [rp_s[t - 1]] if (es == 0 and t > 0) else []

                    @k.step("scalar", deps=[s_m] + rp_dep)
                    def s_e(e, es=es, ps=ps):
                        return e.copy(out=rtmp[:, es * 512:(es + 1) * 512],
                                      in_=ps[:])
                    ev_s.append(s_e)

                dep_r = [ev_s[t * 4 + 3], tbl_dep]
                if t > 0:
                    dep_r.append(tr_s[-1])

                @k.step("vector", deps=dep_r)
                def s_r(e, t=t, ctab=ctab, stab=stab):
                    return rope_emit(e, t, ctab, stab)
                rp_s.append(s_r)

                for q4 in range(4):
                    gi = t * 4 + q4
                    ps = pt1 if gi % 2 == 0 else pt2
                    deps = [s_r]
                    if gi >= 2:
                        deps.append(tr_s[gi - 2])

                    @k.step("tensor", deps=deps)
                    def s_t(e, q4=q4, ps=ps):
                        last = None
                        for j in range(4):
                            dc = q4 * 4 + j
                            last = e.transpose(
                                ps[:, ts_(j)],
                                qtile[:, dc * 128:(dc + 1) * 128], identb[:])
                        return last

                    dep_te = [s_t]
                    if per_tile_store and q4 == 0 and t > 0:
                        dep_te.append(st_s[-1])

                    @k.step("scalar", deps=dep_te)
                    def s_te2(e, t=t, q4=q4, ps=ps, dstT=dstT):
                        last = None
                        for j in range(4):
                            dc = q4 * 4 + j
                            if per_tile_store:
                                o0 = (q4 * 4 + j) * 128
                                last = e.copy(out=krTt[:, o0:o0 + 128],
                                              in_=ps[:, ts_(j)])
                            else:
                                last = e.copy(
                                    out=dstT[:, dc * TOK + t * 128:
                                             dc * TOK + t * 128 + 128],
                                    in_=ps[:, ts_(j)])
                        return last
                    tr_s.append(s_te2)

                if per_tile_store:
                    @k.step("sync", deps=[tr_s[-1]], ndma=1)
                    def s_sto(e, sd, t=t, dst_dram=dst_dram):
                        e.dma_start(
                            out=bass.AP(dst_dram, t * 128,
                                        [[TOK, 128], [128 * TOK, DCH], [1, 128]]),
                            in_=krTt[:].rearrange("p (c f) -> p c f", c=DCH),
                        ).then_inc(sd, 16)
                    st_s.append(s_sto)
            return (st_s[-1] if per_tile_store else tr_s[-1])

        kr_done = proj_rope_store(x2t, wkrT, DCH, tcos2, tsin2, None, krtp,
                                  [wkr_done], True, s_tb2)

        @k.step("gpsimd", deps=[kr_done], cc=True)
        def s_agk(e):
            return e.collective_compute(
                "AllGather", ALU.bypass, replica_groups=rg4,
                ins=[krtp.ap().opt()], outs=[krta.ap().opt()])

        wqrn = v16(O_R1, 8192)
        wqrT = v16(O_R1 + 8192, 8192)
        tcos = v32(O_R2, 512)
        tsin = v32(O_R2 + 1024, 512)

        @k.step("gpsimd", deps=[kr_done], ndma=1)
        def s_qrl(e, sd):
            e.dma_start(
                out=wqrn.rearrange("p (c f) -> p c f", c=DCH),
                in_=bass.AP(wqr.tensor, 0, [[C, 128], [128 * C, DCH], [1, C]]),
            ).then_inc(sd, 16)

        @k.step("sync", deps=[kr_done], ndma=2)
        def s_tb1(e, sd):
            e.dma_start(out=tcos.rearrange("p (t f) -> p t f", t=TILES),
                        in_=bass.AP(cosq.tensor, 0,
                                    [[64, 128], [128 * 64, TILES], [1, 64]])
                        ).then_inc(sd, 16)
            e.dma_start(out=tsin.rearrange("p (t f) -> p t f", t=TILES),
                        in_=bass.AP(sinq.tensor, 0,
                                    [[64, 128], [128 * 64, TILES], [1, 64]])
                        ).then_inc(sd, 16)

        qr_tr = []
        jobs3 = [(dc, cc) for dc in range(DCH) for cc in range(4)]
        for jg in range(len(jobs3) // 4):
            ps = pt1 if jg % 2 == 0 else pt2
            deps = [s_qrl, kr_done] if jg < 2 else [qr_tr[jg - 2]]

            @k.step("tensor", deps=deps)
            def s_q3(e, jg=jg, ps=ps):
                last = None
                for j in range(4):
                    dc, cc2 = jobs3[jg * 4 + j]
                    last = e.transpose(
                        ps[:, ts_(j)],
                        wqrn[:, dc * C + cc2 * 128: dc * C + cc2 * 128 + 128],
                        identb[:])
                return last

            @k.step("scalar", deps=[s_q3])
            def s_q3e(e, jg=jg, ps=ps):
                last = None
                for j in range(4):
                    dc, cc2 = jobs3[jg * 4 + j]
                    last = e.copy(
                        out=wqrT[:, cc2 * D + dc * 128: cc2 * D + dc * 128 + 128],
                        in_=ps[:, ts_(j)])
                return last
            qr_tr.append(s_q3e)
        wqr_done = qr_tr[-1]

        qr_done = proj_rope_store(cqT, wqrT, 4, tcos, tsin, qrT, None,
                                  [wqr_done], False, s_tb1)

        kb_kr = [v16(O_R1, 8192), v16(O_R1 + 8192, 8192)]
        kb_ct = [v16(O_R2, 2048), v16(O_R2 + 2048, 2048)]
        kb_cv = [v16(O_R2 + 4096, 2048), v16(O_R2 + 6144, 2048)]
        ost = [v32(O_R2 + 8192 + i * 1024, 512) for i in range(TILES)]

        @k.step("vector", deps=[qr_done, s_agk])
        def s_finit(e):
            e.memset(mrun[:], -1e30)
            e.memset(lrun[:], 0.0)
            last = None
            for t in range(TILES):
                last = e.memset(ost[t][:], 0.0)
            return last

        kdma = []
        smx_v2 = []
        smx_s1 = []
        pt_ev = []
        ou_mm = []
        pend = []

        def emit_pt_ov(kk2, kt, t, ps, s_s1h):
            deps = [s_s1h]
            if kk2 >= 1:
                deps.append(pt_ev[kk2 - 1])

            @k.step("tensor", deps=deps)
            def s_pt(e):
                last = None
                for j in range(4):
                    last = e.transpose(ppt[:, ts_(j)],
                                       ptile[:, j * 128:(j + 1) * 128],
                                       identb[:])
                return last

            dep_e = [s_pt]
            if kk2 >= 1:
                dep_e.append(ou_mm[kk2 - 1])

            @k.step("scalar", deps=dep_e)
            def s_pte(e):
                return e.copy(out=pTt[:], in_=ppt[:])
            pt_ev.append(s_pte)

            dep_o = [s_pte]
            if kk2 >= 1:
                dep_o.append(smx_v2[kk2 - 1])

            @k.step("tensor", deps=dep_o)
            def s_ov(e, kt=kt):
                last = None
                for cc in range(4):
                    last = e.matmul(
                        pov[:],
                        pTt[:, cc * 128:(cc + 1) * 128],
                        kb_cv[kt % 2][:, cc * C:(cc + 1) * C],
                        start=(cc == 0), stop=(cc == 3))
                return last
            ou_mm.append(s_ov)

            @k.step("vector", deps=[s_ov, s_s1h])
            def s_v2(e, t=t):
                e.scalar_tensor_tensor(out=lrun[:, t:t + 1],
                                       in0=lrun[:, t:t + 1],
                                       scalar=fCO[:, t:t + 1],
                                       in1=fRS[:, t:t + 1], op0=ALU.mult,
                                       op1=ALU.add)
                return e.scalar_tensor_tensor(out=ost[t][:], in0=ost[t][:],
                                              scalar=fCO[:, t:t + 1],
                                              in1=pov[:], op0=ALU.mult,
                                              op1=ALU.add)
            smx_v2.append(s_v2)
            return s_pt

        last_pt = [None]
        for kt in range(KTS):
            deps = [s_agk, s_agc, s_agct, s_finit, qr_done]
            if kt >= 2:
                deps.append(smx_v2[(kt - 2) * TILES + TILES - 1])

            @k.step("sync", deps=deps, ndma=3)
            def s_kd(e, sd, kt=kt):
                b = kt // 2
                c0 = (kt % 2) * 512
                e.dma_start(
                    out=kb_kr[kt % 2].rearrange("p (c f) -> p c f", c=DCH),
                    in_=bass.AP(krta, (b * D) * TOK + c0,
                                [[TOK, 128], [128 * TOK, DCH], [1, 512]]),
                ).then_inc(sd, 16)
                e.dma_start(
                    out=kb_ct[kt % 2].rearrange("p (c f) -> p c f", c=4),
                    in_=bass.AP(ckvta, (b * C) * TOK + c0,
                                [[TOK, 128], [128 * TOK, 4], [1, 512]]),
                ).then_inc(sd, 16)
                e.dma_start(
                    out=kb_cv[kt % 2].rearrange("p (c f) -> p c f", c=4),
                    in_=bass.AP(ckva, kt * 512 * C,
                                [[C, 128], [128 * C, 4], [1, C]]),
                ).then_inc(sd, 16)
            kdma.append(s_kd)

            for t in range(TILES):
                kk2 = kt * TILES + t
                ps = psc if kk2 % 2 == 0 else psc2
                deps = [s_kd]
                if kk2 >= 2:
                    deps.append(smx_s1[kk2 - 2])

                @k.step("tensor", deps=deps)
                def s_sc2(e, kt=kt, t=t, ps=ps):
                    last = None
                    for cc in range(4):
                        last = e.matmul(
                            ps[:],
                            aqT[:, cc * TOK + t * 128: cc * TOK + t * 128 + 128],
                            kb_ct[kt % 2][:, cc * 512:(cc + 1) * 512],
                            start=(cc == 0), stop=False)
                    for ec in range(DCH):
                        last = e.matmul(
                            ps[:],
                            qrT[:, ec * TOK + t * 128: ec * TOK + t * 128 + 128],
                            kb_kr[kt % 2][:, ec * 512:(ec + 1) * 512],
                            start=False, stop=(ec == DCH - 1))
                    return last

                @k.step("vector", deps=[s_sc2])
                def s_v1(e, t=t, ps=ps):
                    e.reduce_max(out=sv1[:], in_=ps[:], axis=AX.X)
                    e.drain()
                    e.tensor_max(out=sv2[:], in0=sv1[:], in1=mrun[:, t:t + 1])
                    e.drain()
                    e.tensor_sub(out=fCA[:, t:t + 1], in0=mrun[:, t:t + 1],
                                 in1=sv2[:])
                    e.drain()
                    e.tensor_copy(out=mrun[:, t:t + 1], in_=sv2[:])
                    return e.tensor_scalar(out=fNM[:, t:t + 1], in0=sv2[:],
                                           scalar1=-1.0, scalar2=None,
                                           op0=ALU.mult)

                if pend:
                    last_pt[0] = emit_pt_ov(*pend.pop())
                dep_s1 = [s_v1]
                if last_pt[0] is not None:
                    dep_s1.append(last_pt[0])

                @k.step("scalar", deps=dep_s1)
                def s_s1(e, t=t, ps=ps):
                    e.activation(out=fCO[:, t:t + 1], in_=fCA[:, t:t + 1],
                                 func=AF.Exp)
                    return e.activation(out=ptile[:], in_=ps[:], func=AF.Exp,
                                        bias=fNM[:, t:t + 1],
                                        accum_out=fRS[:, t:t + 1])
                smx_s1.append(s_s1)
                pend.append((kk2, kt, t, ps, s_s1))

        emit_pt_ov(*pend.pop())

        u_done = []
        um_ev = []
        for t in range(TILES):
            @k.step("vector", deps=[smx_v2[-1]] + ([u_done[-1]] if u_done else []))
            def s_uo(e, t=t):
                e.reciprocal(out=sv1[:], in_=lrun[:, t:t + 1])
                e.drain()
                return e.tensor_scalar(out=ptile[:, 0:C], in0=ost[t][:],
                                       scalar1=sv1[:], scalar2=None,
                                       op0=ALU.mult)

            @k.step("tensor", deps=[s_uo] + ([um_ev[-1]] if um_ev else []))
            def s_ut(e):
                last = None
                for j in range(4):
                    last = e.transpose(ppt[:, ts_(j)],
                                       ptile[:, j * 128:(j + 1) * 128],
                                       identb[:])
                return last

            @k.step("scalar", deps=[s_ut])
            def s_ue(e):
                return e.copy(out=pTt[:], in_=ppt[:])

            um_s = []
            for es in range(4):
                ps = psc if es % 2 == 0 else psc2
                deps = [s_ue]
                if es >= 2:
                    deps.append(um_ev[-2 + es - 2 + 1] if False else um_ev[-1])

                @k.step("tensor", deps=deps)
                def s_um(e, es=es, ps=ps):
                    last = None
                    for cc in range(4):
                        last = e.matmul(
                            ps[:],
                            pTt[:, cc * 128:(cc + 1) * 128],
                            absw[:, cc * D + es * 512: cc * D + es * 512 + 512],
                            start=(cc == 0), stop=(cc == 3))
                    return last
                um_s.append(s_um)

                @k.step("scalar", deps=[s_um])
                def s_ume(e, es=es, ps=ps):
                    return e.copy(out=rtmp[:, es * 512:(es + 1) * 512],
                                  in_=ps[:])
                um_ev.append(s_ume)

            @k.step("scalar", deps=[um_ev[-1]])
            def s_us(e, t=t):
                e.activation(out=rtm2[:], in_=rtmp[:], func=AF.Square,
                             accum_out=ss1[:])
                e.drain()
                e.activation(out=ss2[:], in_=ss1[:], func=AF.Ln, bias=cbv(CB_RMS),
                             scale=cbv(CB_ID))
                e.drain()
                return e.activation(out=cRS[:, t:t + 1], in_=ss2[:],
                                    func=AF.Exp, scale=cbv(CB_MH))

            @k.step("vector", deps=[s_us, s_gbc])
            def s_ux(e, t=t):
                xt = x_sb[:, t * D:(t + 1) * D]
                e.scalar_tensor_tensor(out=rtm2[:], in0=rtmp[:],
                                       scalar=cRS[:, t:t + 1], in1=g3b[:],
                                       op0=ALU.mult, op1=ALU.mult)
                e.drain()
                return e.tensor_add(out=xt, in0=xt, in1=rtm2[:])
            u_done.append(s_ux)

        k.barrier()
        s_.close()

    attn_phase()
    ffn_phase("b")

    @k.step("gpsimd", ndma=1)
    def s_out(e, sd):
        e.dma_start(
            out=bass.AP(out.tensor, 0, [[D, 128], [128 * D, TILES], [1, D]]),
            in_=x_sb[:].rearrange("p (t d) -> p t d", t=TILES),
        ).then_inc(sd, 16)

    @k.step("gpsimd", deps=[s_out])
    def s_fin(e):
        return e.memset(svrow[0:1, 0:1], 0.0)

    k.lower(nc)
    top.close()
    return nc, tapd


def _host_inputs(inputs):
    h = np.ascontiguousarray(
        np.asarray(inputs["L0_attn_h"]).reshape(NCORES * TOK, D))
    wsyn = np.ascontiguousarray(np.asarray(inputs["L0_ffn_W_syn"]))
    wout = np.ascontiguousarray(np.asarray(inputs["L0_ffn_W_out"]))
    f32 = np.float32
    i = np.arange(64, dtype=np.float64)
    freqs = 1.0 / (10000.0 ** (2.0 * i / 128.0))
    ident = np.eye(128).astype(np.float16)
    ones1 = np.ones((1, 128), np.float16)
    onesf = np.ones((128, 1), f32)
    iota = np.broadcast_to(np.arange(8, dtype=f32), (128, 8)).copy()
    maps = []
    for c in range(NCORES):
        pos = (c % 4) * TOK + np.arange(TOK, dtype=np.float64)
        ang = pos[:, None] * freqs[None, :]
        cos = np.cos(ang).astype(f32)
        sin = np.sin(ang).astype(f32)
        m = {
            "h": np.ascontiguousarray(h[c * TOK:(c + 1) * TOK]),
            "wsyn": np.ascontiguousarray(wsyn[c * TOK:(c + 1) * TOK]),
            "wout": np.ascontiguousarray(wout[c * 256:(c + 1) * 256]),
            "wdkv": np.asarray(inputs["L0_attn_w_d_kv"]),
            "wdq": np.asarray(inputs["L0_attn_w_d_q"]),
            "wuk": np.asarray(inputs["L0_attn_w_u_k"]),
            "wuv": np.asarray(inputs["L0_attn_w_u_v"]),
            "wuq": np.asarray(inputs["L0_attn_w_u_q"]),
            "wkr": np.asarray(inputs["L0_attn_w_kr"]),
            "wqr": np.asarray(inputs["L0_attn_w_qr"]),
            "wo": np.asarray(inputs["L0_attn_w_o"]),
            "g1": np.asarray(inputs["L0_attn_g1"]).reshape(1, C),
            "g2": np.asarray(inputs["L0_attn_g2"]).reshape(1, C),
            "g3": np.asarray(inputs["L0_attn_g3"]).reshape(1, D),
            "bsyn": np.asarray(inputs["L0_ffn_b_syn"]).reshape(1, N),
            "bout": np.asarray(inputs["L0_ffn_b_out"]).reshape(1, D),
            "gamma": np.asarray(inputs["L0_ffn_gamma"]).reshape(1, D),
            "beta": np.asarray(inputs["L0_ffn_beta"]).reshape(1, D),
            "cosq": (cos / 12.0).astype(f32),
            "sinq": (sin / 12.0).astype(f32),
            "cosk": cos,
            "sink": sin,
            "ident": ident,
            "ones1": ones1,
            "onesf": onesf,
            "iota8": iota,
        }
        maps.append(m)
    return maps


def kernel(**inputs):
    nc, _ = build()
    maps = _host_inputs(inputs)
    res = run_bass_kernel_spmd(nc, maps, core_ids=list(range(NCORES)))
    outs = np.concatenate([r["out"] for r in res.results], axis=0)
    return np.ascontiguousarray(outs.reshape(2, SEQ, D).astype(np.float32))


# version 14
# speedup vs baseline: 1.1325x; 1.0454x over previous
import contextlib

import numpy as np
import ml_dtypes
import concourse.bass as bass
import concourse.mybir as mybir
from concourse.bass_utils import run_bass_kernel_spmd

F32 = mybir.dt.float32
F16 = mybir.dt.float16
AF = mybir.ActivationFunctionType
ALU = mybir.AluOpType
AX = mybir.AxisListType

NCORES = 8
TOK = 1024
D = 2048
N = 8192
C = 512
SEQ = 4096
TILES = 8
GT = 4
NGRP = 2
HSLB = 32
DCH = 16
NCH = 64
KTS = 8
KKT = 819.0
CBIG = 100.0
LN_EPS = 1e-5
RMS_EPS = 1.1920929e-07

ENGS = ("sync", "scalar", "vector", "tensor", "gpsimd")


def ts_(i, n=128):
    return slice(i * n, (i + 1) * n)


class Step:
    __slots__ = ("eng", "kind", "seq", "slot", "async_")


NPOOL = 12


class K:

    def __init__(self):
        self.items = []
        self.ce = {e: 0 for e in ENGS}
        self.dcount = {e: 0 for e in ENGS}
        self.dval = {(e, i): 0 for e in ENGS for i in range(NPOOL)}
        self.dhist = {e: [] for e in ENGS}
        self.lastb = ()

    def step(self, eng, deps=(), ndma=0, cc=False):
        def deco(fn):
            st = Step()
            st.eng = eng
            st.async_ = cc
            if ndma:
                st.kind = "d"
                idx = self.dcount[eng]
                self.dcount[eng] += 1
                st.slot = idx % NPOOL
                self.dval[(eng, st.slot)] += ndma * 16
                st.seq = self.dval[(eng, st.slot)]
                extra = ()
                if idx >= NPOOL:
                    extra = (self.dhist[eng][idx - NPOOL],)
                self.dhist[eng].append(st)
                self.items.append((st, fn,
                                   tuple(deps) + extra + tuple(self.lastb)))
            else:
                st.kind = "c"
                st.slot = -1
                self.ce[eng] += 1
                st.seq = self.ce[eng]
                self.items.append((st, fn, tuple(deps) + tuple(self.lastb)))
            return st
        return deco

    def barrier(self, engines=None):
        engines = tuple(engines) if engines is not None else ENGS
        lasts = {}
        for st, fn, deps in self.items:
            if st.eng in engines:
                lasts[(st.eng, st.kind == "d", st.slot)] = st
        deps = tuple(lasts.values())
        outs = []
        for e in engines:
            st = Step()
            st.eng = e
            st.kind = "b"
            st.slot = -1
            st.async_ = False
            self.ce[e] += 1
            st.seq = self.ce[e]
            self.items.append((st, None, deps))
            outs.append(st)
        self.lastb = tuple(outs)
        return outs

    def lower(self, nc):
        with contextlib.ExitStack() as ctx:
            sems = {}
            for e in ENGS:
                sems["se_" + e] = ctx.enter_context(nc.semaphore("se_" + e))
                for i in range(NPOOL):
                    sems[f"sd_{e}_{i}"] = ctx.enter_context(
                        nc.semaphore(f"sd_{e}_{i}"))
            block = ctx.enter_context(nc.Block())
            streams = {e: [] for e in ENGS}
            for st, fn, deps in self.items:
                streams[st.eng].append((st, fn, deps))

            def mk(e):
                def run(eng):
                    mark = {}
                    for st, fn, deps in streams[e]:
                        need_drain = False
                        for d in deps:
                            if d.kind == "d":
                                sem, val = f"sd_{d.eng}_{d.slot}", d.seq
                            else:
                                if d.eng == e and not getattr(d, "async_", False):
                                    need_drain = True
                                    continue
                                sem, val = "se_" + d.eng, d.seq
                            if mark.get(sem, 0) >= val:
                                continue
                            mark[sem] = val
                            eng.wait_ge(sems[sem], val)
                        if need_drain:
                            eng.drain()
                        if st.kind == "b":
                            eng.sem_inc(sems["se_" + e], 1)
                        elif st.kind == "d":
                            fn(eng, sems[f"sd_{e}_{st.slot}"])
                        else:
                            last = fn(eng)
                            last.then_inc(sems["se_" + e], 1)
                return run
            block.sync(mk("sync"))
            block.scalar(mk("scalar"))
            block.vector(mk("vector"))
            block.tensor(mk("tensor"))
            block.gpsimd(mk("gpsimd"))


def build(taps=()):
    nc = bass.Bass("TRN2", target_bir_lowering=False)
    k = K()
    tapd = {}

    def din(name, shape, dt=F32):
        return nc.dram_tensor(name, shape, dt, kind="ExternalInput").ap()

    h = din("h", [TOK, D])
    wsyn = din("wsyn", [TOK, D])
    wout = din("wout", [256, N])
    wdkv = din("wdkv", [C, D])
    wdq = din("wdq", [C, D])
    wuk = din("wuk", [D, C])
    wuv = din("wuv", [D, C])
    wuq = din("wuq", [D, C])
    wkr = din("wkr", [D, D])
    wqr = din("wqr", [D, C])
    wo = din("wo", [D, D])
    g1 = din("g1", [1, C])
    g2 = din("g2", [1, C])
    g3 = din("g3", [1, D])
    bsyn = din("bsyn", [1, N])
    bout = din("bout", [1, D])
    gamma = din("gamma", [1, D])
    beta = din("beta", [1, D])
    cosq = din("cosq", [TOK, 64])
    sinq = din("sinq", [TOK, 64])
    cosk = din("cosk", [TOK, 64])
    sink = din("sink", [TOK, 64])
    ident = din("ident", [128, 128], F16)
    ones1 = din("ones1", [1, 128], F16)
    onesf = din("onesf", [128, 1])
    iota8 = din("iota8", [128, 8])
    out = nc.dram_tensor("out", [TOK, D], F32, kind="ExternalOutput").ap()

    sqp = nc.dram_tensor("sqp", [1, 2], F32)
    sqa = nc.dram_tensor("sqa", [1, 2], F32)
    tsynp4 = [nc.dram_tensor(f"tsynp{j}", [D, 256], F16) for j in range(4)]
    tsyna4 = [nc.dram_tensor(f"tsyna{j}", [NCORES * D, 256], F16)
              for j in range(4)]
    toutp2 = [nc.dram_tensor(f"toutp{j}", [N // 2, 256], F16)
              for j in range(2)]
    touta2 = [nc.dram_tensor(f"touta{j}", [NCORES * N // 2, 256], F16)
              for j in range(2)]
    bsynp_d = nc.dram_tensor("bsynp_d", [1, N], F16)
    boutp_d = nc.dram_tensor("boutp_d", [1, D], F16)
    krtp = nc.dram_tensor("krtp", [D, TOK], F16)
    krta = nc.dram_tensor("krta", [4 * D, TOK], F16)
    ckvp = nc.dram_tensor("ckvp", [TOK, C], F16)
    ckva = nc.dram_tensor("ckva", [4 * TOK, C], F16)
    ckvtp = nc.dram_tensor("ckvtp", [C, TOK], F16)
    ckvta = nc.dram_tensor("ckvta", [4 * C, TOK], F16)

    top = contextlib.ExitStack()
    e_ = top.enter_context
    x_sb = e_(nc.sbuf_tensor("x_sb", [128, TILES * D], F16))
    identb = e_(nc.sbuf_tensor("identb", [128, 128], F16))
    ones1b = e_(nc.sbuf_tensor("ones1b", [1, 128], F16))
    onesfb = e_(nc.sbuf_tensor("onesfb", [128, 1], F32))
    iotab = e_(nc.sbuf_tensor("iotab", [128, 8], F32))
    gb2 = e_(nc.sbuf_tensor("gb2", [128, D], F16))
    bb2 = e_(nc.sbuf_tensor("bb2", [128, D], F16))
    ssynb = e_(nc.sbuf_tensor("ssynb", [128, 1], F32))
    soutb = e_(nc.sbuf_tensor("soutb", [128, 1], F32))
    hsynb = e_(nc.sbuf_tensor("hsynb", [128, 1], F32))
    houtb = e_(nc.sbuf_tensor("houtb", [128, 1], F32))
    sv1 = e_(nc.sbuf_tensor("sv1", [128, 1], F32))
    sv2 = e_(nc.sbuf_tensor("sv2", [128, 1], F32))
    sv3 = e_(nc.sbuf_tensor("sv3", [128, 1], F32))
    sv4 = e_(nc.sbuf_tensor("sv4", [128, 1], F32))
    ss1 = e_(nc.sbuf_tensor("ss1", [128, 1], F32))
    ss2 = e_(nc.sbuf_tensor("ss2", [128, 1], F32))
    cS = e_(nc.sbuf_tensor("cS", [128, 8], F32))
    cQ = e_(nc.sbuf_tensor("cQ", [128, 8], F32))
    cMU = e_(nc.sbuf_tensor("cMU", [128, 8], F32))
    cVR = e_(nc.sbuf_tensor("cVR", [128, 8], F32))
    cRS = e_(nc.sbuf_tensor("cRS", [128, 8], F32))
    cTH = e_(nc.sbuf_tensor("cTH", [128, 8], F32))
    u8 = e_(nc.sbuf_tensor("u8", [128, 8], F32))
    d8 = e_(nc.sbuf_tensor("d8", [128, 8], F32))
    hot = e_(nc.sbuf_tensor("hot", [128, 8], F32))
    cb = e_(nc.sbuf_tensor("cb", [128, 12], F32))
    svrow = e_(nc.sbuf_tensor("svrow", [1, 4], F32))
    identso = e_(nc.sbuf_tensor("identso", [128, 128], F16))
    u16 = e_(nc.sbuf_tensor("u16", [128, 16], F32))
    d16 = e_(nc.sbuf_tensor("d16", [128, 16], F32))
    onesr32 = e_(nc.sbuf_tensor("onesr32", [1, 128], F32))
    (CB_LN, CB_RMS, CB_MH, CB_PH, CB_MO, CB_I12, CB_IC, CB_ID,
     CB_IN) = range(9)

    def cbv(i):
        return cb[:, i:i + 1]

    @k.step("sync", ndma=4)
    def s_c0(e, sd):
        e.dma_start(out=identb[:], in_=ident).then_inc(sd, 16)
        e.dma_start(out=ones1b[:], in_=ones1).then_inc(sd, 16)
        e.dma_start(out=onesfb[:], in_=onesf).then_inc(sd, 16)
        e.dma_start(out=iotab[:], in_=iota8).then_inc(sd, 16)

    @k.step("vector")
    def s_cb(e):
        e.memset(cb[:, 0:1], LN_EPS)
        e.memset(cb[:, 1:2], RMS_EPS)
        e.memset(cb[:, 2:3], -0.5)
        e.memset(cb[:, 3:4], 0.5)
        e.memset(cb[:, 4:5], -1.0)
        e.memset(onesr32[:], 1.0)
        e.memset(cb[:, 5:6], 1.0 / 12.0)
        e.memset(cb[:, 6:7], 1.0 / C)
        e.memset(cb[:, 7:8], 1.0 / D)
        return e.memset(cb[:, 8:9], 1.0 / N)

    @k.step("gpsimd", ndma=3)
    def s_x0(e, sd):
        e.dma_start(
            out=x_sb[:].rearrange("p (t d) -> p t d", t=TILES),
            in_=bass.AP(h.tensor, 0, [[D, 128], [128 * D, TILES], [1, D]]),
        ).then_inc(sd, 16)
        e.dma_start(out=gb2[0:1, :], in_=gamma).then_inc(sd, 16)
        e.dma_start(out=bb2[0:1, :], in_=beta).then_inc(sd, 16)

    @k.step("vector", deps=[s_x0])
    def s_gb(e):
        e.tensor_scalar(out=gb2[0:1, :], in0=gb2[0:1, :], scalar1=0.5,
                        scalar2=None, op0=ALU.mult)
        return e.tensor_scalar(out=bb2[0:1, :], in0=bb2[0:1, :], scalar1=0.5,
                               scalar2=None, op0=ALU.mult)


    qs = contextlib.ExitStack()
    wf = qs.enter_context(nc.sbuf_tensor("wf", [128, 16384], F32))
    tq = qs.enter_context(nc.sbuf_tensor("tq", [128, 16384], F16))
    tq2 = qs.enter_context(nc.sbuf_tensor("tq2", [128, 16384], F16))
    tq3 = qs.enter_context(nc.sbuf_tensor("tq3", [128, 16384], F16))
    sqsb = qs.enter_context(nc.sbuf_tensor("sqsb", [1, 2], F32))
    qacc = qs.enter_context(nc.sbuf_tensor("qacc", [128, 2], F32))
    pq = qs.enter_context(nc.psum_tensor("pq", [128, 1024], F16))
    pq2 = qs.enter_context(nc.psum_tensor("pq2", [128, 1024], F16))
    psml = qs.enter_context(nc.psum_tensor("psml", [1, 2], F32))
    pbc = qs.enter_context(nc.psum_tensor("pbc", [128, 512], F32))

    bcast_ev = []
    for which, buf in ((0, gb2), (1, bb2)):
        for sl in range(4):
            @k.step("tensor", deps=[s_gb, s_c0] + ([bcast_ev[-1]] if bcast_ev else []))
            def s_bcm(e, buf=buf, sl=sl):
                return e.matmul(pbc[:], ones1b[0:1, :],
                                buf[0:1, sl * 512:(sl + 1) * 512],
                                start=True, stop=True)

            @k.step("scalar", deps=[s_bcm])
            def s_bce(e, buf=buf, sl=sl):
                return e.copy(out=buf[:, sl * 512:(sl + 1) * 512], in_=pbc[:])
            bcast_ev.append(s_bce)
    s_gbb = bcast_ev[-1]

    @k.step("sync", ndma=1)
    def s_ldsyn(e, sd):
        e.dma_start(
            out=wf[:].rearrange("p (c f) -> p c f", c=8),
            in_=bass.AP(wsyn.tensor, 0, [[D, 128], [128 * D, 8], [1, D]]),
        ).then_inc(sd, 16)

    @k.step("gpsimd", ndma=1)
    def s_ldoutf16(e, sd):
        e.dma_start(
            out=tq[:].rearrange("p (c f) -> p c f", c=2),
            in_=bass.AP(wout.tensor, 0, [[N, 128], [128 * N, 2], [1, N]]),
        ).then_inc(sd, 16)

    @k.step("scalar", deps=[s_ldsyn])
    def s_abs0(e):
        return e.activation(out=tq2[:], in_=wf[:], func=AF.Abs,
                            accum_out=qacc[:, 0:1])

    @k.step("scalar", deps=[s_ldoutf16])
    def s_abs1(e):
        return e.activation(out=tq3[:], in_=tq[:], func=AF.Abs,
                            accum_out=qacc[:, 1:2])

    @k.step("tensor", deps=[s_abs0, s_abs1, s_c0])
    def s_sqmm(e):
        e.matmul(psml[0:1, 0:1], qacc[:, 0:1], onesfb[:], start=True, stop=True)
        return e.matmul(psml[0:1, 1:2], qacc[:, 1:2], onesfb[:], start=True,
                        stop=True)

    @k.step("scalar", deps=[s_sqmm])
    def s_sqev(e):
        return e.copy(out=sqsb[:], in_=psml[:])

    @k.step("gpsimd", deps=[s_sqev], ndma=1)
    def s_sqst(e, sd):
        e.dma_start(out=sqp.ap(), in_=sqsb[:]).then_inc(sd, 16)

    @k.step("gpsimd", deps=[s_sqst], cc=True)
    def s_ar(e):
        return e.collective_compute(
            "AllReduce", ALU.add, replica_groups=[list(range(NCORES))],
            ins=[sqp.ap().opt()], outs=[sqa.ap().opt()])

    @k.step("gpsimd", deps=[s_ar], ndma=1)
    def s_sqld(e, sd):
        e.dma_start(out=sqsb[:], in_=sqa.ap()).then_inc(sd, 16)

    nelem = float(N) * float(D)

    @k.step("vector", deps=[s_sqld])
    def s_scale(e):
        e.tensor_scalar(out=svrow[0:1, 0:1], in0=sqsb[0:1, 0:1],
                        scalar1=1.0 / nelem, scalar2=None, op0=ALU.mult)
        e.tensor_scalar(out=svrow[0:1, 1:2], in0=sqsb[0:1, 1:2],
                        scalar1=1.0 / nelem, scalar2=None, op0=ALU.mult)
        e.drain()
        e.tensor_scalar(out=svrow[0:1, 2:3], in0=svrow[0:1, 0:1], scalar1=0.5,
                        scalar2=0.5e-8, op0=ALU.mult, op1=ALU.add)
        return e.tensor_scalar(out=svrow[0:1, 3:4], in0=svrow[0:1, 1:2],
                               scalar1=0.5, scalar2=0.5e-8, op0=ALU.mult,
                               op1=ALU.add)

    @k.step("tensor", deps=[s_scale, s_cb])
    def s_scm(e):
        return e.matmul(pbc[:, 0:4], onesr32[0:1, :], svrow[0:1, :],
                        start=True, stop=True)

    @k.step("scalar", deps=[s_scm])
    def s_scbc(e):
        e.copy(out=ssynb[:], in_=pbc[:, 0:1])
        e.copy(out=soutb[:], in_=pbc[:, 1:2])
        e.copy(out=hsynb[:], in_=pbc[:, 2:3])
        return e.copy(out=houtb[:], in_=pbc[:, 3:4])

    @k.step("vector", deps=[s_scbc, s_c0])
    def s_iso(e):
        e.reciprocal(out=sv4[:], in_=soutb[:])
        e.drain()
        e.tensor_scalar(out=identso[:], in0=identb[:], scalar1=sv4[:],
                        scalar2=None, op0=ALU.mult)
        e.drain()
        return e.tensor_copy(out=sv4[:], in_=soutb[:])

    def quant_mat(which, dep_load, extra_q_deps=()):
        nchk, ncol = (8, D) if which == 0 else (2, N)
        half_sb = hsynb if which == 0 else houtb

        @k.step("vector", deps=[dep_load, s_scbc] + list(extra_q_deps))
        def s_q(e, which=which, half_sb=half_sb):
            e.tensor_scalar(out=sv1[:], in0=half_sb[:], scalar1=-1.0,
                            scalar2=None, op0=ALU.mult)
            src = wf[:] if which == 0 else tq[:]
            e.tensor_scalar(out=tq2[:], in0=src, scalar1=half_sb[:],
                            scalar2=None, op0=ALU.is_ge)
            e.drain()
            e.tensor_scalar(out=tq3[:], in0=src, scalar1=sv1[:],
                            scalar2=None, op0=ALU.is_le)
            e.drain()
            return e.tensor_sub(out=tq[:], in0=tq2[:], in1=tq3[:])

        ccn = ncol // 128
        ngrp = (nchk * ccn) // 8
        tr_steps = []
        te_steps = []
        for g in range(ngrp):
            ps = pq if g % 2 == 0 else pq2
            deps = [s_q, s_c0] if g < 2 else [te_steps[g - 2]]

            @k.step("tensor", deps=deps)
            def s_tr(e, g=g, ps=ps, ccn=ccn, ncol=ncol):
                last = None
                for j in range(8):
                    blk = g * 8 + j
                    rc, cc = blk // ccn, blk % ccn
                    last = e.transpose(
                        ps[:, ts_(j)],
                        tq[:, rc * ncol + cc * 128: rc * ncol + cc * 128 + 128],
                        identb[:])
                return last
            tr_steps.append(s_tr)

            @k.step("scalar", deps=[s_tr])
            def s_te(e, g=g, ps=ps, ccn=ccn, nchk=nchk):
                last = None
                for j in range(8):
                    blk = g * 8 + j
                    rc, cc = blk // ccn, blk % ccn
                    o0 = cc * (128 * nchk) + rc * 128
                    last = e.copy(out=tq2[:, o0:o0 + 128], in_=ps[:, ts_(j)])
                return last
            te_steps.append(s_te)

        wps = []
        if which == 0:
            for j in range(4):
                @k.step("sync", deps=[te_steps[min(4 * j + 3,
                                                   len(te_steps) - 1)]],
                        ndma=1)
                def s_wpj(e, sd, j=j):
                    e.dma_start(
                        out=bass.AP(tsynp4[j], 0,
                                    [[256, 128], [128 * 256, 16], [1, 256]]),
                        in_=bass.AP(tq2[:].tensor, j * 256,
                                    [[16384, 128], [1024, 16], [1, 256]]),
                    ).then_inc(sd, 16)
                wps.append(s_wpj)
        else:
            for j in range(2):
                @k.step("sync", deps=[te_steps[-1]], ndma=1)
                def s_wpj(e, sd, j=j):
                    e.dma_start(
                        out=bass.AP(toutp2[j], 0,
                                    [[256, 128], [128 * 256, 32], [1, 256]]),
                        in_=bass.AP(tq2[:].tensor, j * 32 * 256,
                                    [[16384, 128], [256, 32], [1, 256]]),
                    ).then_inc(sd, 16)
                wps.append(s_wpj)
        return wps, tr_steps[-1]

    wp_out, tr_out = quant_mat(1, s_ldoutf16)
    wp_syn, tr_syn = quant_mat(0, s_ldsyn,
                               extra_q_deps=wp_out + [tr_out])

    ag_syn4 = []
    for j in range(4):
        @k.step("gpsimd", deps=[wp_syn[j]], cc=True)
        def s_agsj(e, j=j):
            return e.collective_compute(
                "AllGather", ALU.bypass, replica_groups=[list(range(NCORES))],
                ins=[tsynp4[j].ap().opt()], outs=[tsyna4[j].ap().opt()])
        ag_syn4.append(s_agsj)
    ag_out2 = []
    for j in range(2):
        @k.step("gpsimd", deps=[wp_out[j]], cc=True)
        def s_agoj(e, j=j):
            return e.collective_compute(
                "AllGather", ALU.bypass, replica_groups=[list(range(NCORES))],
                ins=[toutp2[j].ap().opt()], outs=[touta2[j].ap().opt()])
        ag_out2.append(s_agoj)

    b_steps = []
    for which in range(2):
        bsrc = bsyn if which == 0 else bout
        bdst = bsynp_d if which == 0 else boutp_d
        s_sb = ssynb if which == 0 else soutb
        nn = N if which == 0 else D
        brow_ap = tq3[0:1, 0:nn].bitcast(F32)[0:1, 0:nn] if False else None
        brow = tq3[:].bitcast(F32)

        @k.step("sync", deps=wp_syn + [tr_syn], ndma=1)
        def s_bld(e, sd, bsrc=bsrc, nn=nn, brow=brow):
            e.dma_start(out=brow[0:1, 0:nn], in_=bsrc).then_inc(sd, 16)

        @k.step("vector", deps=[s_bld, s_scbc] + list(b_steps))
        def s_bdiv(e, nn=nn, s_sb=s_sb, brow=brow):
            e.reciprocal(out=sv1[0:1, 0:1], in_=s_sb[0:1, 0:1])
            e.drain()
            return e.tensor_scalar(out=tq[0:1, 0:nn], in0=brow[0:1, 0:nn],
                                   scalar1=sv1[0:1, 0:1], scalar2=None,
                                   op0=ALU.mult)

        @k.step("sync", deps=[s_bdiv], ndma=1)
        def s_bst(e, sd, bdst=bdst, nn=nn):
            e.dma_start(out=bdst.ap()[0:1, 0:nn],
                        in_=tq[0:1, 0:nn]).then_inc(sd, 16)
        b_steps.append(s_bst)

    k.barrier(engines=("sync", "scalar", "vector", "tensor"))
    qs.close()

    def ffn_phase(tag):
        ng_, gt_ = 4, 2
        fs = contextlib.ExitStack()
        en = fs.enter_context
        z2 = [en(nc.sbuf_tensor(f"z{tag}{i}", [128, gt_ * N], F16))
              for i in range(2)]
        x1g2 = [en(nc.sbuf_tensor(f"x1g{tag}{i}", [128, gt_ * D], F16))
                for i in range(2)]
        x1t2 = [en(nc.sbuf_tensor(f"x1t{tag}{i}", [128, DCH * 256], F16))
                for i in range(2)]
        wsl = [en(nc.sbuf_tensor(f"wsl{tag}{i}", [128, DCH * 256], F16))
               for i in range(2)]
        woc = [en(nc.sbuf_tensor(f"woc{tag}{i}", [128, 2048], F16))
               for i in range(3)]
        sB = en(nc.sbuf_tensor(f"sB{tag}", [128, N], F16))
        ysc = en(nc.sbuf_tensor(f"ysc{tag}", [128, N // 2], F16))
        jnk = en(nc.sbuf_tensor(f"jnk{tag}", [128, D], F16))
        tmp4 = [en(nc.sbuf_tensor(f"tmp4{tag}{t}", [128, D], F16))
                for t in range(gt_)]
        bst = [en(nc.sbuf_tensor(f"bst{tag}{i}", [1, 256], F16))
               for i in range(2)]
        bsto = [en(nc.sbuf_tensor(f"bsto{tag}{i}", [1, 512], F16))
                for i in range(2)]
        pz = [en(nc.psum_tensor(f"pz{tag}{i}", [128, 256], F32))
              for i in range(2)]
        pht = [en(nc.psum_tensor(f"pht{tag}{i}", [128, 1024], F16))
               for i in range(2)]
        px = [en(nc.psum_tensor(f"px{tag}{i}", [128, 1024], F16))
              for i in range(2)]
        po = [en(nc.psum_tensor(f"po{tag}{i}", [128, 512], F32))
              for i in range(gt_)]

        zmm, zev, omm = [], [], []
        th_steps, htr_ev, x1tev = [], [], []
        ln_apply_g, x1t_done_g, zmm_g_last, omm_g_last = {}, {}, {}, {}
        hid_g, ep, ep_last, l2a_last, grp_done = {}, {}, {}, {}, {}
        grp_res = {}
        stat_sig = [None]

        def emit_ln1(g):
            buf = g % 2
            apl = []
            for t in range(gt_):
                col = buf * 2 + t
                deps = [] if g >= 2 else [s_x0, s_gbb]

                @k.step("scalar", deps=deps)
                def s_st(e, g=g, t=t, col=col):
                    xt = x_sb[:, (g * gt_ + t) * D:(g * gt_ + t + 1) * D]
                    e.activation(out=jnk[:], in_=xt, func=AF.Copy,
                                 accum_out=cS[:, col:col + 1])
                    e.drain()
                    return e.activation(out=jnk[:], in_=xt, func=AF.Square,
                                        accum_out=cQ[:, col:col + 1])

                apdeps = [s_st] + ([omm_g_last[g - 2]] if g >= 2 else [])

                @k.step("scalar", deps=apdeps)
                def s_ap(e, g=g, t=t, buf=buf, col=col):
                    xt = x_sb[:, (g * gt_ + t) * D:(g * gt_ + t + 1) * D]
                    e.activation(out=cMU[:, col:col + 1],
                                 in_=cS[:, col:col + 1], func=AF.Identity,
                                 scale=cbv(CB_ID))
                    e.drain()
                    e.activation(out=ss1[:], in_=cMU[:, col:col + 1],
                                 func=AF.Square)
                    e.drain()
                    e.activation(out=ss2[:], in_=ss1[:], func=AF.Identity,
                                 scale=cbv(CB_MO))
                    e.drain()
                    e.activation(out=cVR[:, col:col + 1],
                                 in_=cQ[:, col:col + 1], func=AF.Identity,
                                 scale=cbv(CB_ID), bias=ss2[:])
                    e.drain()
                    e.activation(out=ss1[:], in_=cVR[:, col:col + 1],
                                 func=AF.Ln, bias=cbv(CB_LN))
                    e.drain()
                    e.activation(out=cRS[:, col:col + 1], in_=ss1[:],
                                 func=AF.Exp, scale=cbv(CB_MH))
                    e.drain()
                    e.activation(out=ss2[:], in_=cMU[:, col:col + 1],
                                 func=AF.Identity, scale=cRS[:, col:col + 1])
                    e.drain()
                    e.activation(out=ss2[:], in_=ss2[:], func=AF.Identity,
                                 scale=cbv(CB_MO))
                    e.drain()
                    return e.activation(out=x1g2[buf][:, t * D:(t + 1) * D],
                                        in_=xt, func=AF.Identity, bias=ss2[:],
                                        scale=cRS[:, col:col + 1])
                apl.append(s_ap)
            ln_apply_g[g] = apl

        def emit_x1tr(g):
            buf = g % 2
            for t in range(gt_):
                for q8 in range(2):
                    gi = len(x1tev)
                    ps = px[gi % 2]
                    deps = [ln_apply_g[g][t]]
                    if gi >= 2:
                        deps.append(x1tev[gi - 2])
                    if g >= 2 and t == 0 and q8 == 0:
                        deps.append(zmm_g_last[g - 2])

                    @k.step("tensor", deps=deps)
                    def s_tr(e, g=g, t=t, q8=q8, ps=ps, buf=buf):
                        last = None
                        for j in range(8):
                            dc = q8 * 8 + j
                            last = e.transpose(
                                ps[:, ts_(j)],
                                x1g2[buf][:, t * D + dc * 128:
                                          t * D + dc * 128 + 128],
                                identb[:])
                        return last

                    @k.step("scalar", deps=[s_tr])
                    def s_tev(e, t=t, q8=q8, ps=ps, buf=buf):
                        last = None
                        for j in range(8):
                            dc = q8 * 8 + j
                            last = e.copy(
                                out=x1t2[buf][:, dc * 256 + t * 128:
                                              dc * 256 + t * 128 + 128],
                                in_=ps[:, ts_(j)])
                        return last
                    x1tev.append(s_tev)
            x1t_done_g[g] = x1tev[-1]

        def emit_z(g):
            buf = g % 2
            hs_seq = [p * 4 + j for j in range(4) for p in range(8)]
            for li, hs in enumerate(hs_seq):
                idx = g * HSLB + li
                depw = [zmm[(idx - 2) * gt_ + gt_ - 1]] if idx >= 2 else []
                depw = depw + [ag_syn4[hs % 4]]

                @k.step("sync", deps=depw + [b_steps[0]], ndma=2)
                def s_w(e, sd, hs=hs, idx=idx):
                    e.dma_start(
                        out=wsl[idx % 2][:].rearrange("p (c f) -> p c f",
                                                      c=DCH),
                        in_=bass.AP(tsyna4[hs % 4], ((hs // 4) * D) * 256,
                                    [[256, 128], [128 * 256, DCH], [1, 256]]),
                    ).then_inc(sd, 16)
                    e.dma_start(out=bst[idx % 2][0:1, :],
                                in_=bsynp_d.ap()[0:1, hs * 256:(hs + 1) * 256]
                                ).then_inc(sd, 16)
                s_b = s_w

                for t in range(gt_):
                    kk = idx * gt_ + t
                    psz = pz[kk % 2]
                    deps = [s_w, s_b, x1t_done_g[g]]
                    if kk >= 2:
                        deps.append(zev[kk - 2])

                    @k.step("tensor", deps=deps)
                    def s_zm(e, hs=hs, t=t, psz=psz, idx=idx, buf=buf):
                        e.matmul(psz[:], ones1b[0:1, :], bst[idx % 2][0:1, :],
                                 start=True, stop=False)
                        last = None
                        for dc in range(DCH):
                            last = e.matmul(
                                psz[:],
                                x1t2[buf][:, dc * 256 + t * 128:
                                          dc * 256 + t * 128 + 128],
                                wsl[idx % 2][:, dc * 256:(dc + 1) * 256],
                                start=False, stop=(dc == DCH - 1))
                        return last
                    zmm.append(s_zm)

                    zdeps = [s_zm]
                    if li == 0 and t == 0 and g >= 2:
                        zdeps.append(omm_g_last[g - 2])

                    @k.step("scalar", deps=zdeps)
                    def s_ze(e, hs=hs, t=t, psz=psz, buf=buf):
                        return e.activation(
                            out=z2[buf][:, t * N + hs * 256:
                                        t * N + hs * 256 + 256],
                            in_=psz[:], func=AF.Identity, scale=ssynb[:])
                    zev.append(s_ze)
            zmm_g_last[g] = zmm[-1]

        def emit_stats0():
            z_done = [zev[HSLB * gt_ - 1]]

            @k.step("scalar", deps=z_done)
            def s_zst(e):
                zt = z2[0][:, 0:N]
                e.activation(out=sB[:], in_=zt, func=AF.Copy,
                             accum_out=cS[:, 4:5])
                e.drain()
                return e.activation(out=sB[:], in_=zt, func=AF.Square,
                                    accum_out=cQ[:, 4:5])

            @k.step("scalar", deps=[s_zst])
            def s_sig(e):
                e.activation(out=cMU[:, 4:5], in_=cS[:, 4:5],
                             func=AF.Identity, scale=cbv(CB_IN))
                e.drain()
                e.activation(out=ss1[:], in_=cMU[:, 4:5], func=AF.Square)
                e.drain()
                e.activation(out=ss2[:], in_=ss1[:], func=AF.Identity,
                             scale=cbv(CB_MO))
                e.drain()
                e.activation(out=cVR[:, 4:5], in_=cQ[:, 4:5],
                             func=AF.Identity, scale=cbv(CB_IN), bias=ss2[:])
                e.drain()
                e.activation(out=ss1[:], in_=cVR[:, 4:5], func=AF.Ln)
                e.drain()
                return e.activation(out=cRS[:, 4:5], in_=ss1[:],
                                    func=AF.Exp, scale=cbv(CB_PH))
            stat_sig[0] = s_sig

        def emit_th(g, t):
            buf = g % 2
            gt = g * gt_ + t
            z_done = [zev[(g + 1) * HSLB * gt_ - 1]]
            if th_steps:
                sig_dep = th_steps[-1]
            else:
                sig_dep = stat_sig[0]
            deps = [sig_dep] + z_done
            if htr_ev:
                deps.append(htr_ev[-1])

            @k.step("vector", deps=deps)
            def s_th(e, g=g, t=t, buf=buf, gt=gt):
                zt = z2[buf][:, t * N:(t + 1) * N]
                th = cTH[:, 4 + gt % 2:5 + gt % 2]
                sig = cRS[:, 4:5]
                if gt == 0:
                    mu = cMU[:, 4:5]
                    e.tensor_scalar(out=sv1[:], in0=sig, scalar1=1.2815516,
                                    scalar2=None, op0=ALU.mult)
                    e.drain()
                    e.tensor_add(out=th, in0=mu, in1=sv1[:])
                else:
                    e.tensor_copy(out=th, in_=cTH[:, 4 + (gt - 1) % 2:
                                                  5 + (gt - 1) % 2])
                e.drain()
                e.tensor_scalar(out=sB[:], in0=zt, scalar1=th,
                                scalar2=None, op0=ALU.is_ge, op1=ALU.add,
                                accum_out=sv1[:])
                e.drain()
                e.tensor_copy(out=sv2[:], in_=th)
                e.tensor_scalar(out=sv3[:], in0=sv1[:], scalar1=-KKT,
                                scalar2=1.0 / 1437.7, op0=ALU.add,
                                op1=ALU.mult)
                e.drain()
                e.tensor_mul(out=sv3[:], in0=sv3[:], in1=sig)
                e.drain()
                e.tensor_add(out=th, in0=th, in1=sv3[:])
                e.drain()
                e.tensor_scalar(out=sB[:], in0=zt, scalar1=th,
                                scalar2=None, op0=ALU.is_ge, op1=ALU.add,
                                accum_out=sv4[:])
                e.drain()
                e.tensor_sub(out=sv3[:], in0=th, in1=sv2[:])
                e.drain()
                e.tensor_sub(out=sv2[:], in0=sv4[:], in1=sv1[:])
                e.drain()
                e.tensor_scalar(out=sv1[:], in0=sv2[:], scalar1=sv2[:],
                                scalar2=0.25, op0=ALU.mult, op1=ALU.add)
                e.drain()
                e.reciprocal(out=sv1[:], in_=sv1[:])
                e.drain()
                e.tensor_mul(out=sv3[:], in0=sv3[:], in1=sv2[:])
                e.drain()
                e.tensor_mul(out=sv3[:], in0=sv3[:], in1=sv1[:])
                e.drain()
                e.tensor_scalar(out=sv1[:], in0=sig, scalar1=-0.01,
                                scalar2=None, op0=ALU.mult)
                e.drain()
                e.tensor_max(out=sv3[:], in0=sv3[:], in1=sv1[:])
                e.drain()
                e.tensor_scalar(out=sv1[:], in0=sig, scalar1=-1e-5,
                                scalar2=None, op0=ALU.mult)
                e.drain()
                e.tensor_tensor(out=sv3[:], in0=sv3[:], in1=sv1[:],
                                op=ALU.min)
                e.tensor_scalar(out=sv4[:], in0=sv4[:], scalar1=-KKT,
                                scalar2=None, op0=ALU.add)
                e.drain()
                e.tensor_mul(out=sv4[:], in0=sv4[:], in1=sv3[:])
                e.drain()
                e.tensor_sub(out=th, in0=th, in1=sv4[:])
                e.drain()
                e.tensor_scalar(out=sB[:], in0=zt, scalar1=th,
                                scalar2=None, op0=ALU.is_ge, op1=ALU.add,
                                accum_out=sv4[:])
                e.drain()
                e.tensor_scalar(out=sv4[:], in0=sv4[:], scalar1=-KKT,
                                scalar2=None, op0=ALU.add)
                e.drain()
                e.tensor_mul(out=sv4[:], in0=sv4[:], in1=sv3[:])
                e.drain()
                e.tensor_sub(out=th, in0=th, in1=sv4[:])
                e.drain()
                e.tensor_scalar(out=sB[:], in0=zt, scalar1=th,
                                scalar2=None, op0=ALU.is_lt, op1=ALU.add,
                                accum_out=sv1[:])
                e.drain()
                xsc = x1t2[buf][:]
                H = N // 2
                for ci in range(2):
                    mh = sB[:, ci * H:(ci + 1) * H]
                    zh = zt[:, ci * H:(ci + 1) * H]
                    e.scalar_tensor_tensor(out=ysc[:], in0=mh, scalar=-CBIG,
                                           in1=zh, op0=ALU.mult,
                                           op1=ALU.subtract)
                    e.drain()
                    e.max(out=u16[:, 8 * ci:8 * ci + 8], in_=ysc[:])
                    e.tensor_scalar(out=xsc, in0=ysc[:], scalar1=-1.0,
                                    scalar2=-CBIG, op0=ALU.mult, op1=ALU.add)
                    e.drain()
                    e.max(out=d16[:, 8 * ci:8 * ci + 8], in_=xsc)
                e.drain()
                e.max(out=u8[:], in_=u16[:])
                e.max(out=d8[:], in_=d16[:])
                e.drain()
                e.tensor_scalar(out=sv2[:], in0=sv1[:], scalar1=-1.0,
                                scalar2=float(N) - KKT, op0=ALU.mult,
                                op1=ALU.add)
                e.drain()
                e.tensor_scalar(out=sv3[:], in0=sv2[:], scalar1=0.0,
                                scalar2=7.0, op0=ALU.max, op1=ALU.min)
                e.drain()
                e.tensor_scalar(out=hot[:], in0=iotab[:], scalar1=sv3[:],
                                scalar2=None, op0=ALU.is_equal)
                e.drain()
                e.tensor_mul(out=hot[:], in0=hot[:], in1=u8[:])
                e.drain()
                e.reduce_sum(out=sv3[:], in_=hot[:], axis=AX.X, negate=True)
                e.tensor_scalar(out=sv4[:], in0=sv2[:], scalar1=-1.0,
                                scalar2=-1.0, op0=ALU.mult, op1=ALU.add)
                e.drain()
                e.tensor_scalar(out=sv4[:], in0=sv4[:], scalar1=0.0,
                                scalar2=7.0, op0=ALU.max, op1=ALU.min)
                e.drain()
                e.tensor_scalar(out=hot[:], in0=iotab[:], scalar1=sv4[:],
                                scalar2=None, op0=ALU.is_equal)
                e.drain()
                e.tensor_mul(out=hot[:], in0=hot[:], in1=d8[:])
                e.drain()
                e.reduce_sum(out=sv4[:], in_=hot[:], axis=AX.X)
                e.tensor_scalar(out=sv1[:], in0=sv2[:], scalar1=0.0,
                                scalar2=None, op0=ALU.is_ge)
                e.drain()
                e.tensor_mul(out=sv3[:], in0=sv3[:], in1=sv1[:])
                e.drain()
                e.tensor_scalar(out=sv1[:], in0=sv1[:], scalar1=-1.0,
                                scalar2=1.0, op0=ALU.mult, op1=ALU.add)
                e.drain()
                e.tensor_mul(out=sv4[:], in0=sv4[:], in1=sv1[:])
                e.drain()
                e.tensor_add(out=th, in0=sv3[:], in1=sv4[:])
                e.drain()
                return e.scalar_tensor_tensor(out=sB[:], in0=zt, scalar=th,
                                              in1=zt, op0=ALU.is_ge,
                                              op1=ALU.mult)
            th_steps.append(s_th)

            for cg in range(8):
                gi = len(htr_ev)
                ps = pht[gi % 2]
                hdeps = [s_th]
                if gi >= 2:
                    hdeps.append(htr_ev[gi - 2])

                @k.step("tensor", deps=hdeps)
                def s_htr(e, t=t, cg=cg, ps=ps):
                    last = None
                    for j in range(8):
                        c = cg * 8 + j
                        last = e.transpose(ps[:, ts_(j)],
                                           sB[:, c * 128:(c + 1) * 128],
                                           identb[:])
                    return last

                @k.step("scalar", deps=[s_htr])
                def s_hte(e, t=t, cg=cg, ps=ps, buf=buf):
                    last = None
                    for j in range(8):
                        c = cg * 8 + j
                        last = e.activation(
                            out=z2[buf][:, t * N + c * 128:
                                        t * N + c * 128 + 128],
                            in_=ps[:, ts_(j)], func=AF.Relu)
                    return last
                htr_ev.append(s_hte)
            hid_g.setdefault(g, []).append(htr_ev[-1])

        def emit_outs(g):
            buf = g % 2
            for ds in range(4):
                dep_bs = list(b_steps)
                if len(omm) >= 16:
                    dep_bs.append(omm[-16])

                @k.step("sync", deps=dep_bs, ndma=1)
                def s_bout(e, sd, ds=ds):
                    e.dma_start(out=bsto[ds % 2][0:1, :],
                                in_=boutp_d.ap()[0:1, ds * 512:(ds + 1) * 512]
                                ).then_inc(sd, 16)

                for cq in range(NCH // 4):
                    i = len(omm)
                    depw = [omm[i - 3]] if i >= 3 else []
                    depw = depw + [ag_out2[cq // 8]]

                    @k.step("sync", deps=depw, ndma=2)
                    def s_wo(e, sd, ds=ds, cq=cq, i=i):
                        blk0 = ds * 2
                        for b in range(2):
                            e.dma_start(
                                out=bass.AP(woc[i % 3][:].tensor, b * 256,
                                            [[2048, 128], [512, 4],
                                             [1, 256]]),
                                in_=bass.AP(touta2[cq // 8],
                                            ((blk0 + b) * (N // 2)
                                             + (cq % 8) * 512) * 256,
                                            [[256, 128], [128 * 256, 4],
                                             [1, 256]]),
                            ).then_inc(sd, 16)

                    deps = [s_wo] + hid_g[g]
                    if i >= 1:
                        deps.append(omm[i - 1])
                    if ds > 0:
                        deps.append(ep[(g, ds - 1, gt_ - 1)])
                    elif cq == 0 and g >= 1:
                        deps.append(ep_last[g - 1])

                    @k.step("tensor", deps=deps + [s_bout])
                    def s_om(e, ds=ds, cq=cq, i=i, buf=buf):
                        last = None
                        for j in range(4):
                            c = cq * 4 + j
                            for t in range(gt_):
                                if c == 0:
                                    e.matmul(po[t][:], ones1b[0:1, :],
                                             bsto[ds % 2][0:1, :],
                                             start=True, stop=False)
                                    e.matmul(
                                        po[t][:], identso[:],
                                        x1g2[buf][:, t * D + ds * 512:
                                                  t * D + ds * 512 + 512],
                                        start=False, stop=False)
                                last = e.matmul(
                                    po[t][:],
                                    z2[buf][:, t * N + c * 128:
                                            t * N + c * 128 + 128],
                                    woc[i % 3][:, j * 512:(j + 1) * 512],
                                    start=False, stop=(c == NCH - 1))
                        return last
                    omm.append(s_om)

                for t in range(gt_):
                    evdeps = [omm[-1]]
                    if g >= 1:
                        evdeps.append(grp_res[(g - 1, t)])

                    @k.step("scalar", deps=evdeps)
                    def s_ev(e, ds=ds, t=t):
                        return e.activation(
                            out=tmp4[t][:, ds * 512:(ds + 1) * 512],
                            in_=po[t][:], func=AF.Identity, scale=soutb[:])
                    ep[(g, ds, t)] = s_ev
            ep_last[g] = ep[(g, 3, gt_ - 1)]
            omm_g_last[g] = omm[-1]

        def emit_ln2(g, t):
            buf = g % 2
            col = 6 + t
            tile_evs = [ep[(g, ds, t)] for ds in range(4)]

            @k.step("scalar", deps=tile_evs)
            def s_l2s(e, t=t, buf=buf, col=col):
                e.activation(out=jnk[:], in_=tmp4[t][:], func=AF.Copy,
                             accum_out=cS[:, col:col + 1])
                e.drain()
                return e.activation(out=jnk[:], in_=tmp4[t][:],
                                    func=AF.Square,
                                    accum_out=cQ[:, col:col + 1])

            @k.step("scalar", deps=[s_l2s])
            def s_l2a(e, t=t, buf=buf, col=col):
                e.activation(out=cMU[:, col:col + 1],
                             in_=cS[:, col:col + 1], func=AF.Identity,
                             scale=cbv(CB_ID))
                e.drain()
                e.activation(out=ss1[:], in_=cMU[:, col:col + 1],
                             func=AF.Square)
                e.drain()
                e.activation(out=ss2[:], in_=ss1[:], func=AF.Identity,
                             scale=cbv(CB_MO))
                e.drain()
                e.activation(out=cVR[:, col:col + 1],
                             in_=cQ[:, col:col + 1], func=AF.Identity,
                             scale=cbv(CB_ID), bias=ss2[:])
                e.drain()
                e.activation(out=ss1[:], in_=cVR[:, col:col + 1], func=AF.Ln,
                             bias=cbv(CB_LN))
                e.drain()
                e.activation(out=cRS[:, col:col + 1], in_=ss1[:], func=AF.Exp,
                             scale=cbv(CB_MH))
                e.drain()
                e.activation(out=ss2[:], in_=cMU[:, col:col + 1],
                             func=AF.Identity, scale=cRS[:, col:col + 1])
                e.drain()
                e.activation(out=ss2[:], in_=ss2[:], func=AF.Identity,
                             scale=cbv(CB_MO))
                e.drain()
                return e.activation(out=tmp4[t][:], in_=tmp4[t][:],
                                    func=AF.Identity, bias=ss2[:],
                                    scale=cRS[:, col:col + 1])
            l2a_last[g] = s_l2a

            @k.step("vector", deps=[s_l2a, s_gbb])
            def s_res(e, g=g, t=t, buf=buf):
                xt = x_sb[:, (g * gt_ + t) * D:(g * gt_ + t + 1) * D]
                e.tensor_mul(out=tmp4[t][:], in0=tmp4[t][:],
                             in1=gb2[:])
                e.drain()
                e.tensor_add(out=xt, in0=xt, in1=tmp4[t][:])
                e.drain()
                return e.tensor_add(out=xt, in0=xt, in1=bb2[:])
            grp_res[(g, t)] = s_res
            grp_done[g] = s_res

        for g in range(ng_ + 2):
            if 1 <= g <= ng_:
                emit_th(g - 1, 0)
            if g < ng_:
                emit_ln1(g)
                emit_x1tr(g)
            if g < ng_:
                emit_z(g)
                if g == 0:
                    emit_stats0()
            if 1 <= g <= ng_:
                emit_th(g - 1, 1)
            if 2 <= g <= ng_ + 1:
                emit_ln2(g - 2, 0)
                emit_ln2(g - 2, 1)
            if 1 <= g <= ng_:
                emit_outs(g - 1)

        k.barrier(engines=("sync", "scalar", "vector", "tensor"))
        fs.close()
        return [grp_done[ng_ - 1]]

    ffn_phase("a")

    def attn_phase():
        s_ = contextlib.ExitStack()
        en = s_.enter_context
        rg4 = [[0, 1, 2, 3], [4, 5, 6, 7]]

        pt1 = en(nc.psum_tensor("pt1", [128, 512], F16))
        pt2 = en(nc.psum_tensor("pt2", [128, 512], F16))
        pa = en(nc.psum_tensor("pa", [128, 512], F32))
        pb = en(nc.psum_tensor("pb", [128, 512], F32))
        psc = en(nc.psum_tensor("psc", [128, 512], F32))
        psc2 = en(nc.psum_tensor("psc2", [128, 512], F32))
        ppt = en(nc.psum_tensor("ppt", [128, 512], F16))
        pov = en(nc.psum_tensor("pov", [128, 512], F32))

        AR = 80896
        ar = en(nc.sbuf_tensor("arena", [128, AR], F16))
        arf = ar[:].bitcast(F32)

        def v16(o, n):
            return ar[:, o:o + n]

        def v32(o, n):
            return arf[:, o // 2: o // 2 + n]

        O_x2t = 0
        O_rtmp = 16384
        O_rtm2 = 20480
        O_absw = 22528
        O_aqT = 30720
        O_cqT = 34816
        O_R1 = 38912
        O_R2 = 55296
        O_R3 = 71680
        x2t = v16(O_x2t, 16384)
        qrT = v16(O_x2t, 16384)
        rtmp = v32(O_rtmp, 2048)
        rtm2 = v16(O_rtm2, 2048)
        absw = v16(O_absw, 8192)
        aqT = v16(O_aqT, 4096)
        cqT = v16(O_cqT, 4096)
        g1b = v16(O_R3, 512)
        g2b = v16(O_R3 + 512, 512)
        g3b = v16(O_R3 + 1024, 2048)
        aqb = v16(O_R3 + 3072, 4096)
        ptile = v16(O_R3 + 3072, 512)
        pTt = v16(O_R3 + 3584, 512)
        fCA = v32(O_R3 + 4096, 8)
        fNM = v32(O_R3 + 4128, 8)
        fCO = v32(O_R3 + 4160, 8)
        fRS = v32(O_R3 + 4192, 8)
        mrun = v32(O_R3 + 4224, 8)
        lrun = v32(O_R3 + 4256, 8)

        @k.step("gpsimd", ndma=3)
        def s_gl(e, sd):
            e.dma_start(out=g1b[0:1, :], in_=g1).then_inc(sd, 16)
            e.dma_start(out=g2b[0:1, :], in_=g2).then_inc(sd, 16)
            e.dma_start(out=g3b[0:1, :], in_=g3).then_inc(sd, 16)

        gbc_ev = []
        for buf, nsl in ((g1b, 1), (g2b, 1), (g3b, 4)):
            for sl in range(nsl):
                @k.step("tensor", deps=[s_gl] + ([gbc_ev[-1]] if gbc_ev else []))
                def s_gm(e, buf=buf, sl=sl):
                    return e.matmul(pa[:], ones1b[0:1, :],
                                    buf[0:1, sl * 512:(sl + 1) * 512],
                                    start=True, stop=True)

                @k.step("scalar", deps=[s_gm])
                def s_ge(e, buf=buf, sl=sl):
                    return e.copy(out=buf[:, sl * 512:(sl + 1) * 512],
                                  in_=pa[:])
                gbc_ev.append(s_ge)
        s_gbc = gbc_ev[-1]

        woT = v16(O_R1, 32768)
        wuvb = v16(O_x2t, 8192)

        @k.step("gpsimd", ndma=2)
        def s_awl(e, sd):
            e.dma_start(
                out=woT.rearrange("p (c f) -> p c f", c=DCH),
                in_=bass.AP(wo.tensor, 0, [[D, 128], [128 * D, DCH], [1, D]]),
            ).then_inc(sd, 16)
            e.dma_start(
                out=wuvb.rearrange("p (c f) -> p c f", c=DCH),
                in_=bass.AP(wuv.tensor, 0, [[C, 128], [128 * C, DCH], [1, C]]),
            ).then_inc(sd, 16)

        wo_tr = []
        for idx, (rc, cc) in enumerate(
                [(r, c) for r in range(DCH) for c in range(r, DCH)]):
            deps = [s_awl] if idx < 2 else [wo_tr[idx - 2]]
            pss = pt1 if idx % 2 == 0 else pt2

            @k.step("tensor", deps=deps)
            def s_wt(e, rc=rc, cc=cc, pss=pss):
                i1 = e.transpose(pss[:, 0:128],
                                 woT[:, rc * D + cc * 128: rc * D + cc * 128 + 128],
                                 identb[:])
                if cc != rc:
                    i1 = e.transpose(pss[:, 128:256],
                                     woT[:, cc * D + rc * 128:
                                         cc * D + rc * 128 + 128], identb[:])
                return i1

            @k.step("scalar", deps=[s_wt])
            def s_we(e, rc=rc, cc=cc, pss=pss):
                i1 = e.copy(out=woT[:, cc * D + rc * 128: cc * D + rc * 128 + 128],
                            in_=pss[:, 0:128])
                if cc != rc:
                    i1 = e.copy(out=woT[:, rc * D + cc * 128:
                                        rc * D + cc * 128 + 128],
                                in_=pss[:, 128:256])
                return i1
            wo_tr.append(s_we)
        wo_done = wo_tr[-1]

        absw_ev = []
        for kk2, (cs, es) in enumerate(
                [(a, b) for a in range(4) for b in range(4)]):
            ps = pa if kk2 % 2 == 0 else pb
            deps = [wo_done] if kk2 < 2 else [absw_ev[kk2 - 2]]

            @k.step("tensor", deps=deps)
            def s_am(e, cs=cs, es=es, ps=ps):
                last = None
                for ec in range(DCH):
                    last = e.matmul(
                        ps[:],
                        wuvb[:, ec * C + cs * 128: ec * C + cs * 128 + 128],
                        woT[:, ec * D + es * 512: ec * D + es * 512 + 512],
                        start=(ec == 0), stop=(ec == DCH - 1))
                return last

            @k.step("scalar", deps=[s_am])
            def s_ae(e, cs=cs, es=es, ps=ps):
                return e.copy(out=absw[:, cs * D + es * 512:
                                       cs * D + es * 512 + 512], in_=ps[:])
            absw_ev.append(s_ae)
        absw_done = absw_ev[-1]

        wqb = v16(O_R1, 8192)
        wkb = v16(O_R1 + 8192, 8192)
        Asb = v16(O_R2, 2048)

        @k.step("gpsimd", deps=[absw_done], ndma=2)
        def s_bl(e, sd):
            e.dma_start(
                out=wqb.rearrange("p (c f) -> p c f", c=DCH),
                in_=bass.AP(wuq.tensor, 0, [[C, 128], [128 * C, DCH], [1, C]]),
            ).then_inc(sd, 16)
            e.dma_start(
                out=wkb.rearrange("p (c f) -> p c f", c=DCH),
                in_=bass.AP(wuk.tensor, 0, [[C, 128], [128 * C, DCH], [1, C]]),
            ).then_inc(sd, 16)

        A_ev = []
        for cs in range(4):
            ps = pa if cs % 2 == 0 else pb
            deps = [s_bl] if cs < 2 else [A_ev[cs - 2]]

            @k.step("tensor", deps=deps)
            def s_Am(e, cs=cs, ps=ps):
                last = None
                for ec in range(DCH):
                    last = e.matmul(
                        ps[:],
                        wqb[:, ec * C + cs * 128: ec * C + cs * 128 + 128],
                        wkb[:, ec * C:(ec + 1) * C],
                        start=(ec == 0), stop=(ec == DCH - 1))
                return last

            @k.step("scalar", deps=[s_Am])
            def s_Ae(e, cs=cs, ps=ps):
                return e.copy(out=Asb[:, cs * C:(cs + 1) * C], in_=ps[:])
            A_ev.append(s_Ae)

        wdqn = v16(O_R1, 8192)
        Bsb = v16(O_R2 + 8192, 8192)

        @k.step("gpsimd", deps=[A_ev[-1]], ndma=1)
        def s_dql(e, sd):
            e.dma_start(
                out=wdqn.rearrange("p (c f) -> p c f", c=4),
                in_=bass.AP(wdq.tensor, 0, [[D, 128], [128 * D, 4], [1, D]]),
            ).then_inc(sd, 16)

        B_ev = []
        for ds in range(DCH):
            ps = pa if ds % 2 == 0 else pb
            deps = [s_dql] if ds < 2 else [B_ev[ds - 2]]

            @k.step("tensor", deps=deps)
            def s_Bm(e, ds=ds, ps=ps):
                last = None
                for cc in range(4):
                    last = e.matmul(
                        ps[:],
                        wdqn[:, cc * D + ds * 128: cc * D + ds * 128 + 128],
                        Asb[:, cc * C:(cc + 1) * C],
                        start=(cc == 0), stop=(cc == 3))
                return last

            @k.step("scalar", deps=[s_Bm])
            def s_Be(e, ds=ds, ps=ps):
                return e.activation(out=Bsb[:, ds * C:(ds + 1) * C], in_=ps[:],
                                    func=AF.Identity, scale=cbv(CB_I12))
            B_ev.append(s_Be)
        B_done = B_ev[-1]

        wdqT = v16(O_R1 + 8192, 8192)
        wnat = v16(O_x2t, 8192)
        wdkvT = v16(O_R2, 8192)

        @k.step("gpsimd", deps=[B_done], ndma=1)
        def s_kvl(e, sd):
            e.dma_start(
                out=wnat.rearrange("p (c f) -> p c f", c=4),
                in_=bass.AP(wdkv.tensor, 0, [[D, 128], [128 * D, 4], [1, D]]),
            ).then_inc(sd, 16)

        jobs = []
        for cc in range(4):
            for dc in range(DCH):
                jobs.append((wdqn, cc, dc, D, wdqT, C))
        for cc in range(4):
            for dc in range(DCH):
                jobs.append((wnat, cc, dc, D, wdkvT, C))
        sw_ev = []
        for jg in range(len(jobs) // 4):
            ps = pt1 if jg % 2 == 0 else pt2
            deps = [s_kvl, B_done] if jg < 2 else [sw_ev[jg - 2]]

            @k.step("tensor", deps=deps)
            def s_st(e, jg=jg, ps=ps):
                last = None
                for j in range(4):
                    srcb, rc, cc2, srcw, dst, dn = jobs[jg * 4 + j]
                    last = e.transpose(
                        ps[:, ts_(j)],
                        srcb[:, rc * srcw + cc2 * 128: rc * srcw + cc2 * 128 + 128],
                        identb[:])
                return last

            @k.step("scalar", deps=[s_st])
            def s_se(e, jg=jg, ps=ps):
                last = None
                for j in range(4):
                    srcb, rc, cc2, srcw, dst, dn = jobs[jg * 4 + j]
                    last = e.copy(
                        out=dst[:, cc2 * dn + rc * 128: cc2 * dn + rc * 128 + 128],
                        in_=ps[:, ts_(j)])
                return last
            sw_ev.append(s_se)
        sw_done = sw_ev[-1]

        x2t_ev = []
        ln2_apply = []
        for t in range(TILES):
            @k.step("scalar", deps=([ln2_apply[-1]] if ln2_apply else []))
            def s_st2(e, t=t):
                xt = x_sb[:, t * D:(t + 1) * D]
                e.activation(out=rtm2[:], in_=xt, func=AF.Copy,
                             accum_out=cS[:, t:t + 1])
                e.drain()
                return e.activation(out=rtm2[:], in_=xt, func=AF.Square,
                                    accum_out=cQ[:, t:t + 1])

            @k.step("vector", deps=[s_st2])
            def s_vr2(e, t=t):
                e.tensor_scalar(out=cMU[:, t:t + 1], in0=cS[:, t:t + 1],
                                scalar1=1.0 / D, scalar2=None, op0=ALU.mult)
                e.drain()
                e.tensor_scalar(out=sv1[:], in0=cMU[:, t:t + 1],
                                scalar1=cMU[:, t:t + 1], scalar2=None,
                                op0=ALU.mult)
                e.tensor_scalar(out=cVR[:, t:t + 1], in0=cQ[:, t:t + 1],
                                scalar1=1.0 / D, scalar2=None, op0=ALU.mult)
                e.drain()
                return e.tensor_sub(out=cVR[:, t:t + 1], in0=cVR[:, t:t + 1],
                                    in1=sv1[:])

            @k.step("scalar", deps=[s_vr2])
            def s_ap2(e, t=t):
                xt = x_sb[:, t * D:(t + 1) * D]
                e.activation(out=ss1[:], in_=cVR[:, t:t + 1], func=AF.Ln,
                             bias=cbv(CB_LN))
                e.drain()
                e.activation(out=cRS[:, t:t + 1], in_=ss1[:], func=AF.Exp,
                             scale=cbv(CB_MH))
                e.drain()
                e.activation(out=ss2[:], in_=cMU[:, t:t + 1], func=AF.Identity,
                             scale=cRS[:, t:t + 1])
                e.drain()
                e.activation(out=ss2[:], in_=ss2[:], func=AF.Identity, scale=cbv(CB_MO))
                e.drain()
                return e.activation(out=rtm2[:], in_=xt, func=AF.Identity,
                                    bias=ss2[:], scale=cRS[:, t:t + 1])
            ln2_apply.append(s_ap2)

            for q4 in range(4):
                gi = t * 4 + q4
                ps = pt1 if gi % 2 == 0 else pt2
                deps = [s_ap2, sw_done]
                if gi >= 2:
                    deps.append(x2t_ev[gi - 2])

                @k.step("tensor", deps=deps)
                def s_xt(e, t=t, q4=q4, ps=ps):
                    last = None
                    for j in range(4):
                        dc = q4 * 4 + j
                        last = e.transpose(ps[:, ts_(j)],
                                           rtm2[:, dc * 128:(dc + 1) * 128],
                                           identb[:])
                    return last

                @k.step("scalar", deps=[s_xt])
                def s_xe(e, t=t, q4=q4, ps=ps):
                    last = None
                    for j in range(4):
                        dc = q4 * 4 + j
                        last = e.copy(
                            out=x2t[:, dc * TOK + t * 128: dc * TOK + t * 128 + 128],
                            in_=ps[:, ts_(j)])
                    return last
                x2t_ev.append(s_xe)
        x2t_done = x2t_ev[-1]

        cqb = v16(O_R1, 4096)
        ckvb = v16(O_R1 + 4096, 4096)
        prj_ev = []
        for t in range(TILES):
            for which in range(3):
                kk2 = t * 3 + which
                ps = (pa, pb, pa)[which]
                wsrc = (wdqT, wdkvT, Bsb)[which]
                deps = [x2t_done, sw_done, B_done]
                if kk2 >= 1:
                    deps.append(prj_ev[kk2 - 1])

                @k.step("tensor", deps=deps)
                def s_pm(e, t=t, ps=ps, wsrc=wsrc):
                    last = None
                    for dc in range(DCH):
                        last = e.matmul(
                            ps[:],
                            x2t[:, dc * TOK + t * 128: dc * TOK + t * 128 + 128],
                            wsrc[:, dc * C:(dc + 1) * C],
                            start=(dc == 0), stop=(dc == DCH - 1))
                    return last

                rcol = cRS if which == 0 else cTH
                if which < 2:
                    @k.step("scalar", deps=[s_pm])
                    def s_ps(e, t=t, ps=ps, rcol=rcol):
                        e.activation(out=rtm2[:, 0:C], in_=ps[:],
                                     func=AF.Square, accum_out=ss1[:])
                        e.drain()
                        e.activation(out=ss2[:], in_=ss1[:], func=AF.Ln,
                                     bias=cbv(CB_RMS), scale=cbv(CB_IC))
                        e.drain()
                        return e.activation(out=rcol[:, t:t + 1], in_=ss2[:],
                                            func=AF.Exp, scale=cbv(CB_MH))

                    @k.step("vector", deps=[s_ps])
                    def s_pv(e, t=t, ps=ps, which=which, rcol=rcol):
                        dst = cqb if which == 0 else ckvb
                        gby = g1b if which == 0 else g2b
                        return e.scalar_tensor_tensor(
                            out=dst[:, t * C:(t + 1) * C], in0=ps[:],
                            scalar=rcol[:, t:t + 1], in1=gby[:], op0=ALU.mult,
                            op1=ALU.mult)
                    prj_ev.append(s_pv)
                else:
                    @k.step("vector", deps=[s_pm])
                    def s_pv2(e, t=t, ps=ps):
                        return e.tensor_copy(out=aqb[:, t * C:(t + 1) * C],
                                             in_=ps[:])
                    prj_ev.append(s_pv2)
        prj_done = prj_ev[-1]

        ckvT = v16(O_R2, 4096)
        jobs2 = []
        for t in range(TILES):
            for cc in range(4):
                jobs2.append((cqb, t, cc, cqT))
        for t in range(TILES):
            for cc in range(4):
                jobs2.append((aqb, t, cc, aqT))
        for t in range(TILES):
            for cc in range(4):
                jobs2.append((ckvb, t, cc, ckvT))
        j2_ev = []
        for jg in range(len(jobs2) // 4):
            ps = pt1 if jg % 2 == 0 else pt2
            deps = [prj_done]
            if jg >= 2:
                deps.append(j2_ev[jg - 2])

            @k.step("tensor", deps=deps)
            def s_j2(e, jg=jg, ps=ps):
                last = None
                for j in range(4):
                    srcb, t, cc, dst = jobs2[jg * 4 + j]
                    last = e.transpose(
                        ps[:, ts_(j)],
                        srcb[:, t * C + cc * 128: t * C + cc * 128 + 128],
                        identb[:])
                return last

            @k.step("scalar", deps=[s_j2])
            def s_j2e(e, jg=jg, ps=ps):
                last = None
                for j in range(4):
                    srcb, t, cc, dst = jobs2[jg * 4 + j]
                    last = e.copy(
                        out=dst[:, cc * TOK + t * 128: cc * TOK + t * 128 + 128],
                        in_=ps[:, ts_(j)])
                return last
            j2_ev.append(s_j2e)
        j2_done = j2_ev[-1]

        @k.step("sync", deps=[prj_done], ndma=1)
        def s_ckvst(e, sd):
            e.dma_start(
                out=bass.AP(ckvp, 0, [[C, 128], [128 * C, TILES], [1, C]]),
                in_=ckvb[:].rearrange("p (t f) -> p t f", t=TILES),
            ).then_inc(sd, 16)

        @k.step("sync", deps=[j2_done], ndma=1)
        def s_ckvtst(e, sd):
            e.dma_start(
                out=bass.AP(ckvtp, 0, [[TOK, 128], [128 * TOK, 4], [1, TOK]]),
                in_=ckvT[:].rearrange("p (c f) -> p c f", c=4),
            ).then_inc(sd, 16)

        @k.step("gpsimd", deps=[s_ckvst], cc=True)
        def s_agc(e):
            return e.collective_compute(
                "AllGather", ALU.bypass, replica_groups=rg4,
                ins=[ckvp.ap().opt()], outs=[ckva.ap().opt()])

        @k.step("gpsimd", deps=[s_ckvtst], cc=True)
        def s_agct(e):
            return e.collective_compute(
                "AllGather", ALU.bypass, replica_groups=rg4,
                ins=[ckvtp.ap().opt()], outs=[ckvta.ap().opt()])

        wkrT = v16(O_R1, 32768)
        tcos2 = v32(O_R3 + 7168, 512)
        tsin2 = v32(O_R3 + 8192, 512)
        qtile = v16(O_R3 + 3072, 2048)
        rstg = v16(O_R3 + 5120, 2048)
        krTt = v16(O_rtm2, 2048)

        @k.step("sync", deps=[s_ckvtst], ndma=2)
        def s_tb2(e, sd):
            e.dma_start(out=tcos2.rearrange("p (t f) -> p t f", t=TILES),
                        in_=bass.AP(cosk.tensor, 0,
                                    [[64, 128], [128 * 64, TILES], [1, 64]])
                        ).then_inc(sd, 16)
            e.dma_start(out=tsin2.rearrange("p (t f) -> p t f", t=TILES),
                        in_=bass.AP(sink.tensor, 0,
                                    [[64, 128], [128 * 64, TILES], [1, 64]])
                        ).then_inc(sd, 16)

        @k.step("gpsimd", deps=[j2_done, s_ckvst], ndma=1)
        def s_krl(e, sd):
            e.dma_start(
                out=wkrT.rearrange("p (c f) -> p c f", c=DCH),
                in_=bass.AP(wkr.tensor, 0, [[D, 128], [128 * D, DCH], [1, D]]),
            ).then_inc(sd, 16)

        kr_tr = []
        for idx, (rc, cc) in enumerate(
                [(r, c) for r in range(DCH) for c in range(r, DCH)]):
            deps = [s_krl, j2_done] if idx < 2 else [kr_tr[idx - 2]]
            pss = pt1 if idx % 2 == 0 else pt2

            @k.step("tensor", deps=deps)
            def s_kt(e, rc=rc, cc=cc, pss=pss):
                i1 = e.transpose(pss[:, 0:128],
                                 wkrT[:, rc * D + cc * 128:
                                      rc * D + cc * 128 + 128], identb[:])
                if cc != rc:
                    i1 = e.transpose(pss[:, 128:256],
                                     wkrT[:, cc * D + rc * 128:
                                          cc * D + rc * 128 + 128], identb[:])
                return i1

            @k.step("scalar", deps=[s_kt])
            def s_ke(e, rc=rc, cc=cc, pss=pss):
                i1 = e.copy(out=wkrT[:, cc * D + rc * 128:
                                     cc * D + rc * 128 + 128], in_=pss[:, 0:128])
                if cc != rc:
                    i1 = e.copy(out=wkrT[:, rc * D + cc * 128:
                                         rc * D + cc * 128 + 128],
                                in_=pss[:, 128:256])
                return i1
            kr_tr.append(s_ke)
        wkr_done = kr_tr[-1]

        def rope_emit(e, t, ctab, stab):
            def sv3(apo, extra, dims):
                return bass.AP(apo.tensor, apo.offset + extra,
                               [[apo.ap[0][0], 128]] + dims)
            ev = sv3(rtmp, 0, [[128, DCH], [2, 64]])
            od = sv3(rtmp, 1, [[128, DCH], [2, 64]])
            cv = sv3(ctab, t * 64, [[0, DCH], [1, 64]])
            sv = sv3(stab, t * 64, [[0, DCH], [1, 64]])
            oev = sv3(qtile, 0, [[128, DCH], [2, 64]])
            ood = sv3(qtile, 1, [[128, DCH], [2, 64]])
            e2 = sv3(rstg, 0, [[128, DCH], [2, 64]])
            o2 = sv3(rstg, 1, [[128, DCH], [2, 64]])
            e.tensor_tensor(out=oev, in0=ev, in1=cv, op=ALU.mult)
            e.tensor_tensor(out=ood, in0=od, in1=cv, op=ALU.mult)
            e.tensor_tensor(out=e2, in0=od, in1=sv, op=ALU.mult)
            e.tensor_tensor(out=o2, in0=ev, in1=sv, op=ALU.mult)
            e.drain()
            e.tensor_tensor(out=oev, in0=oev, in1=e2, op=ALU.subtract)
            return e.tensor_tensor(out=ood, in0=ood, in1=o2, op=ALU.add)

        def proj_rope_store(stat_src, w_src, n_k, ctab, stab, dstT, dst_dram,
                            mm_deps, per_tile_store, tbl_dep):
            ev_s = []
            rp_s = []
            tr_s = []
            st_s = []
            for t in range(TILES):
                for es in range(4):
                    kk2 = t * 4 + es
                    ps = pa if kk2 % 2 == 0 else pb
                    deps = list(mm_deps)
                    if kk2 >= 2:
                        deps.append(ev_s[kk2 - 2])

                    @k.step("tensor", deps=deps)
                    def s_m(e, t=t, es=es, ps=ps, stat_src=stat_src,
                            w_src=w_src, n_k=n_k):
                        last = None
                        for ic in range(n_k):
                            last = e.matmul(
                                ps[:],
                                stat_src[:, ic * TOK + t * 128:
                                         ic * TOK + t * 128 + 128],
                                w_src[:, ic * D + es * 512:
                                      ic * D + es * 512 + 512],
                                start=(ic == 0), stop=(ic == n_k - 1))
                        return last

                    rp_dep = [rp_s[t - 1]] if (es == 0 and t > 0) else []

                    @k.step("scalar", deps=[s_m] + rp_dep)
                    def s_e(e, es=es, ps=ps):
                        return e.copy(out=rtmp[:, es * 512:(es + 1) * 512],
                                      in_=ps[:])
                    ev_s.append(s_e)

                dep_r = [ev_s[t * 4 + 3], tbl_dep]
                if t > 0:
                    dep_r.append(tr_s[-1])

                @k.step("vector", deps=dep_r)
                def s_r(e, t=t, ctab=ctab, stab=stab):
                    return rope_emit(e, t, ctab, stab)
                rp_s.append(s_r)

                for q4 in range(4):
                    gi = t * 4 + q4
                    ps = pt1 if gi % 2 == 0 else pt2
                    deps = [s_r]
                    if gi >= 2:
                        deps.append(tr_s[gi - 2])

                    @k.step("tensor", deps=deps)
                    def s_t(e, q4=q4, ps=ps):
                        last = None
                        for j in range(4):
                            dc = q4 * 4 + j
                            last = e.transpose(
                                ps[:, ts_(j)],
                                qtile[:, dc * 128:(dc + 1) * 128], identb[:])
                        return last

                    dep_te = [s_t]
                    if per_tile_store and q4 == 0 and t > 0:
                        dep_te.append(st_s[-1])

                    @k.step("scalar", deps=dep_te)
                    def s_te2(e, t=t, q4=q4, ps=ps, dstT=dstT):
                        last = None
                        for j in range(4):
                            dc = q4 * 4 + j
                            if per_tile_store:
                                o0 = (q4 * 4 + j) * 128
                                last = e.copy(out=krTt[:, o0:o0 + 128],
                                              in_=ps[:, ts_(j)])
                            else:
                                last = e.copy(
                                    out=dstT[:, dc * TOK + t * 128:
                                             dc * TOK + t * 128 + 128],
                                    in_=ps[:, ts_(j)])
                        return last
                    tr_s.append(s_te2)

                if per_tile_store:
                    @k.step("sync", deps=[tr_s[-1]], ndma=1)
                    def s_sto(e, sd, t=t, dst_dram=dst_dram):
                        e.dma_start(
                            out=bass.AP(dst_dram, t * 128,
                                        [[TOK, 128], [128 * TOK, DCH], [1, 128]]),
                            in_=krTt[:].rearrange("p (c f) -> p c f", c=DCH),
                        ).then_inc(sd, 16)
                    st_s.append(s_sto)
            return (st_s[-1] if per_tile_store else tr_s[-1])

        kr_done = proj_rope_store(x2t, wkrT, DCH, tcos2, tsin2, None, krtp,
                                  [wkr_done], True, s_tb2)

        @k.step("gpsimd", deps=[kr_done], cc=True)
        def s_agk(e):
            return e.collective_compute(
                "AllGather", ALU.bypass, replica_groups=rg4,
                ins=[krtp.ap().opt()], outs=[krta.ap().opt()])

        wqrn = v16(O_R1, 8192)
        wqrT = v16(O_R1 + 8192, 8192)
        tcos = v32(O_R2, 512)
        tsin = v32(O_R2 + 1024, 512)

        @k.step("gpsimd", deps=[kr_done], ndma=1)
        def s_qrl(e, sd):
            e.dma_start(
                out=wqrn.rearrange("p (c f) -> p c f", c=DCH),
                in_=bass.AP(wqr.tensor, 0, [[C, 128], [128 * C, DCH], [1, C]]),
            ).then_inc(sd, 16)

        @k.step("sync", deps=[kr_done], ndma=2)
        def s_tb1(e, sd):
            e.dma_start(out=tcos.rearrange("p (t f) -> p t f", t=TILES),
                        in_=bass.AP(cosq.tensor, 0,
                                    [[64, 128], [128 * 64, TILES], [1, 64]])
                        ).then_inc(sd, 16)
            e.dma_start(out=tsin.rearrange("p (t f) -> p t f", t=TILES),
                        in_=bass.AP(sinq.tensor, 0,
                                    [[64, 128], [128 * 64, TILES], [1, 64]])
                        ).then_inc(sd, 16)

        qr_tr = []
        jobs3 = [(dc, cc) for dc in range(DCH) for cc in range(4)]
        for jg in range(len(jobs3) // 4):
            ps = pt1 if jg % 2 == 0 else pt2
            deps = [s_qrl, kr_done] if jg < 2 else [qr_tr[jg - 2]]

            @k.step("tensor", deps=deps)
            def s_q3(e, jg=jg, ps=ps):
                last = None
                for j in range(4):
                    dc, cc2 = jobs3[jg * 4 + j]
                    last = e.transpose(
                        ps[:, ts_(j)],
                        wqrn[:, dc * C + cc2 * 128: dc * C + cc2 * 128 + 128],
                        identb[:])
                return last

            @k.step("scalar", deps=[s_q3])
            def s_q3e(e, jg=jg, ps=ps):
                last = None
                for j in range(4):
                    dc, cc2 = jobs3[jg * 4 + j]
                    last = e.copy(
                        out=wqrT[:, cc2 * D + dc * 128: cc2 * D + dc * 128 + 128],
                        in_=ps[:, ts_(j)])
                return last
            qr_tr.append(s_q3e)
        wqr_done = qr_tr[-1]

        qr_done = proj_rope_store(cqT, wqrT, 4, tcos, tsin, qrT, None,
                                  [wqr_done], False, s_tb1)

        kb_kr = [v16(O_R1, 8192), v16(O_R1 + 8192, 8192)]
        kb_ct = [v16(O_R2, 2048), v16(O_R2 + 2048, 2048)]
        kb_cv = [v16(O_R2 + 4096, 2048), v16(O_R2 + 6144, 2048)]
        ost = [v32(O_R2 + 8192 + i * 1024, 512) for i in range(TILES)]

        @k.step("vector", deps=[qr_done, s_agk])
        def s_finit(e):
            e.memset(mrun[:], -1e30)
            e.memset(lrun[:], 0.0)
            last = None
            for t in range(TILES):
                last = e.memset(ost[t][:], 0.0)
            return last

        kdma = []
        smx_v2 = []
        smx_s1 = []
        pt_ev = []
        ou_mm = []
        pend = []

        def emit_pt_ov(kk2, kt, t, ps, s_s1h):
            deps = [s_s1h]
            if kk2 >= 1:
                deps.append(pt_ev[kk2 - 1])

            @k.step("tensor", deps=deps)
            def s_pt(e):
                last = None
                for j in range(4):
                    last = e.transpose(ppt[:, ts_(j)],
                                       ptile[:, j * 128:(j + 1) * 128],
                                       identb[:])
                return last

            dep_e = [s_pt]
            if kk2 >= 1:
                dep_e.append(ou_mm[kk2 - 1])

            @k.step("scalar", deps=dep_e)
            def s_pte(e):
                return e.copy(out=pTt[:], in_=ppt[:])
            pt_ev.append(s_pte)

            dep_o = [s_pte]
            if kk2 >= 1:
                dep_o.append(smx_v2[kk2 - 1])

            @k.step("tensor", deps=dep_o)
            def s_ov(e, kt=kt):
                last = None
                for cc in range(4):
                    last = e.matmul(
                        pov[:],
                        pTt[:, cc * 128:(cc + 1) * 128],
                        kb_cv[kt % 2][:, cc * C:(cc + 1) * C],
                        start=(cc == 0), stop=(cc == 3))
                return last
            ou_mm.append(s_ov)

            @k.step("vector", deps=[s_ov, s_s1h])
            def s_v2(e, t=t):
                e.scalar_tensor_tensor(out=lrun[:, t:t + 1],
                                       in0=lrun[:, t:t + 1],
                                       scalar=fCO[:, t:t + 1],
                                       in1=fRS[:, t:t + 1], op0=ALU.mult,
                                       op1=ALU.add)
                return e.scalar_tensor_tensor(out=ost[t][:], in0=ost[t][:],
                                              scalar=fCO[:, t:t + 1],
                                              in1=pov[:], op0=ALU.mult,
                                              op1=ALU.add)
            smx_v2.append(s_v2)
            return s_pt

        last_pt = [None]
        for kt in range(KTS):
            deps = [s_agk, s_agc, s_agct, s_finit, qr_done]
            if kt >= 2:
                deps.append(smx_v2[(kt - 2) * TILES + TILES - 1])

            @k.step("sync", deps=deps, ndma=3)
            def s_kd(e, sd, kt=kt):
                b = kt // 2
                c0 = (kt % 2) * 512
                e.dma_start(
                    out=kb_kr[kt % 2].rearrange("p (c f) -> p c f", c=DCH),
                    in_=bass.AP(krta, (b * D) * TOK + c0,
                                [[TOK, 128], [128 * TOK, DCH], [1, 512]]),
                ).then_inc(sd, 16)
                e.dma_start(
                    out=kb_ct[kt % 2].rearrange("p (c f) -> p c f", c=4),
                    in_=bass.AP(ckvta, (b * C) * TOK + c0,
                                [[TOK, 128], [128 * TOK, 4], [1, 512]]),
                ).then_inc(sd, 16)
                e.dma_start(
                    out=kb_cv[kt % 2].rearrange("p (c f) -> p c f", c=4),
                    in_=bass.AP(ckva, kt * 512 * C,
                                [[C, 128], [128 * C, 4], [1, C]]),
                ).then_inc(sd, 16)
            kdma.append(s_kd)

            for t in range(TILES):
                kk2 = kt * TILES + t
                ps = psc if kk2 % 2 == 0 else psc2
                deps = [s_kd]
                if kk2 >= 2:
                    deps.append(smx_s1[kk2 - 2])

                @k.step("tensor", deps=deps)
                def s_sc2(e, kt=kt, t=t, ps=ps):
                    last = None
                    for cc in range(4):
                        last = e.matmul(
                            ps[:],
                            aqT[:, cc * TOK + t * 128: cc * TOK + t * 128 + 128],
                            kb_ct[kt % 2][:, cc * 512:(cc + 1) * 512],
                            start=(cc == 0), stop=False)
                    for ec in range(DCH):
                        last = e.matmul(
                            ps[:],
                            qrT[:, ec * TOK + t * 128: ec * TOK + t * 128 + 128],
                            kb_kr[kt % 2][:, ec * 512:(ec + 1) * 512],
                            start=False, stop=(ec == DCH - 1))
                    return last

                @k.step("vector", deps=[s_sc2])
                def s_v1(e, t=t, ps=ps):
                    e.reduce_max(out=sv1[:], in_=ps[:], axis=AX.X)
                    e.drain()
                    e.tensor_max(out=sv2[:], in0=sv1[:], in1=mrun[:, t:t + 1])
                    e.drain()
                    e.tensor_sub(out=fCA[:, t:t + 1], in0=mrun[:, t:t + 1],
                                 in1=sv2[:])
                    e.drain()
                    e.tensor_copy(out=mrun[:, t:t + 1], in_=sv2[:])
                    return e.tensor_scalar(out=fNM[:, t:t + 1], in0=sv2[:],
                                           scalar1=-1.0, scalar2=None,
                                           op0=ALU.mult)

                if pend:
                    last_pt[0] = emit_pt_ov(*pend.pop())
                dep_s1 = [s_v1]
                if last_pt[0] is not None:
                    dep_s1.append(last_pt[0])

                @k.step("scalar", deps=dep_s1)
                def s_s1(e, t=t, ps=ps):
                    e.activation(out=fCO[:, t:t + 1], in_=fCA[:, t:t + 1],
                                 func=AF.Exp)
                    return e.activation(out=ptile[:], in_=ps[:], func=AF.Exp,
                                        bias=fNM[:, t:t + 1],
                                        accum_out=fRS[:, t:t + 1])
                smx_s1.append(s_s1)
                pend.append((kk2, kt, t, ps, s_s1))

        emit_pt_ov(*pend.pop())

        u_done = []
        um_ev = []
        for t in range(TILES):
            @k.step("vector", deps=[smx_v2[-1]] + ([u_done[-1]] if u_done else []))
            def s_uo(e, t=t):
                e.reciprocal(out=sv1[:], in_=lrun[:, t:t + 1])
                e.drain()
                return e.tensor_scalar(out=ptile[:, 0:C], in0=ost[t][:],
                                       scalar1=sv1[:], scalar2=None,
                                       op0=ALU.mult)

            @k.step("tensor", deps=[s_uo] + ([um_ev[-1]] if um_ev else []))
            def s_ut(e):
                last = None
                for j in range(4):
                    last = e.transpose(ppt[:, ts_(j)],
                                       ptile[:, j * 128:(j + 1) * 128],
                                       identb[:])
                return last

            @k.step("scalar", deps=[s_ut])
            def s_ue(e):
                return e.copy(out=pTt[:], in_=ppt[:])

            um_s = []
            for es in range(4):
                ps = psc if es % 2 == 0 else psc2
                deps = [s_ue]
                if es >= 2:
                    deps.append(um_ev[-2 + es - 2 + 1] if False else um_ev[-1])

                @k.step("tensor", deps=deps)
                def s_um(e, es=es, ps=ps):
                    last = None
                    for cc in range(4):
                        last = e.matmul(
                            ps[:],
                            pTt[:, cc * 128:(cc + 1) * 128],
                            absw[:, cc * D + es * 512: cc * D + es * 512 + 512],
                            start=(cc == 0), stop=(cc == 3))
                    return last
                um_s.append(s_um)

                @k.step("scalar", deps=[s_um])
                def s_ume(e, es=es, ps=ps):
                    return e.copy(out=rtmp[:, es * 512:(es + 1) * 512],
                                  in_=ps[:])
                um_ev.append(s_ume)

            @k.step("scalar", deps=[um_ev[-1]])
            def s_us(e, t=t):
                e.activation(out=rtm2[:], in_=rtmp[:], func=AF.Square,
                             accum_out=ss1[:])
                e.drain()
                e.activation(out=ss2[:], in_=ss1[:], func=AF.Ln, bias=cbv(CB_RMS),
                             scale=cbv(CB_ID))
                e.drain()
                return e.activation(out=cRS[:, t:t + 1], in_=ss2[:],
                                    func=AF.Exp, scale=cbv(CB_MH))

            @k.step("vector", deps=[s_us, s_gbc])
            def s_ux(e, t=t):
                xt = x_sb[:, t * D:(t + 1) * D]
                e.scalar_tensor_tensor(out=rtm2[:], in0=rtmp[:],
                                       scalar=cRS[:, t:t + 1], in1=g3b[:],
                                       op0=ALU.mult, op1=ALU.mult)
                e.drain()
                return e.tensor_add(out=xt, in0=xt, in1=rtm2[:])
            u_done.append(s_ux)

        k.barrier()
        s_.close()

    attn_phase()
    ffn_phase("b")

    @k.step("gpsimd", ndma=1)
    def s_out(e, sd):
        e.dma_start(
            out=bass.AP(out.tensor, 0, [[D, 128], [128 * D, TILES], [1, D]]),
            in_=x_sb[:].rearrange("p (t d) -> p t d", t=TILES),
        ).then_inc(sd, 16)

    @k.step("gpsimd", deps=[s_out])
    def s_fin(e):
        return e.memset(svrow[0:1, 0:1], 0.0)

    k.lower(nc)
    top.close()
    return nc, tapd


def _host_inputs(inputs):
    h = np.ascontiguousarray(
        np.asarray(inputs["L0_attn_h"]).reshape(NCORES * TOK, D))
    wsyn = np.ascontiguousarray(np.asarray(inputs["L0_ffn_W_syn"]))
    wout = np.ascontiguousarray(np.asarray(inputs["L0_ffn_W_out"]))
    f32 = np.float32
    i = np.arange(64, dtype=np.float64)
    freqs = 1.0 / (10000.0 ** (2.0 * i / 128.0))
    ident = np.eye(128).astype(np.float16)
    ones1 = np.ones((1, 128), np.float16)
    onesf = np.ones((128, 1), f32)
    iota = np.broadcast_to(np.arange(8, dtype=f32), (128, 8)).copy()
    maps = []
    for c in range(NCORES):
        pos = (c % 4) * TOK + np.arange(TOK, dtype=np.float64)
        ang = pos[:, None] * freqs[None, :]
        cos = np.cos(ang).astype(f32)
        sin = np.sin(ang).astype(f32)
        m = {
            "h": np.ascontiguousarray(h[c * TOK:(c + 1) * TOK]),
            "wsyn": np.ascontiguousarray(wsyn[c * TOK:(c + 1) * TOK]),
            "wout": np.ascontiguousarray(wout[c * 256:(c + 1) * 256]),
            "wdkv": np.asarray(inputs["L0_attn_w_d_kv"]),
            "wdq": np.asarray(inputs["L0_attn_w_d_q"]),
            "wuk": np.asarray(inputs["L0_attn_w_u_k"]),
            "wuv": np.asarray(inputs["L0_attn_w_u_v"]),
            "wuq": np.asarray(inputs["L0_attn_w_u_q"]),
            "wkr": np.asarray(inputs["L0_attn_w_kr"]),
            "wqr": np.asarray(inputs["L0_attn_w_qr"]),
            "wo": np.asarray(inputs["L0_attn_w_o"]),
            "g1": np.asarray(inputs["L0_attn_g1"]).reshape(1, C),
            "g2": np.asarray(inputs["L0_attn_g2"]).reshape(1, C),
            "g3": np.asarray(inputs["L0_attn_g3"]).reshape(1, D),
            "bsyn": np.asarray(inputs["L0_ffn_b_syn"]).reshape(1, N),
            "bout": np.asarray(inputs["L0_ffn_b_out"]).reshape(1, D),
            "gamma": np.asarray(inputs["L0_ffn_gamma"]).reshape(1, D),
            "beta": np.asarray(inputs["L0_ffn_beta"]).reshape(1, D),
            "cosq": (cos / 12.0).astype(f32),
            "sinq": (sin / 12.0).astype(f32),
            "cosk": cos,
            "sink": sin,
            "ident": ident,
            "ones1": ones1,
            "onesf": onesf,
            "iota8": iota,
        }
        maps.append(m)
    return maps


def kernel(**inputs):
    nc, _ = build()
    maps = _host_inputs(inputs)
    res = run_bass_kernel_spmd(nc, maps, core_ids=list(range(NCORES)))
    outs = np.concatenate([r["out"] for r in res.results], axis=0)
    return np.ascontiguousarray(outs.reshape(2, SEQ, D).astype(np.float32))


# version 15
# speedup vs baseline: 1.1531x; 1.0181x over previous
import contextlib

import numpy as np
import ml_dtypes
import concourse.bass as bass
import concourse.mybir as mybir
from concourse.bass_utils import run_bass_kernel_spmd

F32 = mybir.dt.float32
F16 = mybir.dt.float16
AF = mybir.ActivationFunctionType
ALU = mybir.AluOpType
AX = mybir.AxisListType

NCORES = 8
TOK = 1024
D = 2048
N = 8192
C = 512
SEQ = 4096
TILES = 8
GT = 4
NGRP = 2
HSLB = 32
DCH = 16
NCH = 64
KTS = 8
KKT = 819.0
CBIG = 100.0
LN_EPS = 1e-5
RMS_EPS = 1.1920929e-07

ENGS = ("sync", "scalar", "vector", "tensor", "gpsimd")


def ts_(i, n=128):
    return slice(i * n, (i + 1) * n)


class Step:
    __slots__ = ("eng", "kind", "seq", "slot", "async_")


NPOOL = 12


class K:

    def __init__(self):
        self.items = []
        self.ce = {e: 0 for e in ENGS}
        self.dcount = {e: 0 for e in ENGS}
        self.dval = {(e, i): 0 for e in ENGS for i in range(NPOOL)}
        self.dhist = {e: [] for e in ENGS}
        self.lastb = ()

    def step(self, eng, deps=(), ndma=0, cc=False):
        def deco(fn):
            st = Step()
            st.eng = eng
            st.async_ = cc
            if ndma:
                st.kind = "d"
                idx = self.dcount[eng]
                self.dcount[eng] += 1
                st.slot = idx % NPOOL
                self.dval[(eng, st.slot)] += ndma * 16
                st.seq = self.dval[(eng, st.slot)]
                extra = ()
                if idx >= NPOOL:
                    extra = (self.dhist[eng][idx - NPOOL],)
                self.dhist[eng].append(st)
                self.items.append((st, fn,
                                   tuple(deps) + extra + tuple(self.lastb)))
            else:
                st.kind = "c"
                st.slot = -1
                self.ce[eng] += 1
                st.seq = self.ce[eng]
                self.items.append((st, fn, tuple(deps) + tuple(self.lastb)))
            return st
        return deco

    def barrier(self, engines=None):
        engines = tuple(engines) if engines is not None else ENGS
        lasts = {}
        for st, fn, deps in self.items:
            if st.eng in engines:
                lasts[(st.eng, st.kind == "d", st.slot)] = st
        deps = tuple(lasts.values())
        outs = []
        for e in engines:
            st = Step()
            st.eng = e
            st.kind = "b"
            st.slot = -1
            st.async_ = False
            self.ce[e] += 1
            st.seq = self.ce[e]
            self.items.append((st, None, deps))
            outs.append(st)
        self.lastb = tuple(outs)
        return outs

    def lower(self, nc):
        with contextlib.ExitStack() as ctx:
            sems = {}
            for e in ENGS:
                sems["se_" + e] = ctx.enter_context(nc.semaphore("se_" + e))
                for i in range(NPOOL):
                    sems[f"sd_{e}_{i}"] = ctx.enter_context(
                        nc.semaphore(f"sd_{e}_{i}"))
            block = ctx.enter_context(nc.Block())
            streams = {e: [] for e in ENGS}
            for st, fn, deps in self.items:
                streams[st.eng].append((st, fn, deps))

            def mk(e):
                def run(eng):
                    mark = {}
                    for st, fn, deps in streams[e]:
                        need_drain = False
                        for d in deps:
                            if d.kind == "d":
                                sem, val = f"sd_{d.eng}_{d.slot}", d.seq
                            else:
                                if d.eng == e and not getattr(d, "async_", False):
                                    need_drain = True
                                    continue
                                sem, val = "se_" + d.eng, d.seq
                            if mark.get(sem, 0) >= val:
                                continue
                            mark[sem] = val
                            eng.wait_ge(sems[sem], val)
                        if need_drain:
                            eng.drain()
                        if st.kind == "b":
                            eng.sem_inc(sems["se_" + e], 1)
                        elif st.kind == "d":
                            fn(eng, sems[f"sd_{e}_{st.slot}"])
                        else:
                            last = fn(eng)
                            last.then_inc(sems["se_" + e], 1)
                return run
            block.sync(mk("sync"))
            block.scalar(mk("scalar"))
            block.vector(mk("vector"))
            block.tensor(mk("tensor"))
            block.gpsimd(mk("gpsimd"))


def build(taps=()):
    nc = bass.Bass("TRN2", target_bir_lowering=False)
    k = K()
    tapd = {}

    def din(name, shape, dt=F32):
        return nc.dram_tensor(name, shape, dt, kind="ExternalInput").ap()

    h = din("h", [TOK, D])
    wsyn = din("wsyn", [TOK, D])
    wout = din("wout", [256, N])
    wdkv = din("wdkv", [C, D])
    wdq = din("wdq", [C, D])
    wuk = din("wuk", [D, C])
    wuv = din("wuv", [D, C])
    wuq = din("wuq", [D, C])
    wkr = din("wkr", [D, D])
    wqr = din("wqr", [D, C])
    wo = din("wo", [D, D])
    g1 = din("g1", [1, C])
    g2 = din("g2", [1, C])
    g3 = din("g3", [1, D])
    bsyn = din("bsyn", [1, N])
    bout = din("bout", [1, D])
    gamma = din("gamma", [1, D])
    beta = din("beta", [1, D])
    cosq = din("cosq", [TOK, 64])
    sinq = din("sinq", [TOK, 64])
    cosk = din("cosk", [TOK, 64])
    sink = din("sink", [TOK, 64])
    ident = din("ident", [128, 128], F16)
    ones1 = din("ones1", [1, 128], F16)
    onesf = din("onesf", [128, 1])
    iota8 = din("iota8", [128, 8])
    out = nc.dram_tensor("out", [TOK, D], F32, kind="ExternalOutput").ap()

    sqp = nc.dram_tensor("sqp", [1, 2], F32)
    sqa = nc.dram_tensor("sqa", [1, 2], F32)
    tsynp4 = [nc.dram_tensor(f"tsynp{j}", [D, 256], F16) for j in range(4)]
    tsyna4 = [nc.dram_tensor(f"tsyna{j}", [NCORES * D, 256], F16)
              for j in range(4)]
    toutp2 = [nc.dram_tensor(f"toutp{j}", [N // 2, 256], F16)
              for j in range(2)]
    touta2 = [nc.dram_tensor(f"touta{j}", [NCORES * N // 2, 256], F16)
              for j in range(2)]
    bsynp_d = nc.dram_tensor("bsynp_d", [1, N], F16)
    boutp_d = nc.dram_tensor("boutp_d", [1, D], F16)
    krtp = nc.dram_tensor("krtp", [D, TOK], F16)
    krta = nc.dram_tensor("krta", [4 * D, TOK], F16)
    ckvp = nc.dram_tensor("ckvp", [TOK, C], F16)
    ckva = nc.dram_tensor("ckva", [4 * TOK, C], F16)
    ckvtp = nc.dram_tensor("ckvtp", [C, TOK], F16)
    ckvta = nc.dram_tensor("ckvta", [4 * C, TOK], F16)

    top = contextlib.ExitStack()
    e_ = top.enter_context
    x_sb = e_(nc.sbuf_tensor("x_sb", [128, TILES * D], F16))
    identb = e_(nc.sbuf_tensor("identb", [128, 128], F16))
    ones1b = e_(nc.sbuf_tensor("ones1b", [1, 128], F16))
    onesfb = e_(nc.sbuf_tensor("onesfb", [128, 1], F32))
    iotab = e_(nc.sbuf_tensor("iotab", [128, 8], F32))
    gb2 = e_(nc.sbuf_tensor("gb2", [128, D], F16))
    bb2 = e_(nc.sbuf_tensor("bb2", [128, D], F16))
    ssynb = e_(nc.sbuf_tensor("ssynb", [128, 1], F32))
    soutb = e_(nc.sbuf_tensor("soutb", [128, 1], F32))
    hsynb = e_(nc.sbuf_tensor("hsynb", [128, 1], F32))
    houtb = e_(nc.sbuf_tensor("houtb", [128, 1], F32))
    sv1 = e_(nc.sbuf_tensor("sv1", [128, 1], F32))
    sv2 = e_(nc.sbuf_tensor("sv2", [128, 1], F32))
    sv3 = e_(nc.sbuf_tensor("sv3", [128, 1], F32))
    sv4 = e_(nc.sbuf_tensor("sv4", [128, 1], F32))
    ss1 = e_(nc.sbuf_tensor("ss1", [128, 1], F32))
    ss2 = e_(nc.sbuf_tensor("ss2", [128, 1], F32))
    cS = e_(nc.sbuf_tensor("cS", [128, 8], F32))
    cQ = e_(nc.sbuf_tensor("cQ", [128, 8], F32))
    cMU = e_(nc.sbuf_tensor("cMU", [128, 8], F32))
    cVR = e_(nc.sbuf_tensor("cVR", [128, 8], F32))
    cRS = e_(nc.sbuf_tensor("cRS", [128, 8], F32))
    cTH = e_(nc.sbuf_tensor("cTH", [128, 8], F32))
    u8 = e_(nc.sbuf_tensor("u8", [128, 8], F32))
    d8 = e_(nc.sbuf_tensor("d8", [128, 8], F32))
    hot = e_(nc.sbuf_tensor("hot", [128, 8], F32))
    cb = e_(nc.sbuf_tensor("cb", [128, 12], F32))
    svrow = e_(nc.sbuf_tensor("svrow", [1, 4], F32))
    identso = e_(nc.sbuf_tensor("identso", [128, 128], F16))
    u16 = e_(nc.sbuf_tensor("u16", [128, 16], F32))
    d16 = e_(nc.sbuf_tensor("d16", [128, 16], F32))
    onesr32 = e_(nc.sbuf_tensor("onesr32", [1, 128], F32))
    (CB_LN, CB_RMS, CB_MH, CB_PH, CB_MO, CB_I12, CB_IC, CB_ID,
     CB_IN) = range(9)

    def cbv(i):
        return cb[:, i:i + 1]

    @k.step("sync", ndma=4)
    def s_c0(e, sd):
        e.dma_start(out=identb[:], in_=ident).then_inc(sd, 16)
        e.dma_start(out=ones1b[:], in_=ones1).then_inc(sd, 16)
        e.dma_start(out=onesfb[:], in_=onesf).then_inc(sd, 16)
        e.dma_start(out=iotab[:], in_=iota8).then_inc(sd, 16)

    @k.step("vector")
    def s_cb(e):
        e.memset(cb[:, 0:1], LN_EPS)
        e.memset(cb[:, 1:2], RMS_EPS)
        e.memset(cb[:, 2:3], -0.5)
        e.memset(cb[:, 3:4], 0.5)
        e.memset(cb[:, 4:5], -1.0)
        e.memset(onesr32[:], 1.0)
        e.memset(cb[:, 5:6], 1.0 / 12.0)
        e.memset(cb[:, 6:7], 1.0 / C)
        e.memset(cb[:, 7:8], 1.0 / D)
        return e.memset(cb[:, 8:9], 1.0 / N)

    @k.step("gpsimd", ndma=3)
    def s_x0(e, sd):
        e.dma_start(
            out=x_sb[:].rearrange("p (t d) -> p t d", t=TILES),
            in_=bass.AP(h.tensor, 0, [[D, 128], [128 * D, TILES], [1, D]]),
        ).then_inc(sd, 16)
        e.dma_start(out=gb2[0:1, :], in_=gamma).then_inc(sd, 16)
        e.dma_start(out=bb2[0:1, :], in_=beta).then_inc(sd, 16)

    @k.step("vector", deps=[s_x0])
    def s_gb(e):
        e.tensor_scalar(out=gb2[0:1, :], in0=gb2[0:1, :], scalar1=0.5,
                        scalar2=None, op0=ALU.mult)
        return e.tensor_scalar(out=bb2[0:1, :], in0=bb2[0:1, :], scalar1=0.5,
                               scalar2=None, op0=ALU.mult)


    qs = contextlib.ExitStack()
    wf = qs.enter_context(nc.sbuf_tensor("wf", [128, 16384], F32))
    tq = qs.enter_context(nc.sbuf_tensor("tq", [128, 16384], F16))
    tq2 = qs.enter_context(nc.sbuf_tensor("tq2", [128, 16384], F16))
    tq3 = qs.enter_context(nc.sbuf_tensor("tq3", [128, 16384], F16))
    sqsb = qs.enter_context(nc.sbuf_tensor("sqsb", [1, 2], F32))
    qacc = qs.enter_context(nc.sbuf_tensor("qacc", [128, 2], F32))
    pq = qs.enter_context(nc.psum_tensor("pq", [128, 1024], F16))
    pq2 = qs.enter_context(nc.psum_tensor("pq2", [128, 1024], F16))
    psml = qs.enter_context(nc.psum_tensor("psml", [1, 2], F32))
    pbc = qs.enter_context(nc.psum_tensor("pbc", [128, 512], F32))

    bcast_ev = []
    for which, buf in ((0, gb2), (1, bb2)):
        for sl in range(4):
            @k.step("tensor", deps=[s_gb, s_c0] + ([bcast_ev[-1]] if bcast_ev else []))
            def s_bcm(e, buf=buf, sl=sl):
                return e.matmul(pbc[:], ones1b[0:1, :],
                                buf[0:1, sl * 512:(sl + 1) * 512],
                                start=True, stop=True)

            @k.step("scalar", deps=[s_bcm])
            def s_bce(e, buf=buf, sl=sl):
                return e.copy(out=buf[:, sl * 512:(sl + 1) * 512], in_=pbc[:])
            bcast_ev.append(s_bce)
    s_gbb = bcast_ev[-1]

    @k.step("sync", ndma=1)
    def s_ldsyn(e, sd):
        e.dma_start(
            out=wf[:].rearrange("p (c f) -> p c f", c=8),
            in_=bass.AP(wsyn.tensor, 0, [[D, 128], [128 * D, 8], [1, D]]),
        ).then_inc(sd, 16)

    @k.step("gpsimd", ndma=1)
    def s_ldoutf16(e, sd):
        e.dma_start(
            out=tq[:].rearrange("p (c f) -> p c f", c=2),
            in_=bass.AP(wout.tensor, 0, [[N, 128], [128 * N, 2], [1, N]]),
        ).then_inc(sd, 16)

    @k.step("scalar", deps=[s_ldsyn])
    def s_abs0(e):
        return e.activation(out=tq2[:], in_=wf[:], func=AF.Abs,
                            accum_out=qacc[:, 0:1])

    @k.step("scalar", deps=[s_ldoutf16])
    def s_abs1(e):
        return e.activation(out=tq3[:], in_=tq[:], func=AF.Abs,
                            accum_out=qacc[:, 1:2])

    @k.step("tensor", deps=[s_abs0, s_abs1, s_c0])
    def s_sqmm(e):
        e.matmul(psml[0:1, 0:1], qacc[:, 0:1], onesfb[:], start=True, stop=True)
        return e.matmul(psml[0:1, 1:2], qacc[:, 1:2], onesfb[:], start=True,
                        stop=True)

    @k.step("scalar", deps=[s_sqmm])
    def s_sqev(e):
        return e.copy(out=sqsb[:], in_=psml[:])

    @k.step("gpsimd", deps=[s_sqev], ndma=1)
    def s_sqst(e, sd):
        e.dma_start(out=sqp.ap(), in_=sqsb[:]).then_inc(sd, 16)

    @k.step("gpsimd", deps=[s_sqst], cc=True)
    def s_ar(e):
        return e.collective_compute(
            "AllReduce", ALU.add, replica_groups=[list(range(NCORES))],
            ins=[sqp.ap().opt()], outs=[sqa.ap().opt()])

    @k.step("gpsimd", deps=[s_ar], ndma=1)
    def s_sqld(e, sd):
        e.dma_start(out=sqsb[:], in_=sqa.ap()).then_inc(sd, 16)

    nelem = float(N) * float(D)

    @k.step("vector", deps=[s_sqld])
    def s_scale(e):
        e.tensor_scalar(out=svrow[0:1, 0:1], in0=sqsb[0:1, 0:1],
                        scalar1=1.0 / nelem, scalar2=None, op0=ALU.mult)
        e.tensor_scalar(out=svrow[0:1, 1:2], in0=sqsb[0:1, 1:2],
                        scalar1=1.0 / nelem, scalar2=None, op0=ALU.mult)
        e.drain()
        e.tensor_scalar(out=svrow[0:1, 2:3], in0=svrow[0:1, 0:1], scalar1=0.5,
                        scalar2=0.5e-8, op0=ALU.mult, op1=ALU.add)
        return e.tensor_scalar(out=svrow[0:1, 3:4], in0=svrow[0:1, 1:2],
                               scalar1=0.5, scalar2=0.5e-8, op0=ALU.mult,
                               op1=ALU.add)

    @k.step("tensor", deps=[s_scale, s_cb])
    def s_scm(e):
        return e.matmul(pbc[:, 0:4], onesr32[0:1, :], svrow[0:1, :],
                        start=True, stop=True)

    @k.step("scalar", deps=[s_scm])
    def s_scbc(e):
        e.copy(out=ssynb[:], in_=pbc[:, 0:1])
        e.copy(out=soutb[:], in_=pbc[:, 1:2])
        e.copy(out=hsynb[:], in_=pbc[:, 2:3])
        return e.copy(out=houtb[:], in_=pbc[:, 3:4])

    @k.step("vector", deps=[s_scbc, s_c0])
    def s_iso(e):
        e.reciprocal(out=sv4[:], in_=soutb[:])
        e.drain()
        e.tensor_scalar(out=identso[:], in0=identb[:], scalar1=sv4[:],
                        scalar2=None, op0=ALU.mult)
        e.drain()
        return e.tensor_copy(out=sv4[:], in_=soutb[:])

    def quant_mat(which, dep_load, extra_q_deps=()):
        nchk, ncol = (8, D) if which == 0 else (2, N)
        half_sb = hsynb if which == 0 else houtb

        @k.step("vector", deps=[dep_load, s_scbc] + list(extra_q_deps))
        def s_q(e, which=which, half_sb=half_sb):
            e.tensor_scalar(out=sv1[:], in0=half_sb[:], scalar1=-1.0,
                            scalar2=None, op0=ALU.mult)
            src = wf[:] if which == 0 else tq[:]
            e.tensor_scalar(out=tq2[:], in0=src, scalar1=half_sb[:],
                            scalar2=None, op0=ALU.is_ge)
            e.drain()
            e.tensor_scalar(out=tq3[:], in0=src, scalar1=sv1[:],
                            scalar2=None, op0=ALU.is_le)
            e.drain()
            return e.tensor_sub(out=tq[:], in0=tq2[:], in1=tq3[:])

        ccn = ncol // 128
        ngrp = (nchk * ccn) // 8
        tr_steps = []
        te_steps = []
        for g in range(ngrp):
            ps = pq if g % 2 == 0 else pq2
            deps = [s_q, s_c0] if g < 2 else [te_steps[g - 2]]

            @k.step("tensor", deps=deps)
            def s_tr(e, g=g, ps=ps, ccn=ccn, ncol=ncol):
                last = None
                for j in range(8):
                    blk = g * 8 + j
                    rc, cc = blk // ccn, blk % ccn
                    last = e.transpose(
                        ps[:, ts_(j)],
                        tq[:, rc * ncol + cc * 128: rc * ncol + cc * 128 + 128],
                        identb[:])
                return last
            tr_steps.append(s_tr)

            @k.step("scalar", deps=[s_tr])
            def s_te(e, g=g, ps=ps, ccn=ccn, nchk=nchk):
                last = None
                for j in range(8):
                    blk = g * 8 + j
                    rc, cc = blk // ccn, blk % ccn
                    o0 = cc * (128 * nchk) + rc * 128
                    last = e.copy(out=tq2[:, o0:o0 + 128], in_=ps[:, ts_(j)])
                return last
            te_steps.append(s_te)

        wps = []
        if which == 0:
            for j in range(4):
                @k.step("sync", deps=[te_steps[min(4 * j + 3,
                                                   len(te_steps) - 1)]],
                        ndma=1)
                def s_wpj(e, sd, j=j):
                    e.dma_start(
                        out=bass.AP(tsynp4[j], 0,
                                    [[256, 128], [128 * 256, 16], [1, 256]]),
                        in_=bass.AP(tq2[:].tensor, j * 256,
                                    [[16384, 128], [1024, 16], [1, 256]]),
                    ).then_inc(sd, 16)
                wps.append(s_wpj)
        else:
            for j in range(2):
                @k.step("sync", deps=[te_steps[-1]], ndma=1)
                def s_wpj(e, sd, j=j):
                    e.dma_start(
                        out=bass.AP(toutp2[j], 0,
                                    [[256, 128], [128 * 256, 32], [1, 256]]),
                        in_=bass.AP(tq2[:].tensor, j * 32 * 256,
                                    [[16384, 128], [256, 32], [1, 256]]),
                    ).then_inc(sd, 16)
                wps.append(s_wpj)
        return wps, tr_steps[-1]

    wp_out, tr_out = quant_mat(1, s_ldoutf16)
    wp_syn, tr_syn = quant_mat(0, s_ldsyn,
                               extra_q_deps=wp_out + [tr_out])

    ag_syn4 = []
    for j in range(4):
        @k.step("gpsimd", deps=[wp_syn[j]], cc=True)
        def s_agsj(e, j=j):
            return e.collective_compute(
                "AllGather", ALU.bypass, replica_groups=[list(range(NCORES))],
                ins=[tsynp4[j].ap().opt()], outs=[tsyna4[j].ap().opt()])
        ag_syn4.append(s_agsj)
    ag_out2 = []
    for j in range(2):
        @k.step("gpsimd", deps=[wp_out[j]], cc=True)
        def s_agoj(e, j=j):
            return e.collective_compute(
                "AllGather", ALU.bypass, replica_groups=[list(range(NCORES))],
                ins=[toutp2[j].ap().opt()], outs=[touta2[j].ap().opt()])
        ag_out2.append(s_agoj)

    b_steps = []
    for which in range(2):
        bsrc = bsyn if which == 0 else bout
        bdst = bsynp_d if which == 0 else boutp_d
        s_sb = ssynb if which == 0 else soutb
        nn = N if which == 0 else D
        brow_ap = tq3[0:1, 0:nn].bitcast(F32)[0:1, 0:nn] if False else None
        brow = tq3[:].bitcast(F32)

        @k.step("sync", deps=wp_syn + [tr_syn], ndma=1)
        def s_bld(e, sd, bsrc=bsrc, nn=nn, brow=brow):
            e.dma_start(out=brow[0:1, 0:nn], in_=bsrc).then_inc(sd, 16)

        @k.step("vector", deps=[s_bld, s_scbc] + list(b_steps))
        def s_bdiv(e, nn=nn, s_sb=s_sb, brow=brow):
            e.reciprocal(out=sv1[0:1, 0:1], in_=s_sb[0:1, 0:1])
            e.drain()
            return e.tensor_scalar(out=tq[0:1, 0:nn], in0=brow[0:1, 0:nn],
                                   scalar1=sv1[0:1, 0:1], scalar2=None,
                                   op0=ALU.mult)

        @k.step("sync", deps=[s_bdiv], ndma=1)
        def s_bst(e, sd, bdst=bdst, nn=nn):
            e.dma_start(out=bdst.ap()[0:1, 0:nn],
                        in_=tq[0:1, 0:nn]).then_inc(sd, 16)
        b_steps.append(s_bst)

    k.barrier(engines=("sync", "scalar", "vector", "tensor"))
    qs.close()

    def ffn_phase(tag):
        ng_, gt_ = 4, 2
        fs = contextlib.ExitStack()
        en = fs.enter_context
        z2 = [en(nc.sbuf_tensor(f"z{tag}{i}", [128, gt_ * N], F16))
              for i in range(2)]
        x1g2 = [en(nc.sbuf_tensor(f"x1g{tag}{i}", [128, gt_ * D], F16))
                for i in range(2)]
        x1t2 = [en(nc.sbuf_tensor(f"x1t{tag}{i}", [128, DCH * 256], F16))
                for i in range(2)]
        wsl = [en(nc.sbuf_tensor(f"wsl{tag}{i}", [128, DCH * 256], F16))
               for i in range(2)]
        woc = [en(nc.sbuf_tensor(f"woc{tag}{i}", [128, 2048], F16))
               for i in range(3)]
        sB = en(nc.sbuf_tensor(f"sB{tag}", [128, N], F16))
        ysc = en(nc.sbuf_tensor(f"ysc{tag}", [128, N // 2], F16))
        jnk = en(nc.sbuf_tensor(f"jnk{tag}", [128, D], F16))
        tmp4 = [en(nc.sbuf_tensor(f"tmp4{tag}{t}", [128, D], F16))
                for t in range(gt_)]
        bst = [en(nc.sbuf_tensor(f"bst{tag}{i}", [1, 256], F16))
               for i in range(2)]
        bsto = [en(nc.sbuf_tensor(f"bsto{tag}{i}", [1, 512], F16))
                for i in range(2)]
        pz = [en(nc.psum_tensor(f"pz{tag}{i}", [128, 256], F32))
              for i in range(2)]
        pht = [en(nc.psum_tensor(f"pht{tag}{i}", [128, 1024], F16))
               for i in range(2)]
        px = [en(nc.psum_tensor(f"px{tag}{i}", [128, 1024], F16))
              for i in range(2)]
        po = [en(nc.psum_tensor(f"po{tag}{i}", [128, 512], F32))
              for i in range(gt_)]

        zmm, zev, omm = [], [], []
        th_steps, htr_ev, x1tev = [], [], []
        ln_apply_g, x1t_done_g, zmm_g_last, omm_g_last = {}, {}, {}, {}
        hid_g, ep, ep_last, l2a_last, grp_done = {}, {}, {}, {}, {}
        grp_res = {}
        stat_sig = [None]

        def emit_ln1(g):
            buf = g % 2
            apl = []
            for t in range(gt_):
                col = buf * 2 + t
                deps = [] if g >= 2 else [s_x0, s_gbb]

                @k.step("scalar", deps=deps)
                def s_st(e, g=g, t=t, col=col):
                    xt = x_sb[:, (g * gt_ + t) * D:(g * gt_ + t + 1) * D]
                    e.activation(out=jnk[:], in_=xt, func=AF.Copy,
                                 accum_out=cS[:, col:col + 1])
                    e.drain()
                    return e.activation(out=jnk[:], in_=xt, func=AF.Square,
                                        accum_out=cQ[:, col:col + 1])

                apdeps = [s_st] + ([omm_g_last[g - 2]] if g >= 2 else [])

                @k.step("scalar", deps=apdeps)
                def s_ap(e, g=g, t=t, buf=buf, col=col):
                    xt = x_sb[:, (g * gt_ + t) * D:(g * gt_ + t + 1) * D]
                    e.activation(out=cMU[:, col:col + 1],
                                 in_=cS[:, col:col + 1], func=AF.Identity,
                                 scale=cbv(CB_ID))
                    e.drain()
                    e.activation(out=ss1[:], in_=cMU[:, col:col + 1],
                                 func=AF.Square)
                    e.drain()
                    e.activation(out=ss2[:], in_=ss1[:], func=AF.Identity,
                                 scale=cbv(CB_MO))
                    e.drain()
                    e.activation(out=cVR[:, col:col + 1],
                                 in_=cQ[:, col:col + 1], func=AF.Identity,
                                 scale=cbv(CB_ID), bias=ss2[:])
                    e.drain()
                    e.activation(out=ss1[:], in_=cVR[:, col:col + 1],
                                 func=AF.Ln, bias=cbv(CB_LN))
                    e.drain()
                    e.activation(out=cRS[:, col:col + 1], in_=ss1[:],
                                 func=AF.Exp, scale=cbv(CB_MH))
                    e.drain()
                    e.activation(out=ss2[:], in_=cMU[:, col:col + 1],
                                 func=AF.Identity, scale=cRS[:, col:col + 1])
                    e.drain()
                    e.activation(out=ss2[:], in_=ss2[:], func=AF.Identity,
                                 scale=cbv(CB_MO))
                    e.drain()
                    return e.activation(out=x1g2[buf][:, t * D:(t + 1) * D],
                                        in_=xt, func=AF.Identity, bias=ss2[:],
                                        scale=cRS[:, col:col + 1])
                apl.append(s_ap)
            ln_apply_g[g] = apl

        def emit_x1tr(g):
            buf = g % 2
            for t in range(gt_):
                for q8 in range(2):
                    gi = len(x1tev)
                    ps = px[gi % 2]
                    deps = [ln_apply_g[g][t]]
                    if gi >= 2:
                        deps.append(x1tev[gi - 2])
                    if g >= 2 and t == 0 and q8 == 0:
                        deps.append(zmm_g_last[g - 2])

                    @k.step("tensor", deps=deps)
                    def s_tr(e, g=g, t=t, q8=q8, ps=ps, buf=buf):
                        last = None
                        for j in range(8):
                            dc = q8 * 8 + j
                            last = e.transpose(
                                ps[:, ts_(j)],
                                x1g2[buf][:, t * D + dc * 128:
                                          t * D + dc * 128 + 128],
                                identb[:])
                        return last

                    @k.step("scalar", deps=[s_tr])
                    def s_tev(e, t=t, q8=q8, ps=ps, buf=buf):
                        last = None
                        for j in range(8):
                            dc = q8 * 8 + j
                            last = e.copy(
                                out=x1t2[buf][:, dc * 256 + t * 128:
                                              dc * 256 + t * 128 + 128],
                                in_=ps[:, ts_(j)])
                        return last
                    x1tev.append(s_tev)
            x1t_done_g[g] = x1tev[-1]

        def emit_z(g):
            buf = g % 2
            hs_seq = [p * 4 + j for j in range(4) for p in range(8)]
            for li, hs in enumerate(hs_seq):
                idx = g * HSLB + li
                depw = [zmm[(idx - 2) * gt_ + gt_ - 1]] if idx >= 2 else []
                depw = depw + [ag_syn4[hs % 4]]

                @k.step("sync", deps=depw + [b_steps[0]], ndma=2)
                def s_w(e, sd, hs=hs, idx=idx):
                    e.dma_start(
                        out=wsl[idx % 2][:].rearrange("p (c f) -> p c f",
                                                      c=DCH),
                        in_=bass.AP(tsyna4[hs % 4], ((hs // 4) * D) * 256,
                                    [[256, 128], [128 * 256, DCH], [1, 256]]),
                    ).then_inc(sd, 16)
                    e.dma_start(out=bst[idx % 2][0:1, :],
                                in_=bsynp_d.ap()[0:1, hs * 256:(hs + 1) * 256]
                                ).then_inc(sd, 16)
                s_b = s_w

                for t in range(gt_):
                    kk = idx * gt_ + t
                    psz = pz[kk % 2]
                    deps = [s_w, s_b, x1t_done_g[g]]
                    if kk >= 2:
                        deps.append(zev[kk - 2])

                    @k.step("tensor", deps=deps)
                    def s_zm(e, hs=hs, t=t, psz=psz, idx=idx, buf=buf):
                        e.matmul(psz[:], ones1b[0:1, :], bst[idx % 2][0:1, :],
                                 start=True, stop=False)
                        last = None
                        for dc in range(DCH):
                            last = e.matmul(
                                psz[:],
                                x1t2[buf][:, dc * 256 + t * 128:
                                          dc * 256 + t * 128 + 128],
                                wsl[idx % 2][:, dc * 256:(dc + 1) * 256],
                                start=False, stop=(dc == DCH - 1))
                        return last
                    zmm.append(s_zm)

                    zdeps = [s_zm]
                    if li == 0 and t == 0 and g >= 2:
                        zdeps.append(omm_g_last[g - 2])

                    @k.step("scalar", deps=zdeps)
                    def s_ze(e, hs=hs, t=t, psz=psz, buf=buf):
                        return e.activation(
                            out=z2[buf][:, t * N + hs * 256:
                                        t * N + hs * 256 + 256],
                            in_=psz[:], func=AF.Identity, scale=ssynb[:])
                    zev.append(s_ze)
            zmm_g_last[g] = zmm[-1]

        def emit_stats0():
            z_done = [zev[HSLB * gt_ - 1]]

            @k.step("scalar", deps=z_done)
            def s_zst(e):
                zt = z2[0][:, 0:N]
                e.activation(out=sB[:], in_=zt, func=AF.Copy,
                             accum_out=cS[:, 4:5])
                e.drain()
                return e.activation(out=sB[:], in_=zt, func=AF.Square,
                                    accum_out=cQ[:, 4:5])

            @k.step("scalar", deps=[s_zst])
            def s_sig(e):
                e.activation(out=cMU[:, 4:5], in_=cS[:, 4:5],
                             func=AF.Identity, scale=cbv(CB_IN))
                e.drain()
                e.activation(out=ss1[:], in_=cMU[:, 4:5], func=AF.Square)
                e.drain()
                e.activation(out=ss2[:], in_=ss1[:], func=AF.Identity,
                             scale=cbv(CB_MO))
                e.drain()
                e.activation(out=cVR[:, 4:5], in_=cQ[:, 4:5],
                             func=AF.Identity, scale=cbv(CB_IN), bias=ss2[:])
                e.drain()
                e.activation(out=ss1[:], in_=cVR[:, 4:5], func=AF.Ln)
                e.drain()
                return e.activation(out=cRS[:, 4:5], in_=ss1[:],
                                    func=AF.Exp, scale=cbv(CB_PH))
            stat_sig[0] = s_sig

        def emit_th(g, t):
            buf = g % 2
            gt = g * gt_ + t
            z_done = [zev[(g + 1) * HSLB * gt_ - 1]]
            if th_steps:
                sig_dep = th_steps[-1]
            else:
                sig_dep = stat_sig[0]
            deps = [sig_dep] + z_done
            if htr_ev:
                deps.append(htr_ev[-1])

            @k.step("vector", deps=deps)
            def s_th(e, g=g, t=t, buf=buf, gt=gt):
                zt = z2[buf][:, t * N:(t + 1) * N]
                th = cTH[:, 4 + gt % 2:5 + gt % 2]
                sig = cRS[:, 4:5]
                if gt == 0:
                    mu = cMU[:, 4:5]
                    e.tensor_scalar(out=sv1[:], in0=sig, scalar1=1.2815516,
                                    scalar2=None, op0=ALU.mult)
                    e.drain()
                    e.tensor_add(out=th, in0=mu, in1=sv1[:])
                else:
                    e.tensor_copy(out=th, in_=cTH[:, 4 + (gt - 1) % 2:
                                                  5 + (gt - 1) % 2])
                e.drain()
                e.tensor_scalar(out=sB[:], in0=zt, scalar1=th,
                                scalar2=None, op0=ALU.is_ge, op1=ALU.add,
                                accum_out=sv1[:])
                e.drain()
                e.tensor_copy(out=sv2[:], in_=th)
                e.tensor_scalar(out=sv3[:], in0=sv1[:], scalar1=-KKT,
                                scalar2=1.0 / 1437.7, op0=ALU.add,
                                op1=ALU.mult)
                e.drain()
                e.tensor_mul(out=sv3[:], in0=sv3[:], in1=sig)
                e.drain()
                e.tensor_add(out=th, in0=th, in1=sv3[:])
                e.drain()
                e.tensor_scalar(out=sB[:], in0=zt, scalar1=th,
                                scalar2=None, op0=ALU.is_ge, op1=ALU.add,
                                accum_out=sv4[:])
                e.drain()
                e.tensor_sub(out=sv3[:], in0=th, in1=sv2[:])
                e.drain()
                e.tensor_sub(out=sv2[:], in0=sv4[:], in1=sv1[:])
                e.drain()
                e.tensor_scalar(out=sv1[:], in0=sv2[:], scalar1=sv2[:],
                                scalar2=0.25, op0=ALU.mult, op1=ALU.add)
                e.drain()
                e.reciprocal(out=sv1[:], in_=sv1[:])
                e.drain()
                e.tensor_mul(out=sv3[:], in0=sv3[:], in1=sv2[:])
                e.drain()
                e.tensor_mul(out=sv3[:], in0=sv3[:], in1=sv1[:])
                e.drain()
                e.tensor_scalar(out=sv1[:], in0=sig, scalar1=-0.01,
                                scalar2=None, op0=ALU.mult)
                e.drain()
                e.tensor_max(out=sv3[:], in0=sv3[:], in1=sv1[:])
                e.drain()
                e.tensor_scalar(out=sv1[:], in0=sig, scalar1=-1e-5,
                                scalar2=None, op0=ALU.mult)
                e.drain()
                e.tensor_tensor(out=sv3[:], in0=sv3[:], in1=sv1[:],
                                op=ALU.min)
                e.tensor_scalar(out=sv4[:], in0=sv4[:], scalar1=-KKT,
                                scalar2=None, op0=ALU.add)
                e.drain()
                e.tensor_mul(out=sv4[:], in0=sv4[:], in1=sv3[:])
                e.drain()
                e.tensor_sub(out=th, in0=th, in1=sv4[:])
                e.drain()
                e.tensor_scalar(out=sB[:], in0=zt, scalar1=th,
                                scalar2=None, op0=ALU.is_ge, op1=ALU.add,
                                accum_out=sv4[:])
                e.drain()
                e.tensor_scalar(out=sv4[:], in0=sv4[:], scalar1=-KKT,
                                scalar2=None, op0=ALU.add)
                e.drain()
                e.tensor_mul(out=sv4[:], in0=sv4[:], in1=sv3[:])
                e.drain()
                e.tensor_sub(out=th, in0=th, in1=sv4[:])
                e.drain()
                e.tensor_scalar(out=sB[:], in0=zt, scalar1=th,
                                scalar2=None, op0=ALU.is_lt, op1=ALU.add,
                                accum_out=sv1[:])
                e.drain()
                xsc = x1t2[buf][:]
                H = N // 2
                for ci in range(2):
                    mh = sB[:, ci * H:(ci + 1) * H]
                    zh = zt[:, ci * H:(ci + 1) * H]
                    e.scalar_tensor_tensor(out=ysc[:], in0=mh, scalar=-CBIG,
                                           in1=zh, op0=ALU.mult,
                                           op1=ALU.subtract)
                    e.drain()
                    e.max(out=u16[:, 8 * ci:8 * ci + 8], in_=ysc[:])
                    e.tensor_scalar(out=xsc, in0=ysc[:], scalar1=-1.0,
                                    scalar2=-CBIG, op0=ALU.mult, op1=ALU.add)
                    e.drain()
                    e.max(out=d16[:, 8 * ci:8 * ci + 8], in_=xsc)
                e.drain()
                e.max(out=u8[:], in_=u16[:])
                e.max(out=d8[:], in_=d16[:])
                e.drain()
                e.tensor_scalar(out=sv2[:], in0=sv1[:], scalar1=-1.0,
                                scalar2=float(N) - KKT, op0=ALU.mult,
                                op1=ALU.add)
                e.drain()
                e.tensor_scalar(out=sv3[:], in0=sv2[:], scalar1=0.0,
                                scalar2=7.0, op0=ALU.max, op1=ALU.min)
                e.drain()
                e.tensor_scalar(out=hot[:], in0=iotab[:], scalar1=sv3[:],
                                scalar2=None, op0=ALU.is_equal)
                e.drain()
                e.tensor_mul(out=hot[:], in0=hot[:], in1=u8[:])
                e.drain()
                e.reduce_sum(out=sv3[:], in_=hot[:], axis=AX.X, negate=True)
                e.tensor_scalar(out=sv4[:], in0=sv2[:], scalar1=-1.0,
                                scalar2=-1.0, op0=ALU.mult, op1=ALU.add)
                e.drain()
                e.tensor_scalar(out=sv4[:], in0=sv4[:], scalar1=0.0,
                                scalar2=7.0, op0=ALU.max, op1=ALU.min)
                e.drain()
                e.tensor_scalar(out=hot[:], in0=iotab[:], scalar1=sv4[:],
                                scalar2=None, op0=ALU.is_equal)
                e.drain()
                e.tensor_mul(out=hot[:], in0=hot[:], in1=d8[:])
                e.drain()
                e.reduce_sum(out=sv4[:], in_=hot[:], axis=AX.X)
                e.tensor_scalar(out=sv1[:], in0=sv2[:], scalar1=0.0,
                                scalar2=None, op0=ALU.is_ge)
                e.drain()
                e.tensor_mul(out=sv3[:], in0=sv3[:], in1=sv1[:])
                e.drain()
                e.tensor_scalar(out=sv1[:], in0=sv1[:], scalar1=-1.0,
                                scalar2=1.0, op0=ALU.mult, op1=ALU.add)
                e.drain()
                e.tensor_mul(out=sv4[:], in0=sv4[:], in1=sv1[:])
                e.drain()
                e.tensor_add(out=th, in0=sv3[:], in1=sv4[:])
                e.drain()
                return e.scalar_tensor_tensor(out=sB[:], in0=zt, scalar=th,
                                              in1=zt, op0=ALU.is_ge,
                                              op1=ALU.mult)
            th_steps.append(s_th)

            for cg in range(8):
                gi = len(htr_ev)
                ps = pht[gi % 2]
                hdeps = [s_th]
                if gi >= 2:
                    hdeps.append(htr_ev[gi - 2])

                @k.step("tensor", deps=hdeps)
                def s_htr(e, t=t, cg=cg, ps=ps):
                    last = None
                    for j in range(8):
                        c = cg * 8 + j
                        last = e.transpose(ps[:, ts_(j)],
                                           sB[:, c * 128:(c + 1) * 128],
                                           identb[:])
                    return last

                @k.step("scalar", deps=[s_htr])
                def s_hte(e, t=t, cg=cg, ps=ps, buf=buf):
                    last = None
                    for j in range(8):
                        c = cg * 8 + j
                        last = e.activation(
                            out=z2[buf][:, t * N + c * 128:
                                        t * N + c * 128 + 128],
                            in_=ps[:, ts_(j)], func=AF.Relu)
                    return last
                htr_ev.append(s_hte)
            hid_g.setdefault(g, []).append(htr_ev[-1])

        def emit_outs(g):
            buf = g % 2
            for ds in range(4):
                dep_bs = list(b_steps)
                if len(omm) >= 16:
                    dep_bs.append(omm[-16])

                @k.step("sync", deps=dep_bs, ndma=1)
                def s_bout(e, sd, ds=ds):
                    e.dma_start(out=bsto[ds % 2][0:1, :],
                                in_=boutp_d.ap()[0:1, ds * 512:(ds + 1) * 512]
                                ).then_inc(sd, 16)

                for cq in range(NCH // 4):
                    i = len(omm)
                    depw = [omm[i - 3]] if i >= 3 else []
                    depw = depw + [ag_out2[cq // 8]]

                    @k.step("sync", deps=depw, ndma=2)
                    def s_wo(e, sd, ds=ds, cq=cq, i=i):
                        blk0 = ds * 2
                        for b in range(2):
                            e.dma_start(
                                out=bass.AP(woc[i % 3][:].tensor, b * 256,
                                            [[2048, 128], [512, 4],
                                             [1, 256]]),
                                in_=bass.AP(touta2[cq // 8],
                                            ((blk0 + b) * (N // 2)
                                             + (cq % 8) * 512) * 256,
                                            [[256, 128], [128 * 256, 4],
                                             [1, 256]]),
                            ).then_inc(sd, 16)

                    deps = [s_wo] + hid_g[g]
                    if i >= 1:
                        deps.append(omm[i - 1])
                    if ds > 0:
                        deps.append(ep[(g, ds - 1, gt_ - 1)])
                    elif cq == 0 and g >= 1:
                        deps.append(ep_last[g - 1])

                    @k.step("tensor", deps=deps + [s_bout])
                    def s_om(e, ds=ds, cq=cq, i=i, buf=buf):
                        last = None
                        for j in range(4):
                            c = cq * 4 + j
                            for t in range(gt_):
                                if c == 0:
                                    e.matmul(po[t][:], ones1b[0:1, :],
                                             bsto[ds % 2][0:1, :],
                                             start=True, stop=False)
                                    e.matmul(
                                        po[t][:], identso[:],
                                        x1g2[buf][:, t * D + ds * 512:
                                                  t * D + ds * 512 + 512],
                                        start=False, stop=False)
                                last = e.matmul(
                                    po[t][:],
                                    z2[buf][:, t * N + c * 128:
                                            t * N + c * 128 + 128],
                                    woc[i % 3][:, j * 512:(j + 1) * 512],
                                    start=False, stop=(c == NCH - 1))
                        return last
                    omm.append(s_om)

                for t in range(gt_):
                    evdeps = [omm[-1]]
                    if g >= 1:
                        evdeps.append(grp_res[(g - 1, t)])

                    @k.step("scalar", deps=evdeps)
                    def s_ev(e, ds=ds, t=t):
                        return e.activation(
                            out=tmp4[t][:, ds * 512:(ds + 1) * 512],
                            in_=po[t][:], func=AF.Identity, scale=soutb[:])
                    ep[(g, ds, t)] = s_ev
            ep_last[g] = ep[(g, 3, gt_ - 1)]
            omm_g_last[g] = omm[-1]

        def emit_ln2(g, t):
            buf = g % 2
            col = 6 + t
            tile_evs = [ep[(g, ds, t)] for ds in range(4)]

            @k.step("scalar", deps=tile_evs)
            def s_l2s(e, t=t, buf=buf, col=col):
                e.activation(out=jnk[:], in_=tmp4[t][:], func=AF.Copy,
                             accum_out=cS[:, col:col + 1])
                e.drain()
                return e.activation(out=jnk[:], in_=tmp4[t][:],
                                    func=AF.Square,
                                    accum_out=cQ[:, col:col + 1])

            @k.step("scalar", deps=[s_l2s])
            def s_l2a(e, t=t, buf=buf, col=col):
                e.activation(out=cMU[:, col:col + 1],
                             in_=cS[:, col:col + 1], func=AF.Identity,
                             scale=cbv(CB_ID))
                e.drain()
                e.activation(out=ss1[:], in_=cMU[:, col:col + 1],
                             func=AF.Square)
                e.drain()
                e.activation(out=ss2[:], in_=ss1[:], func=AF.Identity,
                             scale=cbv(CB_MO))
                e.drain()
                e.activation(out=cVR[:, col:col + 1],
                             in_=cQ[:, col:col + 1], func=AF.Identity,
                             scale=cbv(CB_ID), bias=ss2[:])
                e.drain()
                e.activation(out=ss1[:], in_=cVR[:, col:col + 1], func=AF.Ln,
                             bias=cbv(CB_LN))
                e.drain()
                e.activation(out=cRS[:, col:col + 1], in_=ss1[:], func=AF.Exp,
                             scale=cbv(CB_MH))
                e.drain()
                e.activation(out=ss2[:], in_=cMU[:, col:col + 1],
                             func=AF.Identity, scale=cRS[:, col:col + 1])
                e.drain()
                e.activation(out=ss2[:], in_=ss2[:], func=AF.Identity,
                             scale=cbv(CB_MO))
                e.drain()
                return e.activation(out=tmp4[t][:], in_=tmp4[t][:],
                                    func=AF.Identity, bias=ss2[:],
                                    scale=cRS[:, col:col + 1])
            l2a_last[g] = s_l2a

            @k.step("vector", deps=[s_l2a, s_gbb])
            def s_res(e, g=g, t=t, buf=buf):
                xt = x_sb[:, (g * gt_ + t) * D:(g * gt_ + t + 1) * D]
                e.tensor_mul(out=tmp4[t][:], in0=tmp4[t][:],
                             in1=gb2[:])
                e.drain()
                e.tensor_add(out=xt, in0=xt, in1=tmp4[t][:])
                e.drain()
                return e.tensor_add(out=xt, in0=xt, in1=bb2[:])
            grp_res[(g, t)] = s_res
            grp_done[g] = s_res

        for g in range(ng_ + 2):
            if 1 <= g <= ng_:
                emit_th(g - 1, 0)
            if g < ng_:
                emit_ln1(g)
                emit_x1tr(g)
            if g < ng_:
                emit_z(g)
                if g == 0:
                    emit_stats0()
            if 1 <= g <= ng_:
                emit_th(g - 1, 1)
            if 2 <= g <= ng_ + 1:
                emit_ln2(g - 2, 0)
                emit_ln2(g - 2, 1)
            if 1 <= g <= ng_:
                emit_outs(g - 1)

        k.barrier(engines=("sync", "scalar", "vector", "tensor"))
        fs.close()
        return [grp_done[ng_ - 1]]

    ffn_phase("a")

    def attn_phase():
        s_ = contextlib.ExitStack()
        en = s_.enter_context
        rg4 = [[0, 1, 2, 3], [4, 5, 6, 7]]

        pt1 = en(nc.psum_tensor("pt1", [128, 1024], F16))
        pt2 = en(nc.psum_tensor("pt2", [128, 1024], F16))
        pa = en(nc.psum_tensor("pa", [128, 512], F32))
        pb = en(nc.psum_tensor("pb", [128, 512], F32))
        psc = en(nc.psum_tensor("psc", [128, 512], F32))
        psc2 = en(nc.psum_tensor("psc2", [128, 512], F32))
        ppt = en(nc.psum_tensor("ppt", [128, 512], F16))
        pov = en(nc.psum_tensor("pov", [128, 512], F32))

        AR = 80896
        ar = en(nc.sbuf_tensor("arena", [128, AR], F16))
        arf = ar[:].bitcast(F32)

        def v16(o, n):
            return ar[:, o:o + n]

        def v32(o, n):
            return arf[:, o // 2: o // 2 + n]

        O_x2t = 0
        O_rtmp = 16384
        O_rtm2 = 20480
        O_absw = 22528
        O_aqT = 30720
        O_cqT = 34816
        O_R1 = 38912
        O_R2 = 55296
        O_R3 = 71680
        x2t = v16(O_x2t, 16384)
        qrT = v16(O_x2t, 16384)
        rtmp = v32(O_rtmp, 2048)
        rtm2 = v16(O_rtm2, 2048)
        absw = v16(O_absw, 8192)
        aqT = v16(O_aqT, 4096)
        cqT = v16(O_cqT, 4096)
        g1b = v16(O_R3, 512)
        g2b = v16(O_R3 + 512, 512)
        g3b = v16(O_R3 + 1024, 2048)
        aqb = v16(O_R3 + 3072, 4096)
        ptile = v16(O_R3 + 3072, 512)
        pTt = v16(O_R3 + 3584, 512)
        fCA = v32(O_R3 + 4096, 8)
        fNM = v32(O_R3 + 4128, 8)
        fCO = v32(O_R3 + 4160, 8)
        fRS = v32(O_R3 + 4192, 8)
        mrun = v32(O_R3 + 4224, 8)
        lrun = v32(O_R3 + 4256, 8)

        @k.step("gpsimd", ndma=3)
        def s_gl(e, sd):
            e.dma_start(out=g1b[0:1, :], in_=g1).then_inc(sd, 16)
            e.dma_start(out=g2b[0:1, :], in_=g2).then_inc(sd, 16)
            e.dma_start(out=g3b[0:1, :], in_=g3).then_inc(sd, 16)

        gbc_ev = []
        for buf, nsl in ((g1b, 1), (g2b, 1), (g3b, 4)):
            for sl in range(nsl):
                @k.step("tensor", deps=[s_gl] + ([gbc_ev[-1]] if gbc_ev else []))
                def s_gm(e, buf=buf, sl=sl):
                    return e.matmul(pa[:], ones1b[0:1, :],
                                    buf[0:1, sl * 512:(sl + 1) * 512],
                                    start=True, stop=True)

                @k.step("scalar", deps=[s_gm])
                def s_ge(e, buf=buf, sl=sl):
                    return e.copy(out=buf[:, sl * 512:(sl + 1) * 512],
                                  in_=pa[:])
                gbc_ev.append(s_ge)
        s_gbc = gbc_ev[-1]

        woT = v16(O_R1, 32768)
        wuvb = v16(O_x2t, 8192)

        @k.step("gpsimd", ndma=2)
        def s_awl(e, sd):
            e.dma_start(
                out=woT.rearrange("p (c f) -> p c f", c=DCH),
                in_=bass.AP(wo.tensor, 0, [[D, 128], [128 * D, DCH], [1, D]]),
            ).then_inc(sd, 16)
            e.dma_start(
                out=wuvb.rearrange("p (c f) -> p c f", c=DCH),
                in_=bass.AP(wuv.tensor, 0, [[C, 128], [128 * C, DCH], [1, C]]),
            ).then_inc(sd, 16)

        wo_tr = []
        wo_pairs = [(r, c) for r in range(DCH) for c in range(r, DCH)]
        for gidx in range(0, len(wo_pairs), 4):
            chunk = wo_pairs[gidx:gidx + 4]
            idx = gidx // 4
            deps = [s_awl] if idx < 2 else [wo_tr[idx - 2]]
            pss = pt1 if idx % 2 == 0 else pt2

            @k.step("tensor", deps=deps)
            def s_wt(e, chunk=chunk, pss=pss):
                i1 = None
                sl = 0
                for rc, cc in chunk:
                    i1 = e.transpose(
                        pss[:, sl * 128:sl * 128 + 128],
                        woT[:, rc * D + cc * 128: rc * D + cc * 128 + 128],
                        identb[:])
                    sl += 1
                    if cc != rc:
                        i1 = e.transpose(
                            pss[:, sl * 128:sl * 128 + 128],
                            woT[:, cc * D + rc * 128: cc * D + rc * 128 + 128],
                            identb[:])
                        sl += 1
                return i1

            @k.step("scalar", deps=[s_wt])
            def s_we(e, chunk=chunk, pss=pss):
                i1 = None
                sl = 0
                for rc, cc in chunk:
                    i1 = e.copy(
                        out=woT[:, cc * D + rc * 128: cc * D + rc * 128 + 128],
                        in_=pss[:, sl * 128:sl * 128 + 128])
                    sl += 1
                    if cc != rc:
                        i1 = e.copy(
                            out=woT[:, rc * D + cc * 128:
                                    rc * D + cc * 128 + 128],
                            in_=pss[:, sl * 128:sl * 128 + 128])
                        sl += 1
                return i1
            wo_tr.append(s_we)
        wo_done = wo_tr[-1]

        absw_ev = []
        for kk2, (cs, es) in enumerate(
                [(a, b) for a in range(4) for b in range(4)]):
            ps = pa if kk2 % 2 == 0 else pb
            deps = [wo_done] if kk2 < 2 else [absw_ev[kk2 - 2]]

            @k.step("tensor", deps=deps)
            def s_am(e, cs=cs, es=es, ps=ps):
                last = None
                for ec in range(DCH):
                    last = e.matmul(
                        ps[:],
                        wuvb[:, ec * C + cs * 128: ec * C + cs * 128 + 128],
                        woT[:, ec * D + es * 512: ec * D + es * 512 + 512],
                        start=(ec == 0), stop=(ec == DCH - 1))
                return last

            @k.step("scalar", deps=[s_am])
            def s_ae(e, cs=cs, es=es, ps=ps):
                return e.copy(out=absw[:, cs * D + es * 512:
                                       cs * D + es * 512 + 512], in_=ps[:])
            absw_ev.append(s_ae)
        absw_done = absw_ev[-1]

        wqb = v16(O_R1, 8192)
        wkb = v16(O_R1 + 8192, 8192)
        Asb = v16(O_R2, 2048)

        @k.step("gpsimd", deps=[absw_done], ndma=2)
        def s_bl(e, sd):
            e.dma_start(
                out=wqb.rearrange("p (c f) -> p c f", c=DCH),
                in_=bass.AP(wuq.tensor, 0, [[C, 128], [128 * C, DCH], [1, C]]),
            ).then_inc(sd, 16)
            e.dma_start(
                out=wkb.rearrange("p (c f) -> p c f", c=DCH),
                in_=bass.AP(wuk.tensor, 0, [[C, 128], [128 * C, DCH], [1, C]]),
            ).then_inc(sd, 16)

        A_ev = []
        for cs in range(4):
            ps = pa if cs % 2 == 0 else pb
            deps = [s_bl] if cs < 2 else [A_ev[cs - 2]]

            @k.step("tensor", deps=deps)
            def s_Am(e, cs=cs, ps=ps):
                last = None
                for ec in range(DCH):
                    last = e.matmul(
                        ps[:],
                        wqb[:, ec * C + cs * 128: ec * C + cs * 128 + 128],
                        wkb[:, ec * C:(ec + 1) * C],
                        start=(ec == 0), stop=(ec == DCH - 1))
                return last

            @k.step("scalar", deps=[s_Am])
            def s_Ae(e, cs=cs, ps=ps):
                return e.copy(out=Asb[:, cs * C:(cs + 1) * C], in_=ps[:])
            A_ev.append(s_Ae)

        wdqn = v16(O_R1, 8192)
        Bsb = v16(O_R2 + 8192, 8192)

        @k.step("gpsimd", deps=[A_ev[-1]], ndma=1)
        def s_dql(e, sd):
            e.dma_start(
                out=wdqn.rearrange("p (c f) -> p c f", c=4),
                in_=bass.AP(wdq.tensor, 0, [[D, 128], [128 * D, 4], [1, D]]),
            ).then_inc(sd, 16)

        B_ev = []
        for ds in range(DCH):
            ps = pa if ds % 2 == 0 else pb
            deps = [s_dql] if ds < 2 else [B_ev[ds - 2]]

            @k.step("tensor", deps=deps)
            def s_Bm(e, ds=ds, ps=ps):
                last = None
                for cc in range(4):
                    last = e.matmul(
                        ps[:],
                        wdqn[:, cc * D + ds * 128: cc * D + ds * 128 + 128],
                        Asb[:, cc * C:(cc + 1) * C],
                        start=(cc == 0), stop=(cc == 3))
                return last

            @k.step("scalar", deps=[s_Bm])
            def s_Be(e, ds=ds, ps=ps):
                return e.activation(out=Bsb[:, ds * C:(ds + 1) * C], in_=ps[:],
                                    func=AF.Identity, scale=cbv(CB_I12))
            B_ev.append(s_Be)
        B_done = B_ev[-1]

        wdqT = v16(O_R1 + 8192, 8192)
        wnat = v16(O_x2t, 8192)
        wdkvT = v16(O_R2, 8192)

        @k.step("gpsimd", deps=[B_done], ndma=1)
        def s_kvl(e, sd):
            e.dma_start(
                out=wnat.rearrange("p (c f) -> p c f", c=4),
                in_=bass.AP(wdkv.tensor, 0, [[D, 128], [128 * D, 4], [1, D]]),
            ).then_inc(sd, 16)

        jobs = []
        for cc in range(4):
            for dc in range(DCH):
                jobs.append((wdqn, cc, dc, D, wdqT, C))
        for cc in range(4):
            for dc in range(DCH):
                jobs.append((wnat, cc, dc, D, wdkvT, C))
        sw_ev = []
        for jg in range(len(jobs) // 8):
            ps = pt1 if jg % 2 == 0 else pt2
            deps = [s_kvl, B_done] if jg < 2 else [sw_ev[jg - 2]]

            @k.step("tensor", deps=deps)
            def s_st(e, jg=jg, ps=ps):
                last = None
                for j in range(8):
                    srcb, rc, cc2, srcw, dst, dn = jobs[jg * 8 + j]
                    last = e.transpose(
                        ps[:, ts_(j)],
                        srcb[:, rc * srcw + cc2 * 128: rc * srcw + cc2 * 128 + 128],
                        identb[:])
                return last

            @k.step("scalar", deps=[s_st])
            def s_se(e, jg=jg, ps=ps):
                last = None
                for j in range(8):
                    srcb, rc, cc2, srcw, dst, dn = jobs[jg * 8 + j]
                    last = e.copy(
                        out=dst[:, cc2 * dn + rc * 128: cc2 * dn + rc * 128 + 128],
                        in_=ps[:, ts_(j)])
                return last
            sw_ev.append(s_se)
        sw_done = sw_ev[-1]

        x2t_ev = []
        ln2_apply = []
        for t in range(TILES):
            @k.step("scalar", deps=([ln2_apply[-1]] if ln2_apply else []))
            def s_st2(e, t=t):
                xt = x_sb[:, t * D:(t + 1) * D]
                e.activation(out=rtm2[:], in_=xt, func=AF.Copy,
                             accum_out=cS[:, t:t + 1])
                e.drain()
                return e.activation(out=rtm2[:], in_=xt, func=AF.Square,
                                    accum_out=cQ[:, t:t + 1])

            @k.step("vector", deps=[s_st2])
            def s_vr2(e, t=t):
                e.tensor_scalar(out=cMU[:, t:t + 1], in0=cS[:, t:t + 1],
                                scalar1=1.0 / D, scalar2=None, op0=ALU.mult)
                e.drain()
                e.tensor_scalar(out=sv1[:], in0=cMU[:, t:t + 1],
                                scalar1=cMU[:, t:t + 1], scalar2=None,
                                op0=ALU.mult)
                e.tensor_scalar(out=cVR[:, t:t + 1], in0=cQ[:, t:t + 1],
                                scalar1=1.0 / D, scalar2=None, op0=ALU.mult)
                e.drain()
                return e.tensor_sub(out=cVR[:, t:t + 1], in0=cVR[:, t:t + 1],
                                    in1=sv1[:])

            @k.step("scalar", deps=[s_vr2])
            def s_ap2(e, t=t):
                xt = x_sb[:, t * D:(t + 1) * D]
                e.activation(out=ss1[:], in_=cVR[:, t:t + 1], func=AF.Ln,
                             bias=cbv(CB_LN))
                e.drain()
                e.activation(out=cRS[:, t:t + 1], in_=ss1[:], func=AF.Exp,
                             scale=cbv(CB_MH))
                e.drain()
                e.activation(out=ss2[:], in_=cMU[:, t:t + 1], func=AF.Identity,
                             scale=cRS[:, t:t + 1])
                e.drain()
                e.activation(out=ss2[:], in_=ss2[:], func=AF.Identity, scale=cbv(CB_MO))
                e.drain()
                return e.activation(out=rtm2[:], in_=xt, func=AF.Identity,
                                    bias=ss2[:], scale=cRS[:, t:t + 1])
            ln2_apply.append(s_ap2)

            for q8 in range(2):
                gi = t * 2 + q8
                ps = pt1 if gi % 2 == 0 else pt2
                deps = [s_ap2, sw_done]
                if gi >= 2:
                    deps.append(x2t_ev[gi - 2])

                @k.step("tensor", deps=deps)
                def s_xt(e, t=t, q8=q8, ps=ps):
                    last = None
                    for j in range(8):
                        dc = q8 * 8 + j
                        last = e.transpose(ps[:, ts_(j)],
                                           rtm2[:, dc * 128:(dc + 1) * 128],
                                           identb[:])
                    return last

                @k.step("scalar", deps=[s_xt])
                def s_xe(e, t=t, q8=q8, ps=ps):
                    last = None
                    for j in range(8):
                        dc = q8 * 8 + j
                        last = e.copy(
                            out=x2t[:, dc * TOK + t * 128: dc * TOK + t * 128 + 128],
                            in_=ps[:, ts_(j)])
                    return last
                x2t_ev.append(s_xe)
        x2t_done = x2t_ev[-1]

        cqb = v16(O_R1, 4096)
        ckvb = v16(O_R1 + 4096, 4096)
        prj_ev = []
        for t in range(TILES):
            for which in range(3):
                kk2 = t * 3 + which
                ps = (pa, pb, pa)[which]
                wsrc = (wdqT, wdkvT, Bsb)[which]
                deps = [x2t_done, sw_done, B_done]
                if kk2 >= 1:
                    deps.append(prj_ev[kk2 - 1])

                @k.step("tensor", deps=deps)
                def s_pm(e, t=t, ps=ps, wsrc=wsrc):
                    last = None
                    for dc in range(DCH):
                        last = e.matmul(
                            ps[:],
                            x2t[:, dc * TOK + t * 128: dc * TOK + t * 128 + 128],
                            wsrc[:, dc * C:(dc + 1) * C],
                            start=(dc == 0), stop=(dc == DCH - 1))
                    return last

                rcol = cRS if which == 0 else cTH
                if which < 2:
                    @k.step("scalar", deps=[s_pm])
                    def s_ps(e, t=t, ps=ps, rcol=rcol):
                        e.activation(out=rtm2[:, 0:C], in_=ps[:],
                                     func=AF.Square, accum_out=ss1[:])
                        e.drain()
                        e.activation(out=ss2[:], in_=ss1[:], func=AF.Ln,
                                     bias=cbv(CB_RMS), scale=cbv(CB_IC))
                        e.drain()
                        return e.activation(out=rcol[:, t:t + 1], in_=ss2[:],
                                            func=AF.Exp, scale=cbv(CB_MH))

                    @k.step("vector", deps=[s_ps])
                    def s_pv(e, t=t, ps=ps, which=which, rcol=rcol):
                        dst = cqb if which == 0 else ckvb
                        gby = g1b if which == 0 else g2b
                        return e.scalar_tensor_tensor(
                            out=dst[:, t * C:(t + 1) * C], in0=ps[:],
                            scalar=rcol[:, t:t + 1], in1=gby[:], op0=ALU.mult,
                            op1=ALU.mult)
                    prj_ev.append(s_pv)
                else:
                    @k.step("vector", deps=[s_pm])
                    def s_pv2(e, t=t, ps=ps):
                        return e.tensor_copy(out=aqb[:, t * C:(t + 1) * C],
                                             in_=ps[:])
                    prj_ev.append(s_pv2)
        prj_done = prj_ev[-1]

        ckvT = v16(O_R2, 4096)
        jobs2 = []
        for t in range(TILES):
            for cc in range(4):
                jobs2.append((cqb, t, cc, cqT))
        for t in range(TILES):
            for cc in range(4):
                jobs2.append((aqb, t, cc, aqT))
        for t in range(TILES):
            for cc in range(4):
                jobs2.append((ckvb, t, cc, ckvT))
        j2_ev = []
        for jg in range(len(jobs2) // 8):
            ps = pt1 if jg % 2 == 0 else pt2
            deps = [prj_done]
            if jg >= 2:
                deps.append(j2_ev[jg - 2])

            @k.step("tensor", deps=deps)
            def s_j2(e, jg=jg, ps=ps):
                last = None
                for j in range(8):
                    srcb, t, cc, dst = jobs2[jg * 8 + j]
                    last = e.transpose(
                        ps[:, ts_(j)],
                        srcb[:, t * C + cc * 128: t * C + cc * 128 + 128],
                        identb[:])
                return last

            @k.step("scalar", deps=[s_j2])
            def s_j2e(e, jg=jg, ps=ps):
                last = None
                for j in range(8):
                    srcb, t, cc, dst = jobs2[jg * 8 + j]
                    last = e.copy(
                        out=dst[:, cc * TOK + t * 128: cc * TOK + t * 128 + 128],
                        in_=ps[:, ts_(j)])
                return last
            j2_ev.append(s_j2e)
        j2_done = j2_ev[-1]

        @k.step("sync", deps=[prj_done], ndma=1)
        def s_ckvst(e, sd):
            e.dma_start(
                out=bass.AP(ckvp, 0, [[C, 128], [128 * C, TILES], [1, C]]),
                in_=ckvb[:].rearrange("p (t f) -> p t f", t=TILES),
            ).then_inc(sd, 16)

        @k.step("sync", deps=[j2_done], ndma=1)
        def s_ckvtst(e, sd):
            e.dma_start(
                out=bass.AP(ckvtp, 0, [[TOK, 128], [128 * TOK, 4], [1, TOK]]),
                in_=ckvT[:].rearrange("p (c f) -> p c f", c=4),
            ).then_inc(sd, 16)

        @k.step("gpsimd", deps=[s_ckvst], cc=True)
        def s_agc(e):
            return e.collective_compute(
                "AllGather", ALU.bypass, replica_groups=rg4,
                ins=[ckvp.ap().opt()], outs=[ckva.ap().opt()])

        @k.step("gpsimd", deps=[s_ckvtst], cc=True)
        def s_agct(e):
            return e.collective_compute(
                "AllGather", ALU.bypass, replica_groups=rg4,
                ins=[ckvtp.ap().opt()], outs=[ckvta.ap().opt()])

        wkrT = v16(O_R1, 32768)
        tcos2 = v32(O_R3 + 7168, 512)
        tsin2 = v32(O_R3 + 8192, 512)
        qtile = v16(O_R3 + 3072, 2048)
        rstg = v16(O_R3 + 5120, 2048)
        krTt = v16(O_rtm2, 2048)

        @k.step("sync", deps=[s_ckvtst], ndma=2)
        def s_tb2(e, sd):
            e.dma_start(out=tcos2.rearrange("p (t f) -> p t f", t=TILES),
                        in_=bass.AP(cosk.tensor, 0,
                                    [[64, 128], [128 * 64, TILES], [1, 64]])
                        ).then_inc(sd, 16)
            e.dma_start(out=tsin2.rearrange("p (t f) -> p t f", t=TILES),
                        in_=bass.AP(sink.tensor, 0,
                                    [[64, 128], [128 * 64, TILES], [1, 64]])
                        ).then_inc(sd, 16)

        @k.step("gpsimd", deps=[j2_done, s_ckvst], ndma=1)
        def s_krl(e, sd):
            e.dma_start(
                out=wkrT.rearrange("p (c f) -> p c f", c=DCH),
                in_=bass.AP(wkr.tensor, 0, [[D, 128], [128 * D, DCH], [1, D]]),
            ).then_inc(sd, 16)

        kr_tr = []
        kr_pairs = [(r, c) for r in range(DCH) for c in range(r, DCH)]
        for gidx in range(0, len(kr_pairs), 4):
            chunk = kr_pairs[gidx:gidx + 4]
            idx = gidx // 4
            deps = [s_krl, j2_done] if idx < 2 else [kr_tr[idx - 2]]
            pss = pt1 if idx % 2 == 0 else pt2

            @k.step("tensor", deps=deps)
            def s_kt(e, chunk=chunk, pss=pss):
                i1 = None
                sl = 0
                for rc, cc in chunk:
                    i1 = e.transpose(
                        pss[:, sl * 128:sl * 128 + 128],
                        wkrT[:, rc * D + cc * 128: rc * D + cc * 128 + 128],
                        identb[:])
                    sl += 1
                    if cc != rc:
                        i1 = e.transpose(
                            pss[:, sl * 128:sl * 128 + 128],
                            wkrT[:, cc * D + rc * 128: cc * D + rc * 128 + 128],
                            identb[:])
                        sl += 1
                return i1

            @k.step("scalar", deps=[s_kt])
            def s_ke(e, chunk=chunk, pss=pss):
                i1 = None
                sl = 0
                for rc, cc in chunk:
                    i1 = e.copy(
                        out=wkrT[:, cc * D + rc * 128: cc * D + rc * 128 + 128],
                        in_=pss[:, sl * 128:sl * 128 + 128])
                    sl += 1
                    if cc != rc:
                        i1 = e.copy(
                            out=wkrT[:, rc * D + cc * 128:
                                     rc * D + cc * 128 + 128],
                            in_=pss[:, sl * 128:sl * 128 + 128])
                        sl += 1
                return i1
            kr_tr.append(s_ke)
        wkr_done = kr_tr[-1]

        def rope_emit(e, t, ctab, stab):
            def sv3(apo, extra, dims):
                return bass.AP(apo.tensor, apo.offset + extra,
                               [[apo.ap[0][0], 128]] + dims)
            ev = sv3(rtmp, 0, [[128, DCH], [2, 64]])
            od = sv3(rtmp, 1, [[128, DCH], [2, 64]])
            cv = sv3(ctab, t * 64, [[0, DCH], [1, 64]])
            sv = sv3(stab, t * 64, [[0, DCH], [1, 64]])
            oev = sv3(qtile, 0, [[128, DCH], [2, 64]])
            ood = sv3(qtile, 1, [[128, DCH], [2, 64]])
            e2 = sv3(rstg, 0, [[128, DCH], [2, 64]])
            o2 = sv3(rstg, 1, [[128, DCH], [2, 64]])
            e.tensor_tensor(out=oev, in0=ev, in1=cv, op=ALU.mult)
            e.tensor_tensor(out=ood, in0=od, in1=cv, op=ALU.mult)
            e.tensor_tensor(out=e2, in0=od, in1=sv, op=ALU.mult)
            e.tensor_tensor(out=o2, in0=ev, in1=sv, op=ALU.mult)
            e.drain()
            e.tensor_tensor(out=oev, in0=oev, in1=e2, op=ALU.subtract)
            return e.tensor_tensor(out=ood, in0=ood, in1=o2, op=ALU.add)

        def proj_rope_store(stat_src, w_src, n_k, ctab, stab, dstT, dst_dram,
                            mm_deps, per_tile_store, tbl_dep):
            ev_s = []
            rp_s = []
            tr_s = []
            st_s = []
            for t in range(TILES):
                for es in range(4):
                    kk2 = t * 4 + es
                    ps = pa if kk2 % 2 == 0 else pb
                    deps = list(mm_deps)
                    if kk2 >= 2:
                        deps.append(ev_s[kk2 - 2])

                    @k.step("tensor", deps=deps)
                    def s_m(e, t=t, es=es, ps=ps, stat_src=stat_src,
                            w_src=w_src, n_k=n_k):
                        last = None
                        for ic in range(n_k):
                            last = e.matmul(
                                ps[:],
                                stat_src[:, ic * TOK + t * 128:
                                         ic * TOK + t * 128 + 128],
                                w_src[:, ic * D + es * 512:
                                      ic * D + es * 512 + 512],
                                start=(ic == 0), stop=(ic == n_k - 1))
                        return last

                    rp_dep = [rp_s[t - 1]] if (es == 0 and t > 0) else []

                    @k.step("scalar", deps=[s_m] + rp_dep)
                    def s_e(e, es=es, ps=ps):
                        return e.copy(out=rtmp[:, es * 512:(es + 1) * 512],
                                      in_=ps[:])
                    ev_s.append(s_e)

                dep_r = [ev_s[t * 4 + 3], tbl_dep]
                if t > 0:
                    dep_r.append(tr_s[-1])

                @k.step("vector", deps=dep_r)
                def s_r(e, t=t, ctab=ctab, stab=stab):
                    return rope_emit(e, t, ctab, stab)
                rp_s.append(s_r)

                for q8 in range(2):
                    gi = t * 2 + q8
                    ps = pt1 if gi % 2 == 0 else pt2
                    deps = [s_r]
                    if gi >= 2:
                        deps.append(tr_s[gi - 2])

                    @k.step("tensor", deps=deps)
                    def s_t(e, q8=q8, ps=ps):
                        last = None
                        for j in range(8):
                            dc = q8 * 8 + j
                            last = e.transpose(
                                ps[:, ts_(j)],
                                qtile[:, dc * 128:(dc + 1) * 128], identb[:])
                        return last

                    dep_te = [s_t]
                    if per_tile_store and q8 == 0 and t > 0:
                        dep_te.append(st_s[-1])

                    @k.step("scalar", deps=dep_te)
                    def s_te2(e, t=t, q8=q8, ps=ps, dstT=dstT):
                        last = None
                        for j in range(8):
                            dc = q8 * 8 + j
                            if per_tile_store:
                                o0 = dc * 128
                                last = e.copy(out=krTt[:, o0:o0 + 128],
                                              in_=ps[:, ts_(j)])
                            else:
                                last = e.copy(
                                    out=dstT[:, dc * TOK + t * 128:
                                             dc * TOK + t * 128 + 128],
                                    in_=ps[:, ts_(j)])
                        return last
                    tr_s.append(s_te2)

                if per_tile_store:
                    @k.step("sync", deps=[tr_s[-1]], ndma=1)
                    def s_sto(e, sd, t=t, dst_dram=dst_dram):
                        e.dma_start(
                            out=bass.AP(dst_dram, t * 128,
                                        [[TOK, 128], [128 * TOK, DCH], [1, 128]]),
                            in_=krTt[:].rearrange("p (c f) -> p c f", c=DCH),
                        ).then_inc(sd, 16)
                    st_s.append(s_sto)
            return (st_s[-1] if per_tile_store else tr_s[-1])

        kr_done = proj_rope_store(x2t, wkrT, DCH, tcos2, tsin2, None, krtp,
                                  [wkr_done], True, s_tb2)

        @k.step("gpsimd", deps=[kr_done], cc=True)
        def s_agk(e):
            return e.collective_compute(
                "AllGather", ALU.bypass, replica_groups=rg4,
                ins=[krtp.ap().opt()], outs=[krta.ap().opt()])

        wqrn = v16(O_R1, 8192)
        wqrT = v16(O_R1 + 8192, 8192)
        tcos = v32(O_R2, 512)
        tsin = v32(O_R2 + 1024, 512)

        @k.step("gpsimd", deps=[kr_done], ndma=1)
        def s_qrl(e, sd):
            e.dma_start(
                out=wqrn.rearrange("p (c f) -> p c f", c=DCH),
                in_=bass.AP(wqr.tensor, 0, [[C, 128], [128 * C, DCH], [1, C]]),
            ).then_inc(sd, 16)

        @k.step("sync", deps=[kr_done], ndma=2)
        def s_tb1(e, sd):
            e.dma_start(out=tcos.rearrange("p (t f) -> p t f", t=TILES),
                        in_=bass.AP(cosq.tensor, 0,
                                    [[64, 128], [128 * 64, TILES], [1, 64]])
                        ).then_inc(sd, 16)
            e.dma_start(out=tsin.rearrange("p (t f) -> p t f", t=TILES),
                        in_=bass.AP(sinq.tensor, 0,
                                    [[64, 128], [128 * 64, TILES], [1, 64]])
                        ).then_inc(sd, 16)

        qr_tr = []
        jobs3 = [(dc, cc) for dc in range(DCH) for cc in range(4)]
        for jg in range(len(jobs3) // 8):
            ps = pt1 if jg % 2 == 0 else pt2
            deps = [s_qrl, kr_done] if jg < 2 else [qr_tr[jg - 2]]

            @k.step("tensor", deps=deps)
            def s_q3(e, jg=jg, ps=ps):
                last = None
                for j in range(8):
                    dc, cc2 = jobs3[jg * 8 + j]
                    last = e.transpose(
                        ps[:, ts_(j)],
                        wqrn[:, dc * C + cc2 * 128: dc * C + cc2 * 128 + 128],
                        identb[:])
                return last

            @k.step("scalar", deps=[s_q3])
            def s_q3e(e, jg=jg, ps=ps):
                last = None
                for j in range(8):
                    dc, cc2 = jobs3[jg * 8 + j]
                    last = e.copy(
                        out=wqrT[:, cc2 * D + dc * 128: cc2 * D + dc * 128 + 128],
                        in_=ps[:, ts_(j)])
                return last
            qr_tr.append(s_q3e)
        wqr_done = qr_tr[-1]

        qr_done = proj_rope_store(cqT, wqrT, 4, tcos, tsin, qrT, None,
                                  [wqr_done], False, s_tb1)

        kb_kr = [v16(O_R1, 8192), v16(O_R1 + 8192, 8192)]
        kb_ct = [v16(O_R2, 2048), v16(O_R2 + 2048, 2048)]
        kb_cv = [v16(O_R2 + 4096, 2048), v16(O_R2 + 6144, 2048)]
        ost = [v32(O_R2 + 8192 + i * 1024, 512) for i in range(TILES)]

        @k.step("vector", deps=[qr_done, s_agk])
        def s_finit(e):
            e.memset(mrun[:], -1e30)
            e.memset(lrun[:], 0.0)
            last = None
            for t in range(TILES):
                last = e.memset(ost[t][:], 0.0)
            return last

        kdma = []
        smx_v2 = []
        smx_s1 = []
        pt_ev = []
        ou_mm = []
        pend = []

        def emit_pt_ov(kk2, kt, t, ps, s_s1h):
            deps = [s_s1h]
            if kk2 >= 1:
                deps.append(pt_ev[kk2 - 1])

            @k.step("tensor", deps=deps)
            def s_pt(e):
                last = None
                for j in range(4):
                    last = e.transpose(ppt[:, ts_(j)],
                                       ptile[:, j * 128:(j + 1) * 128],
                                       identb[:])
                return last

            dep_e = [s_pt]
            if kk2 >= 1:
                dep_e.append(ou_mm[kk2 - 1])

            @k.step("scalar", deps=dep_e)
            def s_pte(e):
                return e.copy(out=pTt[:], in_=ppt[:])
            pt_ev.append(s_pte)

            dep_o = [s_pte]
            if kk2 >= 1:
                dep_o.append(smx_v2[kk2 - 1])

            @k.step("tensor", deps=dep_o)
            def s_ov(e, kt=kt):
                last = None
                for cc in range(4):
                    last = e.matmul(
                        pov[:],
                        pTt[:, cc * 128:(cc + 1) * 128],
                        kb_cv[kt % 2][:, cc * C:(cc + 1) * C],
                        start=(cc == 0), stop=(cc == 3))
                return last
            ou_mm.append(s_ov)

            @k.step("vector", deps=[s_ov, s_s1h])
            def s_v2(e, t=t):
                e.scalar_tensor_tensor(out=lrun[:, t:t + 1],
                                       in0=lrun[:, t:t + 1],
                                       scalar=fCO[:, t:t + 1],
                                       in1=fRS[:, t:t + 1], op0=ALU.mult,
                                       op1=ALU.add)
                return e.scalar_tensor_tensor(out=ost[t][:], in0=ost[t][:],
                                              scalar=fCO[:, t:t + 1],
                                              in1=pov[:], op0=ALU.mult,
                                              op1=ALU.add)
            smx_v2.append(s_v2)
            return s_pt

        last_pt = [None]
        for kt in range(KTS):
            deps = [s_agk, s_agc, s_agct, s_finit, qr_done]
            if kt >= 2:
                deps.append(smx_v2[(kt - 2) * TILES + TILES - 1])

            @k.step("sync", deps=deps, ndma=3)
            def s_kd(e, sd, kt=kt):
                b = kt // 2
                c0 = (kt % 2) * 512
                e.dma_start(
                    out=kb_kr[kt % 2].rearrange("p (c f) -> p c f", c=DCH),
                    in_=bass.AP(krta, (b * D) * TOK + c0,
                                [[TOK, 128], [128 * TOK, DCH], [1, 512]]),
                ).then_inc(sd, 16)
                e.dma_start(
                    out=kb_ct[kt % 2].rearrange("p (c f) -> p c f", c=4),
                    in_=bass.AP(ckvta, (b * C) * TOK + c0,
                                [[TOK, 128], [128 * TOK, 4], [1, 512]]),
                ).then_inc(sd, 16)
                e.dma_start(
                    out=kb_cv[kt % 2].rearrange("p (c f) -> p c f", c=4),
                    in_=bass.AP(ckva, kt * 512 * C,
                                [[C, 128], [128 * C, 4], [1, C]]),
                ).then_inc(sd, 16)
            kdma.append(s_kd)

            for t in range(TILES):
                kk2 = kt * TILES + t
                ps = psc if kk2 % 2 == 0 else psc2
                deps = [s_kd]
                if kk2 >= 2:
                    deps.append(smx_s1[kk2 - 2])

                @k.step("tensor", deps=deps)
                def s_sc2(e, kt=kt, t=t, ps=ps):
                    last = None
                    for cc in range(4):
                        last = e.matmul(
                            ps[:],
                            aqT[:, cc * TOK + t * 128: cc * TOK + t * 128 + 128],
                            kb_ct[kt % 2][:, cc * 512:(cc + 1) * 512],
                            start=(cc == 0), stop=False)
                    for ec in range(DCH):
                        last = e.matmul(
                            ps[:],
                            qrT[:, ec * TOK + t * 128: ec * TOK + t * 128 + 128],
                            kb_kr[kt % 2][:, ec * 512:(ec + 1) * 512],
                            start=False, stop=(ec == DCH - 1))
                    return last

                @k.step("vector", deps=[s_sc2])
                def s_v1(e, t=t, ps=ps):
                    e.reduce_max(out=sv1[:], in_=ps[:], axis=AX.X)
                    e.drain()
                    e.tensor_max(out=sv2[:], in0=sv1[:], in1=mrun[:, t:t + 1])
                    e.drain()
                    e.tensor_sub(out=fCA[:, t:t + 1], in0=mrun[:, t:t + 1],
                                 in1=sv2[:])
                    e.drain()
                    e.tensor_copy(out=mrun[:, t:t + 1], in_=sv2[:])
                    return e.tensor_scalar(out=fNM[:, t:t + 1], in0=sv2[:],
                                           scalar1=-1.0, scalar2=None,
                                           op0=ALU.mult)

                if pend:
                    last_pt[0] = emit_pt_ov(*pend.pop())
                dep_s1 = [s_v1]
                if last_pt[0] is not None:
                    dep_s1.append(last_pt[0])

                @k.step("scalar", deps=dep_s1)
                def s_s1(e, t=t, ps=ps):
                    e.activation(out=fCO[:, t:t + 1], in_=fCA[:, t:t + 1],
                                 func=AF.Exp)
                    return e.activation(out=ptile[:], in_=ps[:], func=AF.Exp,
                                        bias=fNM[:, t:t + 1],
                                        accum_out=fRS[:, t:t + 1])
                smx_s1.append(s_s1)
                pend.append((kk2, kt, t, ps, s_s1))

        emit_pt_ov(*pend.pop())

        u_done = []
        um_ev = []
        for t in range(TILES):
            @k.step("vector", deps=[smx_v2[-1]] + ([u_done[-1]] if u_done else []))
            def s_uo(e, t=t):
                e.reciprocal(out=sv1[:], in_=lrun[:, t:t + 1])
                e.drain()
                return e.tensor_scalar(out=ptile[:, 0:C], in0=ost[t][:],
                                       scalar1=sv1[:], scalar2=None,
                                       op0=ALU.mult)

            @k.step("tensor", deps=[s_uo] + ([um_ev[-1]] if um_ev else []))
            def s_ut(e):
                last = None
                for j in range(4):
                    last = e.transpose(ppt[:, ts_(j)],
                                       ptile[:, j * 128:(j + 1) * 128],
                                       identb[:])
                return last

            @k.step("scalar", deps=[s_ut])
            def s_ue(e):
                return e.copy(out=pTt[:], in_=ppt[:])

            um_s = []
            for es in range(4):
                ps = psc if es % 2 == 0 else psc2
                deps = [s_ue]
                if es >= 2:
                    deps.append(um_ev[-2 + es - 2 + 1] if False else um_ev[-1])

                @k.step("tensor", deps=deps)
                def s_um(e, es=es, ps=ps):
                    last = None
                    for cc in range(4):
                        last = e.matmul(
                            ps[:],
                            pTt[:, cc * 128:(cc + 1) * 128],
                            absw[:, cc * D + es * 512: cc * D + es * 512 + 512],
                            start=(cc == 0), stop=(cc == 3))
                    return last
                um_s.append(s_um)

                @k.step("scalar", deps=[s_um])
                def s_ume(e, es=es, ps=ps):
                    return e.copy(out=rtmp[:, es * 512:(es + 1) * 512],
                                  in_=ps[:])
                um_ev.append(s_ume)

            @k.step("scalar", deps=[um_ev[-1]])
            def s_us(e, t=t):
                e.activation(out=rtm2[:], in_=rtmp[:], func=AF.Square,
                             accum_out=ss1[:])
                e.drain()
                e.activation(out=ss2[:], in_=ss1[:], func=AF.Ln, bias=cbv(CB_RMS),
                             scale=cbv(CB_ID))
                e.drain()
                return e.activation(out=cRS[:, t:t + 1], in_=ss2[:],
                                    func=AF.Exp, scale=cbv(CB_MH))

            @k.step("vector", deps=[s_us, s_gbc])
            def s_ux(e, t=t):
                xt = x_sb[:, t * D:(t + 1) * D]
                e.scalar_tensor_tensor(out=rtm2[:], in0=rtmp[:],
                                       scalar=cRS[:, t:t + 1], in1=g3b[:],
                                       op0=ALU.mult, op1=ALU.mult)
                e.drain()
                return e.tensor_add(out=xt, in0=xt, in1=rtm2[:])
            u_done.append(s_ux)

        k.barrier()
        s_.close()

    attn_phase()
    ffn_phase("b")

    @k.step("gpsimd", ndma=1)
    def s_out(e, sd):
        e.dma_start(
            out=bass.AP(out.tensor, 0, [[D, 128], [128 * D, TILES], [1, D]]),
            in_=x_sb[:].rearrange("p (t d) -> p t d", t=TILES),
        ).then_inc(sd, 16)

    @k.step("gpsimd", deps=[s_out])
    def s_fin(e):
        return e.memset(svrow[0:1, 0:1], 0.0)

    k.lower(nc)
    top.close()
    return nc, tapd


def _host_inputs(inputs):
    h = np.ascontiguousarray(
        np.asarray(inputs["L0_attn_h"]).reshape(NCORES * TOK, D))
    wsyn = np.ascontiguousarray(np.asarray(inputs["L0_ffn_W_syn"]))
    wout = np.ascontiguousarray(np.asarray(inputs["L0_ffn_W_out"]))
    f32 = np.float32
    i = np.arange(64, dtype=np.float64)
    freqs = 1.0 / (10000.0 ** (2.0 * i / 128.0))
    ident = np.eye(128).astype(np.float16)
    ones1 = np.ones((1, 128), np.float16)
    onesf = np.ones((128, 1), f32)
    iota = np.broadcast_to(np.arange(8, dtype=f32), (128, 8)).copy()
    maps = []
    for c in range(NCORES):
        pos = (c % 4) * TOK + np.arange(TOK, dtype=np.float64)
        ang = pos[:, None] * freqs[None, :]
        cos = np.cos(ang).astype(f32)
        sin = np.sin(ang).astype(f32)
        m = {
            "h": np.ascontiguousarray(h[c * TOK:(c + 1) * TOK]),
            "wsyn": np.ascontiguousarray(wsyn[c * TOK:(c + 1) * TOK]),
            "wout": np.ascontiguousarray(wout[c * 256:(c + 1) * 256]),
            "wdkv": np.asarray(inputs["L0_attn_w_d_kv"]),
            "wdq": np.asarray(inputs["L0_attn_w_d_q"]),
            "wuk": np.asarray(inputs["L0_attn_w_u_k"]),
            "wuv": np.asarray(inputs["L0_attn_w_u_v"]),
            "wuq": np.asarray(inputs["L0_attn_w_u_q"]),
            "wkr": np.asarray(inputs["L0_attn_w_kr"]),
            "wqr": np.asarray(inputs["L0_attn_w_qr"]),
            "wo": np.asarray(inputs["L0_attn_w_o"]),
            "g1": np.asarray(inputs["L0_attn_g1"]).reshape(1, C),
            "g2": np.asarray(inputs["L0_attn_g2"]).reshape(1, C),
            "g3": np.asarray(inputs["L0_attn_g3"]).reshape(1, D),
            "bsyn": np.asarray(inputs["L0_ffn_b_syn"]).reshape(1, N),
            "bout": np.asarray(inputs["L0_ffn_b_out"]).reshape(1, D),
            "gamma": np.asarray(inputs["L0_ffn_gamma"]).reshape(1, D),
            "beta": np.asarray(inputs["L0_ffn_beta"]).reshape(1, D),
            "cosq": (cos / 12.0).astype(f32),
            "sinq": (sin / 12.0).astype(f32),
            "cosk": cos,
            "sink": sin,
            "ident": ident,
            "ones1": ones1,
            "onesf": onesf,
            "iota8": iota,
        }
        maps.append(m)
    return maps


def kernel(**inputs):
    nc, _ = build()
    maps = _host_inputs(inputs)
    res = run_bass_kernel_spmd(nc, maps, core_ids=list(range(NCORES)))
    outs = np.concatenate([r["out"] for r in res.results], axis=0)
    return np.ascontiguousarray(outs.reshape(2, SEQ, D).astype(np.float32))
